# version 04
# baseline (speedup 1.0000x reference)
import numpy as np
from contextlib import ExitStack

import concourse.bass as bass
import concourse.mybir as mybir
from concourse.bass_utils import run_bass_kernel_spmd

F32 = mybir.dt.float32
BF16 = mybir.dt.bfloat16
AF = mybir.ActivationFunctionType
ALU = mybir.AluOpType
AX = mybir.AxisListType

T, D, H, DH, FF = 4096, 1024, 16, 64, 4096
NT, NC8, NQ = 32, 8, 8
C23 = 12582912.0
C52 = 6755399441055744.0
EPS = 1e-5
VS = 97


def dap(t, off, dims):
    return bass.AP(t, off, [list(d) for d in dims])


class DR:
    _OPS = {"tensor_scalar", "tensor_tensor", "tensor_copy", "reduce_sum",
            "reduce_max", "tensor_reduce", "reciprocal", "memset",
            "scalar_tensor_tensor", "tensor_single_scalar", "activation"}

    def __init__(self, e):
        self._e = e

    def __getattr__(self, name):
        f = getattr(self._e, name)
        if name in DR._OPS:
            def g(*a, **k):
                r = f(*a, **k)
                self._e.drain()
                return r
            return g
        return f


def _stub_out(nc, blk, out, wt, sQ):
    @blk.sync
    def _(s):
        for tt in range(32):
            s.dma_start(bass.AP(out, tt * 131072, [[1024, 128], [1, 1024]]),
                        bass.AP(wt, 0, [[4096, 128], [1, 1024]])).then_inc(sQ.h, 16)
    sQ.n += 512

    @blk.sync
    def _(s):
        s.wait_ge(sQ.h, sQ.n)


def build():
    nc = bass.Bass("TRN2", target_bir_lowering=False,
                   detect_race_conditions=False)

    x_in = nc.dram_tensor("x", [T, D], F32, kind="ExternalInput")
    w_q = nc.dram_tensor("wq", [D, D], F32, kind="ExternalInput")
    w_k = nc.dram_tensor("wk", [D, D], F32, kind="ExternalInput")
    w_v = nc.dram_tensor("wv", [D, D], F32, kind="ExternalInput")
    w_o = nc.dram_tensor("wo", [D, D], F32, kind="ExternalInput")
    w_1 = nc.dram_tensor("w1", [FF, D], F32, kind="ExternalInput")
    b_1 = nc.dram_tensor("b1", [32, 128], F32, kind="ExternalInput")
    w_2 = nc.dram_tensor("w2", [D, FF], F32, kind="ExternalInput")
    b_2 = nc.dram_tensor("b2", [8, 128], F32, kind="ExternalInput")
    out = nc.dram_tensor("out", [T, D], F32, kind="ExternalOutput")

    _k = dict(kind="ExternalOutput") if _DEBUG_SCRATCH else {}
    s_ctx = nc.dram_tensor("s_ctx", [T, D], BF16, **_k)
    s_x1 = nc.dram_tensor("s_x1", [T, D], F32, **_k)
    s_dbg = nc.dram_tensor("s_dbg", [8, 128, 1024], F32, **_k) if _DEBUG_SCRATCH         else None
    s_w1t = nc.dram_tensor("s_w1t", [32, 8, 128, 128], BF16)
    s_w2t = nc.dram_tensor("s_w2t", [8, 32, 128, 128], BF16)

    g_es = ExitStack()

    def sb(stk, name, shape, dtype, **kw):
        return stk.enter_context(nc.sbuf_tensor(name, shape, dtype, **kw))

    wsel = sb(g_es, "wsel", [128, 1], F32)
    ones = sb(g_es, "ones", [128, 128], F32)
    kq = sb(g_es, "kq", [128, 64], F32)
    bias_h = sb(g_es, "bias_h", [128, H], F32)
    gam = sb(g_es, "gam", [128, 8], F32)
    b1c = sb(g_es, "b1c", [128, 32], F32)
    b2c = sb(g_es, "b2c", [128, 16], F32)
    hot = sb(g_es, "hot", [128, 8], F32)
    hdsel = sb(g_es, "hdsel", [128, 33], BF16)
    gat = sb(g_es, "gat", [128, 128], BF16)
    gatT = sb(g_es, "gatT", [128, 128], BF16, is_transpose_dest=True)

    r_es = ExitStack()
    tb1 = sb(r_es, "tb1", [128, 4096], F32)
    tb2 = sb(r_es, "tb2", [128, 4096], F32)

    w_es = ExitStack()
    wbig = sb(w_es, "wbig", [128, 4096], F32)
    qbuf = sb(w_es, "qbuf", [128, 4096], BF16)
    wtst = sb(w_es, "wtst", [128, 4096], BF16, is_transpose_dest=True)
    wt_q = sb(w_es, "wt_q", [128, NC8 * D], BF16, is_transpose_dest=True)
    bk_es = ExitStack()
    wt_k = sb(bk_es, "wt_k", [128, NC8 * D], BF16, is_transpose_dest=True)
    wt_v = sb(bk_es, "wt_v", [128, NC8 * D], BF16, is_transpose_dest=True)

    p_es = ExitStack()
    ps_g = p_es.enter_context(nc.psum_tensor("ps_g", [128, 512], F32))
    ps_a = p_es.enter_context(nc.psum_tensor("ps_a", [128, 1024], F32))
    ps_b = p_es.enter_context(nc.psum_tensor("ps_b", [128, 1024], F32))
    ps_c = p_es.enter_context(nc.psum_tensor("ps_c", [128, 512], F32))
    ps_d = p_es.enter_context(nc.psum_tensor("ps_d", [128, 512], F32))

    class Ctr:
        def __init__(self, name):
            self.h = g_es.enter_context(nc.semaphore(name))
            self.n = 0
    sQ, sG, sV, sA, sM = Ctr("sQ"), Ctr("sG"), Ctr("sV"), Ctr("sA"), Ctr("sM")

    blk = g_es.enter_context(nc.Block())

    def rnd(v, dst, src_ap, n, post_scale=None, clip1=False):
        v.tensor_scalar(tb1[:, 0:n], src_ap, C23, C23, ALU.add, ALU.subtract)
        v.tensor_scalar(tb2[:, 0:n], src_ap, C52, C52, ALU.add, ALU.subtract)
        v.tensor_tensor(tb2[:, 0:n], tb2[:, 0:n], tb1[:, 0:n], ALU.subtract)
        if post_scale is None and not clip1:
            return v.scalar_tensor_tensor(dst, tb2[:, 0:n], wsel[:, 0:1],
                                          tb1[:, 0:n], ALU.mult, ALU.add)
        v.scalar_tensor_tensor(tb1[:, 0:n], tb2[:, 0:n], wsel[:, 0:1],
                               tb1[:, 0:n], ALU.mult, ALU.add)
        if clip1:
            return v.tensor_scalar(dst, tb1[:, 0:n], -1.0, 1.0, ALU.max, ALU.min)
        return v.tensor_scalar(dst, tb1[:, 0:n], post_scale, None, ALU.mult)

    def rsqrt(s, dst, tmp, src):
        s.activation(tmp, src, AF.Ln)
        s.drain()
        return s.activation(dst, tmp, AF.Exp, bias=0.0, scale=-0.5)

    @blk.scalar
    def _(s):
        s.add_instruction(mybir.InstLoadActFuncSet(
            name=nc.get_next_instruction_name(), act_func_set_id=6, ins=[], outs=[]))

    @blk.vector
    def _(v):
        v.memset(wsel[:, :], 2.0 ** -30)
        v.drain()
        v.tensor_scalar(wsel[:, :], wsel[:, :], 1.0, 1.0, ALU.add, ALU.subtract)
        v.drain()
        v.tensor_scalar(wsel[:, :], wsel[:, :], 1e30, 1.0, ALU.mult, ALU.min)
        v.drain()
        v.memset(ones[:, :], 1.0)
        v.memset(hot[:, :], 0.0)
        v.memset(gat[:, :], 0.0)
        v.memset(hdsel[:, :], 0.0)
        v.memset(hdsel[0:64, 0:1], 1.0)
        v.memset(hdsel[64:128, 32:33], 1.0)
        v.memset(wbig[0:32, 128:256], 0.0).then_inc(sV.h, 1)
    sV.n += 1
    n_ms0 = sV.n

    @blk.gpsimd
    def _(g):
        g.wait_ge(sV.h, n_ms0)
        g.dma_start(dap(wbig, 0, [[4096, 32], [1, 128]]),
                    dap(b_1, 0, [[128, 32], [1, 128]])).then_inc(sG.h, 16)
        g.dma_start(dap(wbig, 128, [[4096, 8], [1, 128]]),
                    dap(b_2, 0, [[128, 8], [1, 128]])).then_inc(sG.h, 16)
    sG.n += 32
    n_bld = sG.n

    @blk.vector
    def _(v):
        v.wait_ge(sG.h, n_bld)
        v.tensor_copy(gat[0:32, 0:128], wbig[0:32, 0:128])
        v.tensor_tensor(qbuf[0:32, 0:128], wbig[0:32, 0:128],
                        gat[0:32, 0:128], ALU.subtract)
        v.tensor_copy(qbuf[0:16, 128:256], wbig[0:16, 128:256])
        v.tensor_tensor(qbuf[0:16, 256:384], wbig[0:16, 128:256],
                        qbuf[0:16, 128:256], ALU.subtract).then_inc(sV.h, 1)
    sV.n += 1
    n_bv = sV.n

    @blk.sync
    def _(s):
        s.wait_ge(sV.h, n_bv)
        s.dma_start(gatT[:, 0:32], gat[0:32, 0:128], transpose=True).then_inc(sQ.h, 16)
        s.dma_start(wtst[:, 0:32], qbuf[0:32, 0:128], transpose=True).then_inc(sQ.h, 16)
        s.dma_start(gatT[:, 32:48], qbuf[0:16, 128:256], transpose=True).then_inc(sQ.h, 16)
        s.dma_start(wtst[:, 32:48], qbuf[0:16, 256:384], transpose=True).then_inc(sQ.h, 16)
    sQ.n += 64
    n_bt = sQ.n

    @blk.vector
    def _(v):
        v.wait_ge(sQ.h, n_bt)
        v.tensor_tensor(b1c[:, :], gatT[:, 0:32], wtst[:, 0:32], ALU.add)
        v.tensor_tensor(b2c[:, :], gatT[:, 32:48], wtst[:, 32:48], ALU.add)
        v.memset(gat[:, :], 0.0).then_inc(sV.h, 1)
    sV.n += 1

    def prep_attn(w_dram, dst_wt, gcol, tguard=0):
        for half in range(2):
            red = sV.n

            @blk.gpsimd
            def _(g, half=half, red=red):
                g.wait_ge(sV.h, red)
                g.dma_start(dap(wbig, 0, [[4096, 128], [1024, 4], [1, 1024]]),
                            dap(w_dram, half * 524288,
                                [[1024, 128], [131072, 4], [1, 1024]])).then_inc(sG.h, 16)
            sG.n += 16
            ld = sG.n

            @blk.vector
            def _(v, half=half, ld=ld):
                v.wait_ge(sG.h, ld)
                v.reduce_sum(kq[:, 40 + half:41 + half], wbig[:, :], AX.X,
                             apply_absolute_value=True).then_inc(sV.h, 1)
            sV.n += 1

        @blk.vector
        def _(v):
            v.drain()
            v.tensor_tensor(kq[:, 42:43], kq[:, 40:41], kq[:, 41:42],
                            ALU.add).then_inc(sV.h, 1)
        sV.n += 1
        pd = sV.n

        @blk.tensor
        def _(t, pd=pd):
            t.wait_ge(sV.h, pd)
            t.matmul(ps_g[:, 0:1], ones[:, :], kq[:, 42:43],
                     start=True, stop=True).then_inc(sM.h, 1)
        sM.n += 1
        md = sM.n

        @blk.vector
        def _(v, md=md):
            v.wait_ge(sM.h, md)
            v = DR(v)
            v.tensor_scalar(kq[:, 43:44], ps_g[:, 0:1], 1.0 / (D * D), EPS,
                            ALU.mult, ALU.max)
            v.tensor_copy(gam[:, gcol:gcol + 1], kq[:, 43:44])
            v.reciprocal(kq[:, 44:45], kq[:, 43:44]).then_inc(sV.h, 1)
        sV.n += 1

        for half in range(2):
            qd0 = sV.n
            tp0 = sQ.n

            @blk.gpsimd
            def _(g, half=half, qd0=qd0):
                g.wait_ge(sV.h, qd0)
                g.dma_start(dap(wbig, 0, [[4096, 128], [1024, 4], [1, 1024]]),
                            dap(w_dram, half * 524288,
                                [[1024, 128], [131072, 4], [1, 1024]])).then_inc(sG.h, 16)
            sG.n += 16
            ld = sG.n

            @blk.vector
            def _(v, half=half, ld=ld, tp0=tp0):
                v.wait_ge(sG.h, ld)
                if half == 1:
                    v.wait_ge(sQ.h, tp0)
                v.tensor_scalar(tb2[:, :], wbig[:, :], kq[:, 44:45], None, ALU.mult)
                v.tensor_scalar(tb1[:, :], tb2[:, :], C23, C23, ALU.add, ALU.subtract)
                v.tensor_scalar(wbig[:, :], tb2[:, :], C52, C52, ALU.add, ALU.subtract)
                v.tensor_tensor(wbig[:, :], wbig[:, :], tb1[:, :], ALU.subtract)
                v.scalar_tensor_tensor(tb1[:, :], wbig[:, :], wsel[:, 0:1],
                                       tb1[:, :], ALU.mult, ALU.add)
                v.tensor_scalar(qbuf[:, :], tb1[:, :], -1.0, 1.0,
                                ALU.max, ALU.min).then_inc(sV.h, 1)
            sV.n += 1
            qd = sV.n

            @blk.sync
            def _(s, half=half, qd=qd, tguard=tguard):
                s.wait_ge(sV.h, qd)
                if tguard and half == 0:
                    s.wait_ge(sM.h, tguard)
                for cl in range(4):
                    c = half * 4 + cl
                    for j in range(NC8):
                        s.dma_start(dst_wt[:, j * D + c * 128:j * D + (c + 1) * 128],
                                    qbuf[:, cl * D + j * 128:cl * D + (j + 1) * 128],
                                    transpose=True).then_inc(sQ.h, 16)
            sQ.n += 16 * 32
        return sQ.n

    def prep_ffn(w_dram, spill, scol, layout):
        ngrp = 8
        for grp in range(ngrp):
            red = sV.n

            @blk.gpsimd
            def _(g, grp=grp, red=red):
                g.wait_ge(sV.h, red)
                if layout == "w1":
                    g.dma_start(dap(wbig, 0, [[4096, 128], [1024, 4], [1, 1024]]),
                                dap(w_dram, grp * 524288,
                                    [[1024, 128], [131072, 4], [1, 1024]])
                                ).then_inc(sG.h, 16)
                else:
                    g.dma_start(dap(wbig, 0, [[4096, 128], [1, 4096]]),
                                dap(w_dram, grp * 524288, [[4096, 128], [1, 4096]])
                                ).then_inc(sG.h, 16)
            sG.n += 16
            ld = sG.n

            @blk.vector
            def _(v, grp=grp, ld=ld):
                v.wait_ge(sG.h, ld)
                v.reduce_sum(kq[:, 45:46], wbig[:, :], AX.X, apply_absolute_value=True)
                v.drain()
                if grp == 0:
                    v.tensor_copy(kq[:, 46:47], kq[:, 45:46]).then_inc(sV.h, 1)
                else:
                    v.tensor_tensor(kq[:, 46:47], kq[:, 46:47], kq[:, 45:46],
                                    ALU.add).then_inc(sV.h, 1)
            sV.n += 1
        pd = sV.n

        @blk.tensor
        def _(t, pd=pd):
            t.wait_ge(sV.h, pd)
            t.matmul(ps_g[:, 0:1], ones[:, :], kq[:, 46:47],
                     start=True, stop=True).then_inc(sM.h, 1)
        sM.n += 1
        md = sM.n

        @blk.vector
        def _(v, md=md):
            v.wait_ge(sM.h, md)
            v = DR(v)
            v.tensor_scalar(kq[:, 43:44], ps_g[:, 0:1], 1.0 / (FF * D), EPS,
                            ALU.mult, ALU.add)
            v.reciprocal(kq[:, 44:45], kq[:, 43:44])
            v.tensor_copy(gam[:, scol:scol + 1], kq[:, 44:45]).then_inc(sV.h, 1)
        sV.n += 1

        for grp in range(ngrp):
            qd0 = sV.n

            @blk.gpsimd
            def _(g, grp=grp, qd0=qd0):
                g.wait_ge(sV.h, qd0)
                if layout == "w1":
                    g.dma_start(dap(wbig, 0, [[4096, 128], [1024, 4], [1, 1024]]),
                                dap(w_dram, grp * 524288,
                                    [[1024, 128], [131072, 4], [1, 1024]])
                                ).then_inc(sG.h, 16)
                else:
                    g.dma_start(dap(wbig, 0, [[4096, 128], [1, 4096]]),
                                dap(w_dram, grp * 524288, [[4096, 128], [1, 4096]])
                                ).then_inc(sG.h, 16)
            sG.n += 16
            ld = sG.n
            spl_prev = sG.n

            @blk.vector
            def _(v, grp=grp, ld=ld):
                v.wait_ge(sG.h, ld)
                v.tensor_scalar(tb2[:, :], wbig[:, :], kq[:, 44:45], None, ALU.mult)
                v.tensor_scalar(tb1[:, :], tb2[:, :], C23, C23, ALU.add, ALU.subtract)
                v.tensor_scalar(wbig[:, :], tb2[:, :], C52, C52, ALU.add, ALU.subtract)
                v.tensor_tensor(wbig[:, :], wbig[:, :], tb1[:, :], ALU.subtract)
                v.scalar_tensor_tensor(tb1[:, :], wbig[:, :], wsel[:, 0:1],
                                       tb1[:, :], ALU.mult, ALU.add)
                v.tensor_scalar(qbuf[:, :], tb1[:, :], -1.0, 1.0,
                                ALU.max, ALU.min).then_inc(sV.h, 1)
            sV.n += 1
            qd = sV.n

            @blk.sync
            def _(s, grp=grp, qd=qd, spl_prev=spl_prev):
                s.wait_ge(sV.h, qd)
                s.wait_ge(sG.h, spl_prev)
                for bi in range(32):
                    s.dma_start(wtst[:, bi * 128:(bi + 1) * 128],
                                qbuf[:, bi * 128:(bi + 1) * 128],
                                transpose=True).then_inc(sQ.h, 16)
            sQ.n += 16 * 32
            tp = sQ.n

            @blk.gpsimd
            def _(g, grp=grp, tp=tp):
                g.wait_ge(sQ.h, tp)
                if layout == "w1":
                    g.dma_start(dap(spill, grp * 4 * 131072,
                                    [[128, 128], [131072, 4], [16384, 8], [1, 128]]),
                                dap(wtst, 0, [[4096, 128], [1024, 4], [128, 8], [1, 128]])
                                ).then_inc(sG.h, 16)
                else:
                    g.dma_start(dap(spill, grp * 524288,
                                    [[128, 128], [16384, 32], [1, 128]]),
                                dap(wtst, 0, [[4096, 128], [128, 32], [1, 128]])
                                ).then_inc(sG.h, 16)
            sG.n += 16

    prep_attn(w_v, wt_v, 2)
    prep_attn(w_q, wt_q, 0)
    n_wprep = prep_attn(w_k, wt_k, 1)
    if _STOP_AFTER == "W":
        @blk.vector
        def _(v):
            v.wait_ge(sQ.h, n_wprep)
            v.wait_ge(sG.h, sG.n)
        _stub_out(nc, blk, out, tb1, sQ)
        for stk in (bk_es, w_es, r_es, p_es, g_es):
            stk.close()
        return nc

    a_es = ExitStack()
    xa = sb(a_es, "xa", [128, 2 * D], F32)
    scr = sb(a_es, "scr", [128, 2 * D], F32)
    xqs = sb(a_es, "xqs", [128, 2 * D], BF16)
    xqTt = sb(a_es, "xqTt", [128, 2 * D], BF16, is_transpose_dest=True)
    q_sb = sb(a_es, "q_sb", [128, D], BF16)
    k_sb = sb(a_es, "k_sb", [128, D], BF16)
    v_sb = sb(a_es, "v_sb", [128, D], BF16)
    prod = sb(a_es, "prod", [128, 16384], BF16)
    sc_sb = sb(a_es, "sc_sb", [128, 256], F32)
    mx_sb = sb(a_es, "mx_sb", [128, 16], F32)
    e_sb = sb(a_es, "e_sb", [128, 256], F32)
    sm_sb = sb(a_es, "sm_sb", [128, 16], F32)
    at_sb = sb(a_es, "at_sb", [128, 256], BF16)
    ctx_f = sb(a_es, "ctx_f", [128, D], F32)
    ctx_sb = sb(a_es, "ctx_sb", [128, 2 * D], BF16)

    pbanks = [(ps_a, 0), (ps_a, 512), (ps_b, 0), (ps_b, 512), (ps_c, 0), (ps_d, 0)]
    a_v, a_t, ev_c, ctx_v, ctx_sp, mm_c = [], [], [], [], [], []

    prep_units = [lambda: prep_ffn(w_1, s_w1t, 4, "w1"),
                  lambda: prep_ffn(w_2, s_w2t, 5, "w2")]
    prep_i = 0

    for tt in range(NT):
        pb = tt % 2
        ko = 20 * pb
        wv_prev = a_v[tt - 2] if tt >= 2 else 0

        @blk.gpsimd
        def _(g, tt=tt, pb=pb, wv_prev=wv_prev):
            if wv_prev:
                g.wait_ge(sV.h, wv_prev)
            g.dma_start(dap(xa, pb * D, [[2 * D, 128], [1, D]]),
                        dap(x_in, tt * 128 * D, [[D, 128], [1, D]])).then_inc(sG.h, 16)
        sG.n += 16
        n_xld = sG.n

        @blk.scalar
        def _(s, pb=pb, ko=ko, n_xld=n_xld, wv_prev=wv_prev):
            s.wait_ge(sG.h, n_xld)
            if wv_prev:
                s.wait_ge(sV.h, wv_prev)
            s.activation(scr[:, pb * D:(pb + 1) * D], xa[:, pb * D:(pb + 1) * D],
                         AF.Square, accum_out=kq[:, ko + 0:ko + 1]).then_inc(sA.h, 1)
        sA.n += 1
        n_sq = sA.n

        @blk.vector
        def _(v, pb=pb, ko=ko, n_sq=n_sq):
            v.wait_ge(sA.h, n_sq)
            v = DR(v)
            xt = xa[:, pb * D:(pb + 1) * D]
            v.reduce_sum(kq[:, ko + 1:ko + 2], xt, AX.X)
            v.tensor_scalar(kq[:, ko + 1:ko + 2], kq[:, ko + 1:ko + 2], 1.0 / D,
                            None, ALU.mult)
            v.tensor_scalar(kq[:, ko + 2:ko + 3], kq[:, ko + 0:ko + 1], 1.0 / D,
                            None, ALU.mult)
            v.tensor_tensor(kq[:, ko + 3:ko + 4], kq[:, ko + 1:ko + 2],
                            kq[:, ko + 1:ko + 2], ALU.mult)
            v.tensor_tensor(kq[:, ko + 2:ko + 3], kq[:, ko + 2:ko + 3],
                            kq[:, ko + 3:ko + 4], ALU.subtract)
            v.tensor_scalar(kq[:, ko + 3:ko + 4], kq[:, ko + 2:ko + 3], EPS,
                            None, ALU.add).then_inc(sV.h, 1)
        sV.n += 1
        n_st = sV.n

        @blk.scalar
        def _(s, ko=ko, n_st=n_st):
            s.wait_ge(sV.h, n_st)
            rsqrt(s, kq[:, ko + 4:ko + 5], kq[:, ko + 5:ko + 6], kq[:, ko + 3:ko + 4]).then_inc(sA.h, 1)
        sA.n += 1
        n_rs = sA.n

        @blk.vector
        def _(v, ko=ko, n_rs=n_rs):
            v.wait_ge(sA.h, n_rs)
            v = DR(v)
            v.tensor_tensor(kq[:, ko + 6:ko + 7], kq[:, ko + 4:ko + 5],
                            kq[:, ko + 4:ko + 5], ALU.mult)
            v.tensor_tensor(kq[:, ko + 6:ko + 7], kq[:, ko + 6:ko + 7],
                            kq[:, ko + 2:ko + 3], ALU.mult)
            v.tensor_scalar(kq[:, ko + 6:ko + 7], kq[:, ko + 6:ko + 7], 1e-6,
                            None, ALU.add).then_inc(sV.h, 1)
        sV.n += 1
        n_f = sV.n

        @blk.scalar
        def _(s, ko=ko, n_f=n_f):
            s.wait_ge(sV.h, n_f)
            rsqrt(s, kq[:, ko + 7:ko + 8], kq[:, ko + 8:ko + 9], kq[:, ko + 6:ko + 7]).then_inc(sA.h, 1)
        sA.n += 1
        n_g = sA.n
        wt_prev = a_t[tt - 2] if tt >= 2 else 0

        @blk.vector
        def _(v, pb=pb, ko=ko, n_g=n_g, wt_prev=wt_prev):
            v.wait_ge(sA.h, n_g)
            v = DR(v)
            if wt_prev:
                v.wait_ge(sQ.h, wt_prev)
            xt = xa[:, pb * D:(pb + 1) * D]
            sc = scr[:, pb * D:(pb + 1) * D]
            v.tensor_tensor(kq[:, ko + 9:ko + 10], kq[:, ko + 4:ko + 5],
                            kq[:, ko + 7:ko + 8], ALU.mult)
            v.tensor_scalar(sc, xt, kq[:, ko + 1:ko + 2], None, ALU.subtract)
            v.reduce_max(kq[:, ko + 10:ko + 11], sc, AX.X, apply_absolute_value=True)
            v.tensor_tensor(kq[:, ko + 10:ko + 11], kq[:, ko + 10:ko + 11],
                            kq[:, ko + 9:ko + 10], ALU.mult)
            v.tensor_scalar(kq[:, ko + 10:ko + 11], kq[:, ko + 10:ko + 11], EPS,
                            None, ALU.max)
            v.reciprocal(kq[:, ko + 11:ko + 12], kq[:, ko + 10:ko + 11])
            v.tensor_scalar(kq[:, ko + 11:ko + 12], kq[:, ko + 11:ko + 12], 127.0,
                            None, ALU.mult)
            v.tensor_scalar(kq[:, ko + 12:ko + 13], kq[:, ko + 10:ko + 11],
                            1.0 / 127.0, None, ALU.mult)
            v.tensor_tensor(kq[:, ko + 13:ko + 14], kq[:, ko + 9:ko + 10],
                            kq[:, ko + 11:ko + 12], ALU.mult)
            v.tensor_scalar(sc, sc, kq[:, ko + 13:ko + 14], None, ALU.mult)
            rnd(v, xqs[:, pb * D:(pb + 1) * D], sc, D,
                post_scale=kq[:, ko + 12:ko + 13]).then_inc(sV.h, 1)
        sV.n += 1
        a_v.append(sV.n)
        n_av = sV.n

        @blk.sync
        def _(s, pb=pb, n_av=n_av):
            s.wait_ge(sV.h, n_av)
            for j in range(NC8):
                s.dma_start(xqTt[:, pb * D + j * 128:pb * D + (j + 1) * 128],
                            xqs[:, pb * D + j * 128:pb * D + (j + 1) * 128],
                            transpose=True).then_inc(sQ.h, 16)
        sQ.n += 16 * NC8
        a_t.append(sQ.n)
        n_at = sQ.n
        w_ev = ev_c[tt - 1] if tt >= 1 else 0

        @blk.tensor
        def _(t, tt=tt, pb=pb, n_at=n_at, w_ev=w_ev):
            t.wait_ge(sQ.h, n_at)
            if tt == 0:
                t.wait_ge(sQ.h, n_wprep)
            if w_ev:
                t.wait_ge(sA.h, w_ev)
            mi = None
            for j in range(NC8):
                lhs = xqTt[:, pb * D + j * 128:pb * D + (j + 1) * 128]
                for pi, wt in enumerate((wt_q, wt_k, wt_v)):
                    for n in range(2):
                        pb_, off = pbanks[pi * 2 + n]
                        mi = t.matmul(pb_[:, off:off + 512], lhs,
                                      wt[:, j * D + n * 512:j * D + (n + 1) * 512],
                                      start=(j == 0), stop=(j == NC8 - 1))
            mi.then_inc(sM.h, 1)
        sM.n += 1
        mm_c.append(sM.n)
        n_mm = sM.n

        @blk.scalar
        def _(s, n_mm=n_mm, w_mix=(ctx_v[tt - 1] if tt >= 1 else 0)):
            s.wait_ge(sM.h, n_mm)
            if w_mix:
                s.wait_ge(sV.h, w_mix)
            mi = None
            for pi, dst in enumerate((q_sb, k_sb, v_sb)):
                for n in range(2):
                    pb_, off = pbanks[pi * 2 + n]
                    mi = s.activation(dst[:, n * 512:(n + 1) * 512],
                                      pb_[:, off:off + 512],
                                      AF.Copy, bias=0.0, scale=gam[:, pi:pi + 1])
            mi.then_inc(sA.h, 1)
        sA.n += 1
        ev_c.append(sA.n)
        n_evc = sA.n

        w_ctx = ctx_sp[tt - 2] if tt >= 2 else 0
        if _DEBUG_SCRATCH and tt == 0:
            @blk.gpsimd
            def _(g, n_evc=n_evc):
                g.wait_ge(sA.h, n_evc)
                g.dma_start(dap(s_dbg, 0 * 131072, [[1024, 128], [1, 1024]]),
                            q_sb[:, :]).then_inc(sG.h, 16)
                g.dma_start(dap(s_dbg, 1 * 131072, [[1024, 128], [1, 1024]]),
                            k_sb[:, :]).then_inc(sG.h, 16)
                g.dma_start(dap(s_dbg, 2 * 131072, [[1024, 128], [1, 1024]]),
                            v_sb[:, :]).then_inc(sG.h, 16)
            sG.n += 48
        n_dbg_qkv = sG.n

        @blk.vector
        def _(v, n_evc=n_evc, w_ctx=w_ctx, pb=pb, n_dbg_qkv=n_dbg_qkv, tt=tt):
            v.wait_ge(sA.h, n_evc)
            if _DEBUG_SCRATCH and tt == 0:
                v.wait_ge(sG.h, n_dbg_qkv)
            v = DR(v)
            qa = bass.AP(q_sb, 0, [[D, 128], [64, 16], [0, 16], [1, 64]])
            ka = bass.AP(k_sb, 0, [[D, 128], [0, 16], [64, 16], [1, 64]])
            pa = bass.AP(prod, 0, [[16384, 128], [1024, 16], [64, 16], [1, 64]])
            v.tensor_tensor(pa, qa, ka, ALU.mult)
            v.tensor_reduce(bass.AP(sc_sb, 0, [[256, 128], [16, 16], [1, 16]]),
                            bass.AP(prod, 0,
                                    [[16384, 128], [1024, 16], [64, 16], [1, 64]]),
                            AX.X, ALU.add)
            v.tensor_reduce(bass.AP(mx_sb, 0, [[16, 128], [1, 16]]),
                            bass.AP(sc_sb, 0, [[256, 128], [16, 16], [1, 16]]),
                            AX.X, ALU.max)
            v.tensor_tensor(sc_sb[:, :],
                            sc_sb[:, :],
                            bass.AP(mx_sb, 0, [[16, 128], [1, 16], [0, 16]]),
                            ALU.subtract).then_inc(sV.h, 1)
        sV.n += 1
        n_scd = sV.n

        @blk.scalar
        def _(s, n_scd=n_scd):
            s.wait_ge(sV.h, n_scd)
            s.activation(e_sb[:, :], sc_sb[:, :], AF.Exp, bias=0.0,
                         scale=0.125).then_inc(sA.h, 1)
        sA.n += 1
        n_exp = sA.n

        @blk.vector
        def _(v, tt=tt, pb=pb, n_exp=n_exp, w_ctx=w_ctx):
            v.wait_ge(sA.h, n_exp)
            v = DR(v)
            if w_ctx:
                v.wait_ge(sQ.h, w_ctx)
            v.tensor_reduce(bass.AP(sm_sb, 0, [[16, 128], [1, 16]]),
                            bass.AP(e_sb, 0, [[256, 128], [16, 16], [1, 16]]),
                            AX.X, ALU.add)
            v.reciprocal(sm_sb[:, :], sm_sb[:, :])
            v.tensor_tensor(at_sb[:, :], e_sb[:, :],
                            bass.AP(sm_sb, 0, [[16, 128], [1, 16], [0, 16]]),
                            ALU.mult)
            v.drain()
            aa = bass.AP(at_sb, 0, [[256, 128], [16, 16], [0, 64], [1, 16]])
            va = bass.AP(v_sb, 0, [[D, 128], [0, 16], [1, 64], [64, 16]])
            p2a = bass.AP(prod, 0, [[16384, 128], [1024, 16], [16, 64], [1, 16]])
            v.tensor_tensor(p2a, aa, va, ALU.mult)
            v.tensor_reduce(bass.AP(ctx_f, 0, [[D, 128], [64, 16], [1, 64]]),
                            bass.AP(prod, 0,
                                    [[16384, 128], [1024, 16], [16, 64], [1, 16]]),
                            AX.X, ALU.add).then_inc(sV.h, 1)
        sV.n += 1
        ctx_v.append(sV.n)
        n_cv = sV.n

        @blk.scalar
        def _(s, pb=pb, n_cv=n_cv, w_ctx=w_ctx):
            s.wait_ge(sV.h, n_cv)
            if w_ctx:
                s.wait_ge(sQ.h, w_ctx)
            s.activation(ctx_sb[:, pb * D:(pb + 1) * D], ctx_f[:, :],
                         AF.Copy).then_inc(sA.h, 1)
        sA.n += 1
        n_cc = sA.n

        if _DEBUG_SCRATCH and tt == 0:
            @blk.gpsimd
            def _(g, n_cv=n_cv):
                g.wait_ge(sV.h, n_cv)
                g.dma_start(dap(s_dbg, 3 * 131072, [[1024, 128], [1, 256]]),
                            sc_sb[:, :]).then_inc(sG.h, 16)
                g.dma_start(dap(s_dbg, 4 * 131072, [[1024, 128], [1, 256]]),
                            e_sb[:, :]).then_inc(sG.h, 16)
                g.dma_start(dap(s_dbg, 5 * 131072, [[1024, 128], [1, 256]]),
                            at_sb[:, :]).then_inc(sG.h, 16)
                g.dma_start(dap(s_dbg, 6 * 131072, [[1024, 128], [1, 1024]]),
                            ctx_f[:, :]).then_inc(sG.h, 16)
                g.dma_start(dap(s_dbg, 7 * 131072, [[1024, 128], [1, 1024]]),
                            xqs[:, 0:D]).then_inc(sG.h, 16)
                g.dma_start(dap(s_dbg, 3 * 131072 + 512, [[1024, 128], [1, 8]]),
                            gam[:, :]).then_inc(sG.h, 16)
                g.dma_start(dap(s_dbg, 3 * 131072 + 576, [[1024, 128], [1, 64]]),
                            kq[:, :]).then_inc(sG.h, 16)
            sG.n += 112

        @blk.sync
        def _(s, tt=tt, pb=pb, n_cv=n_cc):
            s.wait_ge(sA.h, n_cv)
            if _DEBUG_SCRATCH and tt == 0:
                s.wait_ge(sG.h, sG.n)
            for h in range(H):
                s.dma_start(dap(s_ctx, h * T * 64 + tt * 128 * 64,
                                [[64, 128], [1, 64]]),
                            dap(ctx_sb, pb * D + h * 64,
                                [[2 * D, 128], [1, 64]])).then_inc(sQ.h, 16)
        sQ.n += 16 * H
        ctx_sp.append(sQ.n)

        if prep_i < len(prep_units) and tt in (10, 20):
            prep_units[prep_i]()
            prep_i += 1

    while prep_i < len(prep_units):
        prep_units[prep_i]()
        prep_i += 1

    n_abc_mm = sM.n
    n_ctx_all = ctx_sp[-1]
    if _STOP_AFTER == "ABC":
        @blk.vector
        def _(v):
            v.wait_ge(sQ.h, sQ.n)
            v.wait_ge(sG.h, sG.n)
            v.wait_ge(sA.h, sA.n)
        _stub_out(nc, blk, out, tb1, sQ)
        for stk in (a_es, bk_es, w_es, r_es, p_es, g_es):
            stk.close()
        return nc
    prep_attn(w_o, wt_q, 3, tguard=n_abc_mm)
    n_prep_q = sQ.n
    n_prep_g = sG.n
    a_es.close()
    bk_es.close()

    xln2T = sb(g_es, "xln2T", [128, NC8 * T], BF16, is_transpose_dest=True,
               side="right")
    d_es = ExitStack()
    ctd = sb(d_es, "ctd", [128, 2 * D], BF16)
    xd = sb(d_es, "xd", [128, 2 * D], F32)
    scrd = sb(d_es, "scrd", [128, 2 * D], F32)
    cqs = sb(d_es, "cqs", [128, 2 * D], BF16)
    cqT = sb(d_es, "cqT", [128, 2 * D], BF16, is_transpose_dest=True)
    aod = sb(d_es, "aod", [128, D], F32)
    x1d = sb(d_es, "x1d", [128, 2 * D], F32)
    xl2 = sb(d_es, "xl2", [128, 2 * D], BF16)

    d_v1, d_tp, d_fin, d_lt, d_xsp = [], [], [], [], []
    for tt in range(NT):
        pb = tt % 2
        ko = 20 * pb
        w_fin = d_fin[tt - 2] if tt >= 2 else 0

        @blk.gpsimd
        def _(g, tt=tt, pb=pb, w_fin=w_fin):
            if tt == 0:
                g.wait_ge(sQ.h, n_ctx_all)
                g.wait_ge(sQ.h, n_prep_q)
                g.wait_ge(sG.h, n_prep_g)
            if w_fin:
                g.wait_ge(sV.h, w_fin)
            g.dma_start(dap(ctd, pb * D, [[2 * D, 128], [1, D]]),
                        dap(s_ctx, tt * 128 * D, [[D, 128], [1, D]])).then_inc(sG.h, 16)
            g.dma_start(dap(xd, pb * D, [[2 * D, 128], [1, D]]),
                        dap(x_in, tt * 128 * D, [[D, 128], [1, D]])).then_inc(sG.h, 16)
        sG.n += 32
        n_ld = sG.n
        w_v1 = d_v1[tt - 2] if tt >= 2 else 0

        @blk.scalar
        def _(s, pb=pb, ko=ko, n_ld=n_ld, w_v1=w_v1):
            s.wait_ge(sG.h, n_ld)
            if w_v1:
                s.wait_ge(sV.h, w_v1)
            s.activation(scrd[:, pb * D:(pb + 1) * D], ctd[:, pb * D:(pb + 1) * D],
                         AF.Square, accum_out=kq[:, ko + 0:ko + 1]).then_inc(sA.h, 1)
        sA.n += 1
        n_sq = sA.n

        @blk.vector
        def _(v, ko=ko, n_sq=n_sq):
            v.wait_ge(sA.h, n_sq)
            v = DR(v)
            v.tensor_scalar(kq[:, ko + 1:ko + 2], kq[:, ko + 0:ko + 1], 1.0 / D,
                            1e-6, ALU.mult, ALU.add).then_inc(sV.h, 1)
        sV.n += 1
        n_ms = sV.n

        @blk.scalar
        def _(s, ko=ko, n_ms=n_ms):
            s.wait_ge(sV.h, n_ms)
            rsqrt(s, kq[:, ko + 2:ko + 3], kq[:, ko + 3:ko + 4], kq[:, ko + 1:ko + 2]).then_inc(sA.h, 1)
        sA.n += 1
        n_rr = sA.n
        w_tp = d_tp[tt - 2] if tt >= 2 else 0

        @blk.vector
        def _(v, pb=pb, ko=ko, n_rr=n_rr, w_tp=w_tp):
            v.wait_ge(sA.h, n_rr)
            v = DR(v)
            if w_tp:
                v.wait_ge(sQ.h, w_tp)
            ct = ctd[:, pb * D:(pb + 1) * D]
            sc = scrd[:, pb * D:(pb + 1) * D]
            v.reduce_max(kq[:, ko + 4:ko + 5], ct, AX.X, apply_absolute_value=True)
            v.tensor_tensor(kq[:, ko + 4:ko + 5], kq[:, ko + 4:ko + 5],
                            kq[:, ko + 2:ko + 3], ALU.mult)
            v.tensor_scalar(kq[:, ko + 4:ko + 5], kq[:, ko + 4:ko + 5], EPS,
                            None, ALU.max)
            v.reciprocal(kq[:, ko + 5:ko + 6], kq[:, ko + 4:ko + 5])
            v.tensor_scalar(kq[:, ko + 5:ko + 6], kq[:, ko + 5:ko + 6], 127.0,
                            None, ALU.mult)
            v.tensor_scalar(kq[:, ko + 6:ko + 7], kq[:, ko + 4:ko + 5], 1.0 / 127.0,
                            None, ALU.mult)
            v.tensor_tensor(kq[:, ko + 7:ko + 8], kq[:, ko + 2:ko + 3],
                            kq[:, ko + 5:ko + 6], ALU.mult)
            v.tensor_scalar(sc, ct, kq[:, ko + 7:ko + 8], None, ALU.mult)
            rnd(v, cqs[:, pb * D:(pb + 1) * D], sc, D,
                post_scale=kq[:, ko + 6:ko + 7]).then_inc(sV.h, 1)
        sV.n += 1
        d_v1.append(sV.n)
        n_v1 = sV.n

        @blk.sync
        def _(s, pb=pb, n_v1=n_v1):
            s.wait_ge(sV.h, n_v1)
            for j in range(NC8):
                s.dma_start(cqT[:, pb * D + j * 128:pb * D + (j + 1) * 128],
                            cqs[:, pb * D + j * 128:pb * D + (j + 1) * 128],
                            transpose=True).then_inc(sQ.h, 16)
        sQ.n += 16 * NC8
        d_tp.append(sQ.n)
        n_tp = sQ.n
        w_ev = d_fin[tt - 1] if tt >= 1 else 0

        @blk.tensor
        def _(t, pb=pb, n_tp=n_tp, w_ev=w_ev):
            t.wait_ge(sQ.h, n_tp)
            if w_ev:
                t.wait_ge(sV.h, w_ev)
            mi = None
            for j in range(NC8):
                lhs = cqT[:, pb * D + j * 128:pb * D + (j + 1) * 128]
                mi = t.matmul(ps_c[:, :], lhs, wt_q[:, j * D + 0:j * D + 512],
                              start=(j == 0), stop=(j == NC8 - 1))
                mi = t.matmul(ps_d[:, :], lhs, wt_q[:, j * D + 512:j * D + 1024],
                              start=(j == 0), stop=(j == NC8 - 1))
            mi.then_inc(sM.h, 1)
        sM.n += 1
        n_omm = sM.n

        @blk.vector
        def _(v, ko=ko, n_omm=n_omm):
            v.wait_ge(sM.h, n_omm)
            v = DR(v)
            v.tensor_scalar(aod[:, 0:512], ps_c[:, :], gam[:, 3:4], None,
                            ALU.mult, ALU.add, accum_out=kq[:, ko + 8:ko + 9])
            v.tensor_scalar(aod[:, 512:1024], ps_d[:, :], gam[:, 3:4], None,
                            ALU.mult, ALU.add, accum_out=kq[:, ko + 9:ko + 10])
            v.tensor_tensor(kq[:, ko + 8:ko + 9], kq[:, ko + 8:ko + 9],
                            kq[:, ko + 9:ko + 10], ALU.add)
            v.tensor_scalar(kq[:, ko + 8:ko + 9], kq[:, ko + 8:ko + 9], 1.0 / D,
                            None, ALU.mult).then_inc(sV.h, 1)
        sV.n += 1
        n_aev = sV.n

        @blk.scalar
        def _(s, pb=pb, ko=ko, n_aev=n_aev):
            s.wait_ge(sV.h, n_aev)
            s.activation(scrd[:, pb * D:(pb + 1) * D], aod[:, :], AF.Square,
                         accum_out=kq[:, ko + 10:ko + 11]).then_inc(sA.h, 1)
        sA.n += 1
        n_sq2 = sA.n

        @blk.vector
        def _(v, ko=ko, n_sq2=n_sq2):
            v.wait_ge(sA.h, n_sq2)
            v = DR(v)
            v.tensor_scalar(kq[:, ko + 10:ko + 11], kq[:, ko + 10:ko + 11], 1.0 / D,
                            None, ALU.mult)
            v.tensor_tensor(kq[:, ko + 11:ko + 12], kq[:, ko + 8:ko + 9],
                            kq[:, ko + 8:ko + 9], ALU.mult)
            v.tensor_tensor(kq[:, ko + 10:ko + 11], kq[:, ko + 10:ko + 11],
                            kq[:, ko + 11:ko + 12], ALU.subtract)
            v.tensor_scalar(kq[:, ko + 10:ko + 11], kq[:, ko + 10:ko + 11], EPS,
                            None, ALU.add).then_inc(sV.h, 1)
        sV.n += 1
        n_vr = sV.n

        @blk.scalar
        def _(s, ko=ko, n_vr=n_vr):
            s.wait_ge(sV.h, n_vr)
            rsqrt(s, kq[:, ko + 12:ko + 13], kq[:, ko + 13:ko + 14],
                  kq[:, ko + 10:ko + 11]).then_inc(sA.h, 1)
        sA.n += 1
        n_rs2 = sA.n

        @blk.vector
        def _(v, pb=pb, ko=ko, n_rs2=n_rs2):
            v.wait_ge(sA.h, n_rs2)
            v = DR(v)
            sc = scrd[:, pb * D:(pb + 1) * D]
            v.tensor_scalar(sc, aod[:, :], kq[:, ko + 8:ko + 9],
                            kq[:, ko + 12:ko + 13], ALU.subtract, ALU.mult)
            v.tensor_tensor(x1d[:, pb * D:(pb + 1) * D], sc,
                            xd[:, pb * D:(pb + 1) * D], ALU.add).then_inc(sV.h, 1)
        sV.n += 1
        n_x1 = sV.n

        @blk.gpsimd
        def _(g, tt=tt, pb=pb, n_x1=n_x1):
            g.wait_ge(sV.h, n_x1)
            g.dma_start(dap(s_x1, tt * 128 * D, [[D, 128], [1, D]]),
                        dap(x1d, pb * D, [[2 * D, 128], [1, D]])).then_inc(sG.h, 16)
        sG.n += 16
        d_xsp.append(sG.n)

        @blk.vector
        def _(v, pb=pb, ko=ko):
            v = DR(v)
            v.reduce_sum(kq[:, ko + 14:ko + 15], x1d[:, pb * D:(pb + 1) * D], AX.X)
            v.tensor_scalar(kq[:, ko + 14:ko + 15], kq[:, ko + 14:ko + 15], 1.0 / D,
                            None, ALU.mult).then_inc(sV.h, 1)
        sV.n += 1
        n_m2 = sV.n

        @blk.scalar
        def _(s, pb=pb, ko=ko, n_m2=n_m2):
            s.wait_ge(sV.h, n_m2)
            s.activation(scrd[:, pb * D:(pb + 1) * D], x1d[:, pb * D:(pb + 1) * D],
                         AF.Square, accum_out=kq[:, ko + 15:ko + 16]).then_inc(sA.h, 1)
        sA.n += 1
        n_sq3 = sA.n

        @blk.vector
        def _(v, ko=ko, n_sq3=n_sq3):
            v.wait_ge(sA.h, n_sq3)
            v = DR(v)
            v.tensor_scalar(kq[:, ko + 15:ko + 16], kq[:, ko + 15:ko + 16], 1.0 / D,
                            None, ALU.mult)
            v.tensor_tensor(kq[:, ko + 16:ko + 17], kq[:, ko + 14:ko + 15],
                            kq[:, ko + 14:ko + 15], ALU.mult)
            v.tensor_tensor(kq[:, ko + 15:ko + 16], kq[:, ko + 15:ko + 16],
                            kq[:, ko + 16:ko + 17], ALU.subtract)
            v.tensor_scalar(kq[:, ko + 15:ko + 16], kq[:, ko + 15:ko + 16], EPS,
                            None, ALU.add).then_inc(sV.h, 1)
        sV.n += 1
        n_v2 = sV.n

        @blk.scalar
        def _(s, ko=ko, n_v2=n_v2):
            s.wait_ge(sV.h, n_v2)
            rsqrt(s, kq[:, ko + 17:ko + 18], kq[:, ko + 18:ko + 19],
                  kq[:, ko + 15:ko + 16]).then_inc(sA.h, 1)
        sA.n += 1
        n_rl2 = sA.n

        @blk.vector
        def _(v, pb=pb, ko=ko, n_rl2=n_rl2):
            v.wait_ge(sA.h, n_rl2)
            v = DR(v)
            v.tensor_scalar(xl2[:, pb * D:(pb + 1) * D], x1d[:, pb * D:(pb + 1) * D],
                            kq[:, ko + 14:ko + 15], kq[:, ko + 17:ko + 18],
                            ALU.subtract, ALU.mult).then_inc(sV.h, 1)
        sV.n += 1
        d_fin.append(sV.n)
        n_fin2 = sV.n

        @blk.sync
        def _(s, tt=tt, pb=pb, n_fin2=n_fin2):
            s.wait_ge(sV.h, n_fin2)
            for j in range(NC8):
                s.dma_start(xln2T[:, j * T + tt * 128:j * T + (tt + 1) * 128],
                            xl2[:, pb * D + j * 128:pb * D + (j + 1) * 128],
                            transpose=True).then_inc(sQ.h, 16)
        sQ.n += 16 * NC8
        d_lt.append(sQ.n)

    n_x1_all = d_xsp[-1]
    if _STOP_AFTER == "D":
        @blk.vector
        def _(v):
            v.wait_ge(sQ.h, sQ.n)
            v.wait_ge(sG.h, sG.n)
        _stub_out(nc, blk, out, tb1, sQ)
        for stk in (d_es, w_es, r_es, p_es, g_es):
            stk.close()
        return nc
    d_es.close()
    w_es.close()
    r_es.close()

    e_es = ExitStack()
    w1s = sb(e_es, "w1s", [128, 2 * 1024], BF16)
    w2s = sb(e_es, "w2s", [128, 2 * 4096], BF16)
    hT = sb(e_es, "hT", [128, 32 * 1024], BF16)
    yst = sb(e_es, "yst", [128, 2 * 1024], BF16)
    ytok = sb(e_es, "ytok", [128, 8 * 1024], BF16, is_transpose_dest=True)
    x1f = sb(e_es, "x1f", [128, 2 * D], F32)
    scre = sb(e_es, "scre", [128, 2 * D], F32)
    obuf = sb(e_es, "obuf", [128, 2 * D], F32)

    out_st = []
    ffn2_mm_prev = 0
    fin_prev_block = 0
    f1_mm = []
    f2_mm = []
    for Tb in range(4):
        relu_cnt = []
        w1_ld = []
        for fc in range(32):
            gi = Tb * 32 + fc
            w_slab = f1_mm[gi - 2] if gi >= 2 else 0

            @blk.gpsimd
            def _(g, Tb=Tb, fc=fc, w_slab=w_slab):
                if Tb == 0 and fc == 0:
                    g.wait_ge(sG.h, n_x1_all)
                if w_slab:
                    g.wait_ge(sM.h, w_slab)
                g.dma_start(dap(w1s, (fc % 2) * 1024, [[2048, 128], [1, 1024]]),
                            dap(s_w1t, fc * 131072, [[128, 128], [16384, 8], [1, 128]])
                            ).then_inc(sG.h, 16)
            sG.n += 16
            w1_ld.append(sG.n)
            n_sl = sG.n
            w_relu = relu_cnt[fc - 1] if fc >= 1 else 0

            @blk.tensor
            def _(t, Tb=Tb, fc=fc, n_sl=n_sl, w_relu=w_relu,
                  w_prev2=ffn2_mm_prev):
                t.wait_ge(sG.h, n_sl)
                if fc == 0:
                    t.wait_ge(sQ.h, d_lt[-1])
                    if w_prev2:
                        t.wait_ge(sM.h, w_prev2)
                if w_relu:
                    t.wait_ge(sA.h, w_relu)
                mi = None
                for dc in range(NC8):
                    lhs = w1s[:, (fc % 2) * 1024 + dc * 128:(fc % 2) * 1024 + (dc + 1) * 128]
                    for n in range(2):
                        dst = ps_c if n == 0 else ps_d
                        mi = t.matmul(dst[:, :], lhs,
                                      xln2T[:, dc * T + Tb * 1024 + n * 512:
                                            dc * T + Tb * 1024 + (n + 1) * 512],
                                      start=(dc == 0), stop=(dc == NC8 - 1))
                mi.then_inc(sM.h, 1)
            sM.n += 1
            f1_mm.append(sM.n)
            n_f1 = sM.n

            @blk.scalar
            def _(s, fc=fc, n_f1=n_f1):
                s.wait_ge(sM.h, n_f1)
                s.activation(hT[:, fc * 1024 + 0:fc * 1024 + 512], ps_c[:, :],
                             AF.Relu, bias=b1c[:, fc:fc + 1], scale=1.0)
                s.activation(hT[:, fc * 1024 + 512:fc * 1024 + 1024], ps_d[:, :],
                             AF.Relu, bias=b1c[:, fc:fc + 1], scale=1.0
                             ).then_inc(sA.h, 1)
            sA.n += 1
            relu_cnt.append(sA.n)
        n_relu_all = sA.n

        w2_ld = []
        yev = []
        ytp = []
        for dc in range(NC8):
            gj = Tb * NC8 + dc
            w_slab2 = f2_mm[gj - 2] if gj >= 2 else 0

            @blk.gpsimd
            def _(g, dc=dc, w_slab2=w_slab2):
                if w_slab2:
                    g.wait_ge(sM.h, w_slab2)
                g.dma_start(dap(w2s, (dc % 2) * 4096, [[8192, 128], [1, 4096]]),
                            dap(s_w2t, dc * 524288, [[128, 128], [16384, 32], [1, 128]])
                            ).then_inc(sG.h, 16)
            sG.n += 16
            w2_ld.append(sG.n)
            n_sl = sG.n
            w_yev = yev[dc - 1] if dc >= 1 else 0

            @blk.tensor
            def _(t, dc=dc, n_sl=n_sl, n_relu_all=n_relu_all, w_yev=w_yev):
                t.wait_ge(sG.h, n_sl)
                if dc == 0:
                    t.wait_ge(sA.h, n_relu_all)
                if w_yev:
                    t.wait_ge(sV.h, w_yev)
                mi = None
                for fc in range(32):
                    lhs = w2s[:, (dc % 2) * 4096 + fc * 128:(dc % 2) * 4096 + (fc + 1) * 128]
                    for n in range(2):
                        mi = t.matmul(ps_a[:, n * 512:(n + 1) * 512], lhs,
                                      hT[:, fc * 1024 + n * 512:fc * 1024 + (n + 1) * 512],
                                      start=(fc == 0), stop=(fc == 31))
                mi.then_inc(sM.h, 1)
            sM.n += 1
            f2_mm.append(sM.n)
            n_f2 = sM.n
            w_ytp = ytp[dc - 2] if dc >= 2 else 0

            @blk.vector
            def _(v, dc=dc, n_f2=n_f2, w_ytp=w_ytp):
                v.wait_ge(sM.h, n_f2)
                if w_ytp:
                    v.wait_ge(sQ.h, w_ytp)
                mi = None
                for n in range(2):
                    mi = v.tensor_scalar(yst[:, (dc % 2) * 1024 + n * 512:
                                             (dc % 2) * 1024 + (n + 1) * 512],
                                         ps_a[:, n * 512:(n + 1) * 512],
                                         b2c[:, dc:dc + 1], None, ALU.add)
                mi.then_inc(sV.h, 1)
            sV.n += 1
            yev.append(sV.n)
            n_ye = sV.n

            @blk.sync
            def _(s, dc=dc, n_ye=n_ye, fin_prev_block=fin_prev_block):
                s.wait_ge(sV.h, n_ye)
                if fin_prev_block:
                    s.wait_ge(sV.h, fin_prev_block)
                for ts_ in range(8):
                    s.dma_start(ytok[:, ts_ * 1024 + dc * 128:ts_ * 1024 + (dc + 1) * 128],
                                yst[:, (dc % 2) * 1024 + ts_ * 128:
                                    (dc % 2) * 1024 + (ts_ + 1) * 128],
                                transpose=True).then_inc(sQ.h, 16)
            sQ.n += 128
            ytp.append(sQ.n)
        n_ytp_all = sQ.n

        fin_ts = []
        for ts_ in range(8):
            pb = ts_ % 2
            ko = 20 * pb
            w_fin = fin_ts[ts_ - 2] if ts_ >= 2 else 0

            @blk.gpsimd
            def _(g, Tb=Tb, ts_=ts_, pb=pb, w_fin=w_fin):
                if w_fin:
                    g.wait_ge(sV.h, w_fin)
                g.dma_start(dap(x1f, pb * D, [[2 * D, 128], [1, D]]),
                            dap(s_x1, (Tb * 1024 + ts_ * 128) * D, [[D, 128], [1, D]])
                            ).then_inc(sG.h, 16)
            sG.n += 16
            n_xl = sG.n

            @blk.vector
            def _(v, ts_=ts_, ko=ko, n_ytp_all=n_ytp_all):
                v.wait_ge(sQ.h, n_ytp_all)
                v = DR(v)
                v.reduce_sum(kq[:, ko + 0:ko + 1], ytok[:, ts_ * 1024:(ts_ + 1) * 1024],
                             AX.X)
                v.tensor_scalar(kq[:, ko + 0:ko + 1], kq[:, ko + 0:ko + 1], 1.0 / D,
                                None, ALU.mult).then_inc(sV.h, 1)
            sV.n += 1
            n_sm = sV.n

            @blk.scalar
            def _(s, ts_=ts_, pb=pb, ko=ko, n_sm=n_sm):
                s.wait_ge(sV.h, n_sm)
                s.activation(scre[:, pb * D:(pb + 1) * D],
                             ytok[:, ts_ * 1024:(ts_ + 1) * 1024], AF.Square,
                             accum_out=kq[:, ko + 1:ko + 2]).then_inc(sA.h, 1)
            sA.n += 1
            n_sq4 = sA.n

            @blk.vector
            def _(v, ko=ko, n_sq4=n_sq4):
                v.wait_ge(sA.h, n_sq4)
                v = DR(v)
                v.tensor_scalar(kq[:, ko + 1:ko + 2], kq[:, ko + 1:ko + 2], 1.0 / D,
                                None, ALU.mult)
                v.tensor_tensor(kq[:, ko + 2:ko + 3], kq[:, ko + 0:ko + 1],
                                kq[:, ko + 0:ko + 1], ALU.mult)
                v.drain()
                v.tensor_tensor(kq[:, ko + 1:ko + 2], kq[:, ko + 1:ko + 2],
                                kq[:, ko + 2:ko + 3], ALU.subtract)
                v.drain()
                v.tensor_scalar(kq[:, ko + 1:ko + 2], kq[:, ko + 1:ko + 2], EPS,
                                None, ALU.add).then_inc(sV.h, 1)
            sV.n += 1
            n_v4 = sV.n

            @blk.scalar
            def _(s, ko=ko, n_v4=n_v4):
                s.wait_ge(sV.h, n_v4)
                rsqrt(s, kq[:, ko + 3:ko + 4], kq[:, ko + 4:ko + 5],
                      kq[:, ko + 1:ko + 2]).then_inc(sA.h, 1)
            sA.n += 1
            n_r4 = sA.n

            w_ost = out_st[len(out_st) - 2] if len(out_st) >= 2 else 0

            @blk.vector
            def _(v, ts_=ts_, pb=pb, ko=ko, n_r4=n_r4, n_xl=n_xl, w_ost=w_ost):
                v.wait_ge(sA.h, n_r4)
                v = DR(v)
                v.wait_ge(sG.h, n_xl)
                if w_ost:
                    v.wait_ge(sQ.h, w_ost)
                sc = scre[:, pb * D:(pb + 1) * D]
                v.tensor_scalar(sc, ytok[:, ts_ * 1024:(ts_ + 1) * 1024],
                                kq[:, ko + 0:ko + 1], kq[:, ko + 3:ko + 4],
                                ALU.subtract, ALU.mult)
                v.tensor_tensor(obuf[:, pb * D:(pb + 1) * D], sc,
                                x1f[:, pb * D:(pb + 1) * D], ALU.add).then_inc(sV.h, 1)
            sV.n += 1
            fin_ts.append(sV.n)
            n_ob = sV.n

            @blk.sync
            def _(s, Tb=Tb, ts_=ts_, pb=pb, n_ob=n_ob):
                s.wait_ge(sV.h, n_ob)
                s.dma_start(dap(out, (Tb * 1024 + ts_ * 128) * D, [[D, 128], [1, D]]),
                            dap(obuf, pb * D, [[2 * D, 128], [1, D]])).then_inc(sQ.h, 16)
            sQ.n += 16
            out_st.append(sQ.n)
        fin_prev_block = fin_ts[-1]
        ffn2_mm_prev = sM.n

    n_final = sQ.n

    @blk.sync
    def _(s):
        s.wait_ge(sQ.h, n_final)

    e_es.close()
    p_es.close()
    g_es.close()
    return nc


_STOP_AFTER = None
_DEBUG_SCRATCH = False
_NC_CACHE = None


def kernel(L0_attn_x, L0_attn_wq, L0_attn_wk, L0_attn_wv, L0_attn_wo,
           L0_ffn_W1, L0_ffn_b1, L0_ffn_W2, L0_ffn_b2):
    global _NC_CACHE
    x = np.ascontiguousarray(np.asarray(L0_attn_x, dtype=np.float32))
    wq = np.ascontiguousarray(np.asarray(L0_attn_wq, dtype=np.float32))
    wk = np.ascontiguousarray(np.asarray(L0_attn_wk, dtype=np.float32))
    wv = np.ascontiguousarray(np.asarray(L0_attn_wv, dtype=np.float32))
    wo = np.ascontiguousarray(np.asarray(L0_attn_wo, dtype=np.float32))
    w1 = np.ascontiguousarray(np.asarray(L0_ffn_W1, dtype=np.float32))
    b1 = np.ascontiguousarray(np.asarray(L0_ffn_b1, dtype=np.float32)).reshape(32, 128)
    w2 = np.ascontiguousarray(np.asarray(L0_ffn_W2, dtype=np.float32))
    b2 = np.ascontiguousarray(np.asarray(L0_ffn_b2, dtype=np.float32)).reshape(8, 128)

    if _NC_CACHE is None:
        _NC_CACHE = build()
    nc = _NC_CACHE
    in_maps = [{"x": np.ascontiguousarray(x[i]), "wq": wq, "wk": wk, "wv": wv,
                "wo": wo, "w1": w1, "b1": b1, "w2": w2, "b2": b2}
               for i in range(8)]
    res = run_bass_kernel_spmd(nc, in_maps, core_ids=list(range(8)))
    return np.stack([r["out"] for r in res.results], axis=0)


# version 05
# speedup vs baseline: 1.1688x; 1.1688x over previous
import numpy as np
from contextlib import ExitStack

import concourse.bass as bass
import concourse.mybir as mybir
from concourse.bass_utils import run_bass_kernel_spmd

F32 = mybir.dt.float32
BF16 = mybir.dt.bfloat16
AF = mybir.ActivationFunctionType
ALU = mybir.AluOpType
AX = mybir.AxisListType

T, D, H, DH, FF = 4096, 1024, 16, 64, 4096
NT, NC8, NS = 32, 8, 8
C23 = 12582912.0
C52 = 6755399441055744.0
EPS = 1e-5


def dap(t, off, dims):
    return bass.AP(t, off, [list(d) for d in dims])


class DR:
    _OPS = {"tensor_scalar", "tensor_tensor", "tensor_copy", "reduce_sum",
            "reduce_max", "tensor_reduce", "reciprocal", "memset",
            "scalar_tensor_tensor", "tensor_single_scalar", "activation"}

    def __init__(self, e):
        self._e = e

    def __getattr__(self, name):
        f = getattr(self._e, name)
        if name in DR._OPS:
            def g(*a, **k):
                r = f(*a, **k)
                self._e.drain()
                return r
            return g
        return f


def build():
    nc = bass.Bass("TRN2", target_bir_lowering=False,
                   detect_race_conditions=False)

    x_in = nc.dram_tensor("x", [T, D], F32, kind="ExternalInput")
    w_q = nc.dram_tensor("wq", [D, D], F32, kind="ExternalInput")
    w_k = nc.dram_tensor("wk", [D, D], F32, kind="ExternalInput")
    w_v = nc.dram_tensor("wv", [D, D], F32, kind="ExternalInput")
    w_o = nc.dram_tensor("wo", [D, D], F32, kind="ExternalInput")
    w_1 = nc.dram_tensor("w1", [FF, D], F32, kind="ExternalInput")
    b_1 = nc.dram_tensor("b1", [32, 128], F32, kind="ExternalInput")
    w_2 = nc.dram_tensor("w2", [D, FF], F32, kind="ExternalInput")
    b_2 = nc.dram_tensor("b2", [8, 128], F32, kind="ExternalInput")
    out = nc.dram_tensor("out", [T, D], F32, kind="ExternalOutput")

    s_ctx = nc.dram_tensor("s_ctx", [T, D], BF16)
    s_x1 = nc.dram_tensor("s_x1", [T, D], F32)
    s_w1t = nc.dram_tensor("s_w1t", [32, 8, 128, 128], BF16)
    s_w2t = nc.dram_tensor("s_w2t", [8, 32, 128, 128], BF16)

    g_es = ExitStack()

    def sb(stk, name, shape, dtype, **kw):
        return stk.enter_context(nc.sbuf_tensor(name, shape, dtype, **kw))

    wsel = sb(g_es, "wsel", [128, 1], F32)
    ones = sb(g_es, "ones", [128, 128], F32)
    kq = sb(g_es, "kq", [128, 64], F32)
    gam = sb(g_es, "gam", [128, 8], F32)
    b1c = sb(g_es, "b1c", [128, 32], F32)
    b2c = sb(g_es, "b2c", [128, 16], F32)
    gat = sb(g_es, "gat", [128, 128], BF16)
    gatT = sb(g_es, "gatT", [128, 128], BF16, is_transpose_dest=True)
    st = sb(g_es, "st", [128, 64], F32)

    def stc(p, i):
        return st[:, 16 * p + 4 * i:16 * p + 4 * (i + 1)]

    def stcol(p, i, t):
        c = 16 * p + 4 * i + t
        return st[:, c:c + 1]

    def stbc(p, i):
        return dap(st, 16 * p + 4 * i, [[64, 128], [1, 4], [0, 1024]])

    r_es = ExitStack()
    tb1 = sb(r_es, "tb1", [128, 4096], F32)
    tb2 = sb(r_es, "tb2", [128, 4096], F32)

    wt_es = ExitStack()
    wt_q = sb(wt_es, "wt_q", [128, NC8 * D], BF16, is_transpose_dest=True)

    w_es = ExitStack()
    wbig = sb(w_es, "wbig", [128, 4096], F32)
    qbuf = sb(w_es, "qbuf", [128, 4096], BF16)
    wtst = sb(w_es, "wtst", [128, 4096], BF16, is_transpose_dest=True)
    bk_es = ExitStack()
    wt_k = sb(bk_es, "wt_k", [128, NC8 * D], BF16, is_transpose_dest=True)
    wt_v = sb(bk_es, "wt_v", [128, NC8 * D], BF16, is_transpose_dest=True)

    p_es = ExitStack()
    ps_g = p_es.enter_context(nc.psum_tensor("ps_g", [128, 512], F32))
    ps_a = p_es.enter_context(nc.psum_tensor("ps_a", [128, 1024], F32))
    ps_b = p_es.enter_context(nc.psum_tensor("ps_b", [128, 1024], F32))
    ps_c = p_es.enter_context(nc.psum_tensor("ps_c", [128, 512], F32))
    ps_d = p_es.enter_context(nc.psum_tensor("ps_d", [128, 512], F32))

    class Ctr:
        def __init__(self, name):
            self.h = g_es.enter_context(nc.semaphore(name))
            self.n = 0
    sQ, sG, sV, sA, sM, sS = (Ctr("sQ"), Ctr("sG"), Ctr("sV"), Ctr("sA"),
                              Ctr("sM"), Ctr("sS"))

    blk = g_es.enter_context(nc.Block())

    def rnd_inplace(v):
        v.tensor_scalar(tb1[:, :], tb2[:, :], C23, C23, ALU.add, ALU.subtract)
        v.tensor_scalar(tb2[:, :], tb2[:, :], C52, C52, ALU.add, ALU.subtract)
        v.tensor_tensor(tb2[:, :], tb2[:, :], tb1[:, :], ALU.subtract)
        return v.scalar_tensor_tensor(tb1[:, :], tb2[:, :], wsel[:, 0:1],
                                      tb1[:, :], ALU.mult, ALU.add)

    def rsqrt(s, dst, tmp, src):
        s.activation(tmp, src, AF.Ln)
        s.drain()
        return s.activation(dst, tmp, AF.Exp, bias=0.0, scale=-0.5)

    @blk.scalar
    def _(s):
        s.add_instruction(mybir.InstLoadActFuncSet(
            name=nc.get_next_instruction_name(), act_func_set_id=6, ins=[], outs=[]))

    @blk.vector
    def _(v):
        v.memset(wsel[:, :], 2.0 ** -30)
        v.drain()
        v.tensor_scalar(wsel[:, :], wsel[:, :], 1.0, 1.0, ALU.add, ALU.subtract)
        v.drain()
        v.tensor_scalar(wsel[:, :], wsel[:, :], 1e30, 1.0, ALU.mult, ALU.min)
        v.drain()
        v.memset(ones[:, :], 1.0)
        v.memset(gat[:, :], 0.0)
        v.memset(wbig[0:32, 128:256], 0.0).then_inc(sV.h, 1)
    sV.n += 1
    n_ms0 = sV.n

    @blk.gpsimd
    def _(g):
        g.wait_ge(sV.h, n_ms0)
        g.dma_start(dap(wbig, 0, [[4096, 32], [1, 128]]),
                    dap(b_1, 0, [[128, 32], [1, 128]])).then_inc(sG.h, 16)
        g.dma_start(dap(wbig, 128, [[4096, 8], [1, 128]]),
                    dap(b_2, 0, [[128, 8], [1, 128]])).then_inc(sG.h, 16)
    sG.n += 32
    n_bld = sG.n

    @blk.vector
    def _(v):
        v.wait_ge(sG.h, n_bld)
        v.tensor_copy(gat[0:32, 0:128], wbig[0:32, 0:128])
        v.tensor_tensor(qbuf[0:32, 0:128], wbig[0:32, 0:128],
                        gat[0:32, 0:128], ALU.subtract)
        v.tensor_copy(qbuf[0:16, 128:256], wbig[0:16, 128:256])
        v.tensor_tensor(qbuf[0:16, 256:384], wbig[0:16, 128:256],
                        qbuf[0:16, 128:256], ALU.subtract).then_inc(sV.h, 1)
    sV.n += 1
    n_bv = sV.n

    @blk.sync
    def _(s):
        s.wait_ge(sV.h, n_bv)
        s.dma_start(gatT[:, 0:32], gat[0:32, 0:128], transpose=True).then_inc(sQ.h, 16)
        s.dma_start(wtst[:, 0:32], qbuf[0:32, 0:128], transpose=True).then_inc(sQ.h, 16)
        s.dma_start(gatT[:, 32:48], qbuf[0:16, 128:256], transpose=True).then_inc(sQ.h, 16)
        s.dma_start(wtst[:, 32:48], qbuf[0:16, 256:384], transpose=True).then_inc(sQ.h, 16)
    sQ.n += 64
    n_bt = sQ.n

    @blk.vector
    def _(v):
        v.wait_ge(sQ.h, n_bt)
        v.tensor_tensor(b1c[:, :], gatT[:, 0:32], wtst[:, 0:32], ALU.add)
        v.tensor_tensor(b2c[:, :], gatT[:, 32:48], wtst[:, 32:48],
                        ALU.add).then_inc(sV.h, 1)
    sV.n += 1

    def prep_attn(w_dram, dst_wt, gcol, tguard=0):
        for half in range(2):
            red = sV.n

            @blk.gpsimd
            def _(g, half=half, red=red):
                g.wait_ge(sV.h, red)
                g.dma_start(dap(wbig, 0, [[4096, 128], [1024, 4], [1, 1024]]),
                            dap(w_dram, half * 524288,
                                [[1024, 128], [131072, 4], [1, 1024]])).then_inc(sG.h, 16)
            sG.n += 16
            ld = sG.n

            @blk.vector
            def _(v, half=half, ld=ld):
                v.wait_ge(sG.h, ld)
                v.reduce_sum(kq[:, 40 + half:41 + half], wbig[:, :], AX.X,
                             apply_absolute_value=True).then_inc(sV.h, 1)
            sV.n += 1

        @blk.vector
        def _(v):
            v.drain()
            v.tensor_tensor(kq[:, 42:43], kq[:, 40:41], kq[:, 41:42],
                            ALU.add).then_inc(sV.h, 1)
        sV.n += 1
        pd = sV.n

        @blk.tensor
        def _(t, pd=pd):
            t.wait_ge(sV.h, pd)
            t.matmul(ps_g[:, 0:1], ones[:, :], kq[:, 42:43],
                     start=True, stop=True).then_inc(sM.h, 1)
        sM.n += 1
        md = sM.n

        @blk.vector
        def _(v, md=md):
            v.wait_ge(sM.h, md)
            v = DR(v)
            v.tensor_scalar(kq[:, 43:44], ps_g[:, 0:1], 1.0 / (D * D), EPS,
                            ALU.mult, ALU.max)
            v.tensor_copy(gam[:, gcol:gcol + 1], kq[:, 43:44])
            v.reciprocal(kq[:, 44:45], kq[:, 43:44]).then_inc(sV.h, 1)
        sV.n += 1

        for half in range(2):
            qd0 = sV.n
            tp0 = sQ.n
            actg = sA.n

            @blk.gpsimd
            def _(g, half=half, qd0=qd0):
                g.wait_ge(sV.h, qd0)
                g.dma_start(dap(wbig, 0, [[4096, 128], [1024, 4], [1, 1024]]),
                            dap(w_dram, half * 524288,
                                [[1024, 128], [131072, 4], [1, 1024]])).then_inc(sG.h, 16)
            sG.n += 16
            ld = sG.n

            @blk.vector
            def _(v, half=half, ld=ld, tp0=tp0, actg=actg):
                v.wait_ge(sG.h, ld)
                if actg:
                    v.wait_ge(sA.h, actg)
                if half == 1:
                    v.wait_ge(sQ.h, tp0)
                v.tensor_scalar(tb2[:, :], wbig[:, :], kq[:, 44:45], None, ALU.mult)
                v.tensor_scalar(tb1[:, :], tb2[:, :], C23, C23, ALU.add, ALU.subtract)
                v.tensor_scalar(wbig[:, :], tb2[:, :], C52, C52, ALU.add, ALU.subtract)
                v.tensor_tensor(wbig[:, :], wbig[:, :], tb1[:, :], ALU.subtract)
                v.scalar_tensor_tensor(tb1[:, :], wbig[:, :], wsel[:, 0:1],
                                       tb1[:, :], ALU.mult, ALU.add)
                v.tensor_scalar(qbuf[:, :], tb1[:, :], -1.0, 1.0,
                                ALU.max, ALU.min).then_inc(sV.h, 1)
            sV.n += 1
            qd = sV.n

            @blk.sync
            def _(s, half=half, qd=qd, tguard=tguard):
                s.wait_ge(sV.h, qd)
                if tguard and half == 0:
                    s.wait_ge(sM.h, tguard)
                for cl in range(4):
                    c = half * 4 + cl
                    for j in range(NC8):
                        s.dma_start(dst_wt[:, j * D + c * 128:j * D + (c + 1) * 128],
                                    qbuf[:, cl * D + j * 128:cl * D + (j + 1) * 128],
                                    transpose=True).then_inc(sQ.h, 16)
            sQ.n += 16 * 32
        return sQ.n

    def prep_ffn(w_dram, spill, layout):
        for grp in range(8):
            red = sV.n

            @blk.gpsimd
            def _(g, grp=grp, red=red):
                g.wait_ge(sV.h, red)
                if layout == "w1":
                    g.dma_start(dap(wbig, 0, [[4096, 128], [1024, 4], [1, 1024]]),
                                dap(w_dram, grp * 524288,
                                    [[1024, 128], [131072, 4], [1, 1024]])
                                ).then_inc(sG.h, 16)
                else:
                    g.dma_start(dap(wbig, 0, [[4096, 128], [1, 4096]]),
                                dap(w_dram, grp * 524288, [[4096, 128], [1, 4096]])
                                ).then_inc(sG.h, 16)
            sG.n += 16
            ld = sG.n

            @blk.vector
            def _(v, grp=grp, ld=ld):
                v.wait_ge(sG.h, ld)
                v.reduce_sum(kq[:, 45:46], wbig[:, :], AX.X, apply_absolute_value=True)
                v.drain()
                if grp == 0:
                    v.tensor_copy(kq[:, 46:47], kq[:, 45:46]).then_inc(sV.h, 1)
                else:
                    v.tensor_tensor(kq[:, 46:47], kq[:, 46:47], kq[:, 45:46],
                                    ALU.add).then_inc(sV.h, 1)
            sV.n += 1
            yield
        pd = sV.n

        @blk.tensor
        def _(t, pd=pd):
            t.wait_ge(sV.h, pd)
            t.matmul(ps_g[:, 0:1], ones[:, :], kq[:, 46:47],
                     start=True, stop=True).then_inc(sM.h, 1)
        sM.n += 1
        md = sM.n

        @blk.vector
        def _(v, md=md):
            v.wait_ge(sM.h, md)
            v = DR(v)
            v.tensor_scalar(kq[:, 43:44], ps_g[:, 0:1], 1.0 / (FF * D), EPS,
                            ALU.mult, ALU.add)
            v.reciprocal(kq[:, 44:45], kq[:, 43:44]).then_inc(sV.h, 1)
        sV.n += 1
        yield

        for grp in range(8):
            qd0 = sV.n
            actg = sA.n

            @blk.gpsimd
            def _(g, grp=grp, qd0=qd0):
                g.wait_ge(sV.h, qd0)
                if layout == "w1":
                    g.dma_start(dap(wbig, 0, [[4096, 128], [1024, 4], [1, 1024]]),
                                dap(w_dram, grp * 524288,
                                    [[1024, 128], [131072, 4], [1, 1024]])
                                ).then_inc(sG.h, 16)
                else:
                    g.dma_start(dap(wbig, 0, [[4096, 128], [1, 4096]]),
                                dap(w_dram, grp * 524288, [[4096, 128], [1, 4096]])
                                ).then_inc(sG.h, 16)
            sG.n += 16
            ld = sG.n
            spl_prev = sG.n

            @blk.vector
            def _(v, grp=grp, ld=ld, actg=actg):
                v.wait_ge(sG.h, ld)
                if actg:
                    v.wait_ge(sA.h, actg)
                v.tensor_scalar(tb2[:, :], wbig[:, :], kq[:, 44:45], None, ALU.mult)
                v.tensor_scalar(tb1[:, :], tb2[:, :], C23, C23, ALU.add, ALU.subtract)
                v.tensor_scalar(wbig[:, :], tb2[:, :], C52, C52, ALU.add, ALU.subtract)
                v.tensor_tensor(wbig[:, :], wbig[:, :], tb1[:, :], ALU.subtract)
                v.scalar_tensor_tensor(tb1[:, :], wbig[:, :], wsel[:, 0:1],
                                       tb1[:, :], ALU.mult, ALU.add)
                v.tensor_scalar(qbuf[:, :], tb1[:, :], -1.0, 1.0,
                                ALU.max, ALU.min).then_inc(sV.h, 1)
            sV.n += 1
            qd = sV.n

            @blk.sync
            def _(s, grp=grp, qd=qd, spl_prev=spl_prev):
                s.wait_ge(sV.h, qd)
                s.wait_ge(sG.h, spl_prev)
                for bi in range(32):
                    s.dma_start(wtst[:, bi * 128:(bi + 1) * 128],
                                qbuf[:, bi * 128:(bi + 1) * 128],
                                transpose=True).then_inc(sQ.h, 16)
            sQ.n += 16 * 32
            tp = sQ.n

            @blk.gpsimd
            def _(g, grp=grp, tp=tp):
                g.wait_ge(sQ.h, tp)
                if layout == "w1":
                    g.dma_start(dap(spill, grp * 4 * 131072,
                                    [[128, 128], [131072, 4], [16384, 8], [1, 128]]),
                                dap(wtst, 0, [[4096, 128], [1024, 4], [128, 8], [1, 128]])
                                ).then_inc(sG.h, 16)
                else:
                    g.dma_start(dap(spill, grp * 524288,
                                    [[128, 128], [16384, 32], [1, 128]]),
                                dap(wtst, 0, [[4096, 128], [128, 32], [1, 128]])
                                ).then_inc(sG.h, 16)
            sG.n += 16
            yield

    prep_attn(w_v, wt_v, 2)
    prep_attn(w_q, wt_q, 0)
    n_wprep = prep_attn(w_k, wt_k, 1)

    a_es = ExitStack()
    xa = sb(a_es, "xa", [128, 4096], F32)
    sqr = sb(a_es, "sqr", [128, 1024], F32)
    xqs = sb(a_es, "xqs", [128, 2 * 4096], BF16)
    xqTt = sb(a_es, "xqTt", [128, 2 * D], BF16, is_transpose_dest=True)
    q_sb = sb(a_es, "q_sb", [128, D], BF16)
    k_sb = sb(a_es, "k_sb", [128, D], BF16)
    v_sb = sb(a_es, "v_sb", [128, D], BF16)
    prod = sb(a_es, "prod", [128, 8192], BF16)
    sc_sb = sb(a_es, "sc_sb", [128, 256], F32)
    mx_sb = sb(a_es, "mx_sb", [128, 16], F32)
    e_sb = sb(a_es, "e_sb", [128, 256], F32)
    sm_sb = sb(a_es, "sm_sb", [128, 16], F32)
    at_sb = sb(a_es, "at_sb", [128, 256], BF16)
    ctx_f = sb(a_es, "ctx_f", [128, D], F32)
    ctx_sb = sb(a_es, "ctx_sb", [128, 2 * D], BF16)

    pbanks = [(ps_a, 0), (ps_a, 512), (ps_b, 0), (ps_b, 512), (ps_c, 0), (ps_d, 0)]
    a_t1a, a_rnd, a_xq, s_tpq, a_xld = [], [], [], [], []
    ev_c, ctx_v, ctx_sp, mm_c = [], [], [], []

    prep_gens = [prep_ffn(w_1, s_w1t, "w1"), prep_ffn(w_2, s_w2t, "w2")]

    def pull_prep(k):
        while k > 0 and prep_gens:
            try:
                next(prep_gens[0])
                k -= 1
            except StopIteration:
                prep_gens.pop(0)

    def emit_xload(S):
        w_t1a = a_t1a[S - 1] if S >= 1 else 0

        @blk.gpsimd
        def _(g, S=S, w_t1a=w_t1a):
            if w_t1a:
                g.wait_ge(sA.h, w_t1a)
            g.dma_start(dap(xa, 0, [[4096, 128], [1024, 4], [1, 1024]]),
                        dap(x_in, S * 4 * 131072,
                            [[1024, 128], [131072, 4], [1, 1024]])).then_inc(sG.h, 16)
        sG.n += 16
        a_xld.append(sG.n)

    emit_xload(0)
    for S in range(NS):
        pb = S % 2
        n_xld = a_xld[S]
        w_rnd2 = a_rnd[S - 2] if S >= 2 else 0

        @blk.scalar
        def _(s, pb=pb, n_xld=n_xld, w_rnd2=w_rnd2):
            s.wait_ge(sG.h, n_xld)
            if w_rnd2:
                s.wait_ge(sV.h, w_rnd2)
            mi = None
            for t in range(4):
                xt = xa[:, t * 1024:(t + 1) * 1024]
                s.activation(sqr[:, :], xt, AF.Copy, accum_out=stcol(pb, 0, t))
                mi = s.activation(sqr[:, :], xt, AF.Square,
                                  accum_out=stcol(pb, 1, t))
            mi.then_inc(sA.h, 1)
        sA.n += 1
        n_st8 = sA.n

        @blk.vector
        def _(v, pb=pb, n_st8=n_st8):
            v.wait_ge(sA.h, n_st8)
            v = DR(v)
            v.tensor_scalar(stc(pb, 0), stc(pb, 0), -1.0 / D, None, ALU.mult)
            v.tensor_scalar(stc(pb, 1), stc(pb, 1), 1.0 / D, None, ALU.mult)
            v.tensor_tensor(stc(pb, 2), stc(pb, 0), stc(pb, 0), ALU.mult)
            v.tensor_tensor(stc(pb, 1), stc(pb, 1), stc(pb, 2), ALU.subtract)
            v.tensor_scalar(stc(pb, 1), stc(pb, 1), 1.0 + 1e-6, 1e-11,
                            ALU.mult, ALU.add).then_inc(sV.h, 1)
        sV.n += 1
        n_ty1 = sV.n

        @blk.scalar
        def _(s, pb=pb, n_ty1=n_ty1):
            s.wait_ge(sV.h, n_ty1)
            rsqrt(s, stc(pb, 2), stc(pb, 3), stc(pb, 1))
            s.drain()
            mi = None
            for t in range(4):
                mi = s.activation(tb2[:, t * 1024:(t + 1) * 1024],
                                  xa[:, t * 1024:(t + 1) * 1024], AF.Identity,
                                  bias=stcol(pb, 0, t), scale=1.0)
            mi.then_inc(sA.h, 1)
        sA.n += 1
        a_t1a.append(sA.n)
        n_t1a = sA.n

        @blk.vector
        def _(v, pb=pb, n_t1a=n_t1a):
            v.wait_ge(sA.h, n_t1a)
            vr = DR(v)
            vr.tensor_reduce(stc(pb, 1),
                             dap(tb2, 0, [[4096, 128], [1024, 4], [1, 1024]]),
                             AX.X, ALU.max, apply_absolute_value=True)
            vr.tensor_tensor(stc(pb, 1), stc(pb, 1), stc(pb, 2), ALU.mult)
            vr.tensor_scalar(stc(pb, 1), stc(pb, 1), EPS, None, ALU.max)
            vr.reciprocal(stc(pb, 3), stc(pb, 1))
            vr.tensor_scalar(stc(pb, 3), stc(pb, 3), 127.0, None, ALU.mult)
            vr.tensor_scalar(stc(pb, 1), stc(pb, 1), 1.0 / 127.0, None, ALU.mult)
            vr.tensor_tensor(stc(pb, 2), stc(pb, 2), stc(pb, 3), ALU.mult)
            v.tensor_tensor(dap(tb2, 0, [[4096, 128], [1024, 4], [1, 1024]]),
                            dap(tb2, 0, [[4096, 128], [1024, 4], [1, 1024]]),
                            stbc(pb, 2), ALU.mult)
            rnd_inplace(v).then_inc(sV.h, 1)
        sV.n += 1
        a_rnd.append(sV.n)
        n_rnd = sV.n
        w_tpq = s_tpq[S - 2] if S >= 2 else 0

        @blk.scalar
        def _(s, pb=pb, n_rnd=n_rnd, w_tpq=w_tpq):
            s.wait_ge(sV.h, n_rnd)
            if w_tpq:
                s.wait_ge(sQ.h, w_tpq)
            mi = None
            for t in range(4):
                mi = s.activation(xqs[:, pb * 4096 + t * 1024:pb * 4096 + (t + 1) * 1024],
                                  tb1[:, t * 1024:(t + 1) * 1024], AF.Copy,
                                  bias=0.0, scale=stcol(pb, 1, t))
            mi.then_inc(sA.h, 1)
        sA.n += 1
        a_xq.append(sA.n)
        n_xq = sA.n

        for t4 in range(4):
            tt = S * 4 + t4
            tpb = tt % 2
            w_mm = mm_c[tt - 2] if tt >= 2 else 0

            @blk.sync
            def _(s, t4=t4, pb=pb, tpb=tpb, n_xq=n_xq, w_mm=w_mm):
                s.wait_ge(sA.h, n_xq)
                if w_mm:
                    s.wait_ge(sM.h, w_mm)
                for j in range(NC8):
                    s.dma_start(xqTt[:, tpb * D + j * 128:tpb * D + (j + 1) * 128],
                                xqs[:, pb * 4096 + t4 * D + j * 128:
                                    pb * 4096 + t4 * D + (j + 1) * 128],
                                transpose=True).then_inc(sQ.h, 16)
            sQ.n += 16 * NC8
            n_tp = sQ.n
            w_ev = ev_c[tt - 1] if tt >= 1 else 0

            @blk.tensor
            def _(t, tt=tt, tpb=tpb, n_tp=n_tp, w_ev=w_ev):
                t.wait_ge(sQ.h, n_tp)
                if tt == 0:
                    t.wait_ge(sQ.h, n_wprep)
                if w_ev:
                    t.wait_ge(sA.h, w_ev)
                mi = None
                for j in range(NC8):
                    lhs = xqTt[:, tpb * D + j * 128:tpb * D + (j + 1) * 128]
                    for pi, wt in enumerate((wt_q, wt_k, wt_v)):
                        for n in range(2):
                            pb_, off = pbanks[pi * 2 + n]
                            mi = t.matmul(pb_[:, off:off + 512], lhs,
                                          wt[:, j * D + n * 512:j * D + (n + 1) * 512],
                                          start=(j == 0), stop=(j == NC8 - 1))
                mi.then_inc(sM.h, 1)
            sM.n += 1
            mm_c.append(sM.n)
            n_mm = sM.n
            w_mix = ctx_v[tt - 1] if tt >= 1 else 0

            @blk.scalar
            def _(s, n_mm=n_mm, w_mix=w_mix):
                s.wait_ge(sM.h, n_mm)
                if w_mix:
                    s.wait_ge(sV.h, w_mix)
                mi = None
                for pi, dst in enumerate((q_sb, k_sb, v_sb)):
                    for n in range(2):
                        pb_, off = pbanks[pi * 2 + n]
                        mi = s.activation(dst[:, n * 512:(n + 1) * 512],
                                          pb_[:, off:off + 512],
                                          AF.Copy, bias=0.0, scale=gam[:, pi:pi + 1])
                mi.then_inc(sA.h, 1)
            sA.n += 1
            ev_c.append(sA.n)
            n_evc = sA.n

            @blk.vector
            def _(v, n_evc=n_evc):
                v.wait_ge(sA.h, n_evc)
                for hh in range(2):
                    qa = dap(q_sb, hh * 512, [[D, 128], [64, 8], [0, 16], [1, 64]])
                    ka = dap(k_sb, 0, [[D, 128], [0, 8], [64, 16], [1, 64]])
                    pa = dap(prod, 0, [[8192, 128], [1024, 8], [64, 16], [1, 64]])
                    v.tensor_tensor(pa, qa, ka, ALU.mult)
                    v.tensor_reduce(dap(sc_sb, hh * 128, [[256, 128], [16, 8], [1, 16]]),
                                    dap(prod, 0,
                                        [[8192, 128], [1024, 8], [64, 16], [1, 64]]),
                                    AX.X, ALU.add)
                v.tensor_reduce(dap(mx_sb, 0, [[16, 128], [1, 16]]),
                                dap(sc_sb, 0, [[256, 128], [16, 16], [1, 16]]),
                                AX.X, ALU.max)
                v.drain()
                v.tensor_tensor(sc_sb[:, :], sc_sb[:, :],
                                dap(mx_sb, 0, [[16, 128], [1, 16], [0, 16]]),
                                ALU.subtract).then_inc(sV.h, 1)
            sV.n += 1
            n_scd = sV.n

            @blk.scalar
            def _(s, n_scd=n_scd):
                s.wait_ge(sV.h, n_scd)
                s.activation(e_sb[:, :], sc_sb[:, :], AF.Exp, bias=0.0,
                             scale=0.125).then_inc(sA.h, 1)
            sA.n += 1
            n_exp = sA.n

            @blk.vector
            def _(v, n_exp=n_exp):
                v.wait_ge(sA.h, n_exp)
                v.tensor_reduce(dap(sm_sb, 0, [[16, 128], [1, 16]]),
                                dap(e_sb, 0, [[256, 128], [16, 16], [1, 16]]),
                                AX.X, ALU.add)
                v.drain()
                v.reciprocal(sm_sb[:, :], sm_sb[:, :])
                v.drain()
                v.tensor_tensor(at_sb[:, :], e_sb[:, :],
                                dap(sm_sb, 0, [[16, 128], [1, 16], [0, 16]]),
                                ALU.mult)
                v.drain()
                mi = None
                for hh in range(2):
                    aa = dap(at_sb, hh * 128, [[256, 128], [16, 8], [0, 64], [1, 16]])
                    va = dap(v_sb, 0, [[D, 128], [0, 8], [1, 64], [64, 16]])
                    p2a = dap(prod, 0, [[8192, 128], [1024, 8], [16, 64], [1, 16]])
                    v.tensor_tensor(p2a, aa, va, ALU.mult)
                    mi = v.tensor_reduce(dap(ctx_f, hh * 512, [[D, 128], [64, 8], [1, 64]]),
                                         dap(prod, 0,
                                             [[8192, 128], [1024, 8], [16, 64], [1, 16]]),
                                         AX.X, ALU.add)
                mi.then_inc(sV.h, 1)
            sV.n += 1
            ctx_v.append(sV.n)
            n_cv = sV.n
            w_ctx = ctx_sp[tt - 2] if tt >= 2 else 0

            @blk.scalar
            def _(s, tpb=tpb, n_cv=n_cv, w_ctx=w_ctx):
                s.wait_ge(sV.h, n_cv)
                if w_ctx:
                    s.wait_ge(sG.h, w_ctx)
                s.activation(ctx_sb[:, tpb * D:(tpb + 1) * D], ctx_f[:, :],
                             AF.Copy).then_inc(sA.h, 1)
            sA.n += 1
            n_cc = sA.n

            @blk.gpsimd
            def _(g, tt=tt, tpb=tpb, n_cc=n_cc):
                g.wait_ge(sA.h, n_cc)
                g.dma_start(dap(s_ctx, tt * 128 * 64,
                                [[64, 128], [262144, 16], [1, 64]]),
                            dap(ctx_sb, tpb * D,
                                [[2 * D, 128], [64, 16], [1, 64]])).then_inc(sG.h, 16)
            sG.n += 16
            ctx_sp.append(sG.n)
        s_tpq.append(sQ.n)
        if S + 1 < NS:
            emit_xload(S + 1)
        pull_prep(6)

    while prep_gens:
        pull_prep(8)

    n_abc_mm = sM.n
    n_ctx_all = ctx_sp[-1]
    prep_attn(w_o, wt_q, 3, tguard=n_abc_mm)
    n_prep_q = sQ.n
    n_prep_g = sG.n
    a_es.close()
    bk_es.close()
    w_es.close()

    xln2T = sb(g_es, "xln2T", [128, NC8 * T], BF16, is_transpose_dest=True,
               side="right")
    d_es = ExitStack()
    ctd = sb(d_es, "ctd", [128, 2 * 4096], BF16)
    xd = sb(d_es, "xd", [128, 4096], F32)
    sqd = sb(d_es, "sqd", [128, 1024], F32)
    cqs = sb(d_es, "cqs", [128, 4096], BF16)
    cqT = sb(d_es, "cqT", [128, 2 * D], BF16, is_transpose_dest=True)
    aod = sb(d_es, "aod", [128, 4096], F32)
    x1d = sb(d_es, "x1d", [128, 4096], F32)
    xl2 = sb(d_es, "xl2", [128, 4096], BF16)

    d_rnd, d_x1, d_fin, d_xsp, d_ld = [], [], [], [], []
    d_tpq_s, d_tps_s, d_ltq_s, d_lts_s = [], [], [], []
    d_mm, d_aev = [], []

    def emit_dload(S):
        pb = S % 2
        ofs = pb * 4096
        w_ctd = d_rnd[S - 2] if S >= 2 else 0
        w_xd = d_x1[S - 1] if S >= 1 else 0

        @blk.gpsimd
        def _(g, S=S, ofs=ofs, w_ctd=w_ctd, w_xd=w_xd):
            if S == 0:
                g.wait_ge(sQ.h, n_ctx_all) if False else None
                g.wait_ge(sG.h, n_ctx_all)
                g.wait_ge(sQ.h, n_prep_q)
                g.wait_ge(sG.h, n_prep_g)
            if w_ctd:
                g.wait_ge(sV.h, w_ctd)
            g.dma_start(dap(ctd, ofs, [[2 * 4096, 128], [1024, 4], [1, 1024]]),
                        dap(s_ctx, S * 4 * 131072,
                            [[1024, 128], [131072, 4], [1, 1024]])).then_inc(sG.h, 16)
            if w_xd:
                g.wait_ge(sV.h, w_xd)
            g.dma_start(dap(xd, 0, [[4096, 128], [1024, 4], [1, 1024]]),
                        dap(x_in, S * 4 * 131072,
                            [[1024, 128], [131072, 4], [1, 1024]])).then_inc(sG.h, 16)
        sG.n += 32
        d_ld.append(sG.n)

    emit_dload(0)
    for S in range(NS):
        pb = S % 2
        ofs = pb * 4096
        n_ld = d_ld[S]
        w_st = d_fin[S - 2] if S >= 2 else 0

        @blk.scalar
        def _(s, pb=pb, ofs=ofs, n_ld=n_ld, w_st=w_st):
            s.wait_ge(sG.h, n_ld)
            if w_st:
                s.wait_ge(sV.h, w_st)
            mi = None
            for t in range(4):
                mi = s.activation(sqd[:, :], ctd[:, ofs + t * 1024:ofs + (t + 1) * 1024],
                                  AF.Square, accum_out=stcol(pb, 0, t))
            mi.then_inc(sA.h, 1)
        sA.n += 1
        n_st4 = sA.n

        @blk.vector
        def _(v, pb=pb, n_st4=n_st4):
            v.wait_ge(sA.h, n_st4)
            v.tensor_scalar(stc(pb, 0), stc(pb, 0), 1.0 / D, 1e-6,
                            ALU.mult, ALU.add).then_inc(sV.h, 1)
        sV.n += 1
        n_ty1 = sV.n

        @blk.scalar
        def _(s, pb=pb, n_ty1=n_ty1):
            s.wait_ge(sV.h, n_ty1)
            rsqrt(s, stc(pb, 2), stc(pb, 3), stc(pb, 0)).then_inc(sA.h, 1)
        sA.n += 1
        n_rr = sA.n

        @blk.vector
        def _(v, pb=pb, ofs=ofs, n_rr=n_rr):
            v.wait_ge(sA.h, n_rr)
            vr = DR(v)
            vr.tensor_reduce(stc(pb, 1),
                             dap(ctd, ofs, [[2 * 4096, 128], [1024, 4], [1, 1024]]),
                             AX.X, ALU.max, apply_absolute_value=True)
            vr.tensor_tensor(stc(pb, 1), stc(pb, 1), stc(pb, 2), ALU.mult)
            vr.tensor_scalar(stc(pb, 1), stc(pb, 1), EPS, None, ALU.max)
            vr.reciprocal(stc(pb, 3), stc(pb, 1))
            vr.tensor_scalar(stc(pb, 3), stc(pb, 3), 127.0, None, ALU.mult)
            vr.tensor_scalar(stc(pb, 1), stc(pb, 1), 1.0 / 127.0, None, ALU.mult)
            vr.tensor_tensor(stc(pb, 2), stc(pb, 2), stc(pb, 3), ALU.mult)
            v.tensor_tensor(dap(tb2, 0, [[4096, 128], [1024, 4], [1, 1024]]),
                            dap(ctd, ofs, [[2 * 4096, 128], [1024, 4], [1, 1024]]),
                            stbc(pb, 2), ALU.mult)
            rnd_inplace(v).then_inc(sV.h, 1)
        sV.n += 1
        d_rnd.append(sV.n)
        n_rnd = sV.n
        w_tpq = d_tpq_s[S - 1] if S >= 1 else 0
        w_tps = d_tps_s[S - 1] if S >= 1 else 0

        @blk.scalar
        def _(s, pb=pb, n_rnd=n_rnd, w_tpq=w_tpq, w_tps=w_tps):
            s.wait_ge(sV.h, n_rnd)
            if w_tpq:
                s.wait_ge(sQ.h, w_tpq)
                s.wait_ge(sS.h, w_tps)
            mi = None
            for t in range(4):
                mi = s.activation(cqs[:, t * 1024:(t + 1) * 1024],
                                  tb1[:, t * 1024:(t + 1) * 1024], AF.Copy,
                                  bias=0.0, scale=stcol(pb, 1, t))
            mi.then_inc(sA.h, 1)
        sA.n += 1
        n_cq = sA.n

        for t4 in range(4):
            tt = S * 4 + t4
            tpb = tt % 2
            w_mmo = d_mm[tt - 2] if tt >= 2 else 0

            def tpd(s, t4=t4, tpb=tpb, n_cq=n_cq, w_mmo=w_mmo, q=(t4 % 2 == 0)):
                s.wait_ge(sA.h, n_cq)
                if w_mmo:
                    s.wait_ge(sM.h, w_mmo)
                for j in range(NC8):
                    s.dma_start(cqT[:, tpb * D + j * 128:tpb * D + (j + 1) * 128],
                                cqs[:, t4 * D + j * 128:t4 * D + (j + 1) * 128],
                                transpose=True).then_inc((sQ if q else sS).h, 16)
            if t4 % 2 == 0:
                blk.sync(tpd)
                sQ.n += 16 * NC8
            else:
                blk.scalar(tpd)
                sS.n += 16 * NC8
            n_tpq, n_tps = sQ.n, sS.n
            w_aev = d_aev[tt - 1] if tt >= 1 else 0

            @blk.tensor
            def _(t, tpb=tpb, n_tpq=n_tpq, n_tps=n_tps, w_aev=w_aev):
                t.wait_ge(sQ.h, n_tpq)
                t.wait_ge(sS.h, n_tps)
                if w_aev:
                    t.wait_ge(sV.h, w_aev)
                mi = None
                for j in range(NC8):
                    lhs = cqT[:, tpb * D + j * 128:tpb * D + (j + 1) * 128]
                    mi = t.matmul(ps_c[:, :], lhs, wt_q[:, j * D + 0:j * D + 512],
                                  start=(j == 0), stop=(j == NC8 - 1))
                    mi = t.matmul(ps_d[:, :], lhs, wt_q[:, j * D + 512:j * D + 1024],
                                  start=(j == 0), stop=(j == NC8 - 1))
                mi.then_inc(sM.h, 1)
            sM.n += 1
            d_mm.append(sM.n)
            n_omm = sM.n

            @blk.vector
            def _(v, t4=t4, n_omm=n_omm):
                v.wait_ge(sM.h, n_omm)
                v.tensor_scalar(aod[:, t4 * D:t4 * D + 512], ps_c[:, :],
                                gam[:, 3:4], None, ALU.mult, ALU.add,
                                accum_out=st[:, 48 + t4:49 + t4])
                v.tensor_scalar(aod[:, t4 * D + 512:(t4 + 1) * D], ps_d[:, :],
                                gam[:, 3:4], None, ALU.mult, ALU.add,
                                accum_out=st[:, 52 + t4:53 + t4]).then_inc(sV.h, 1)
            sV.n += 1
            d_aev.append(sV.n)
        n_aev4 = sV.n

        @blk.scalar
        def _(s, pb=pb, n_aev4=n_aev4):
            s.wait_ge(sV.h, n_aev4)
            mi = None
            for t in range(4):
                mi = s.activation(sqd[:, :], aod[:, t * 1024:(t + 1) * 1024],
                                  AF.Square, accum_out=stcol(pb, 1, t))
            mi.then_inc(sA.h, 1)
        sA.n += 1
        n_sqa = sA.n

        @blk.vector
        def _(v, pb=pb, n_sqa=n_sqa):
            v.wait_ge(sA.h, n_sqa)
            v = DR(v)
            v.tensor_tensor(stc(pb, 0), st[:, 48:52], st[:, 52:56], ALU.add)
            v.tensor_scalar(stc(pb, 0), stc(pb, 0), -1.0 / D, None, ALU.mult)
            v.tensor_scalar(stc(pb, 1), stc(pb, 1), 1.0 / D, None, ALU.mult)
            v.tensor_tensor(stc(pb, 2), stc(pb, 0), stc(pb, 0), ALU.mult)
            v.tensor_tensor(stc(pb, 1), stc(pb, 1), stc(pb, 2), ALU.subtract)
            v.tensor_scalar(stc(pb, 1), stc(pb, 1), EPS, None,
                            ALU.add).then_inc(sV.h, 1)
        sV.n += 1
        n_ty2 = sV.n

        @blk.scalar
        def _(s, pb=pb, n_ty2=n_ty2):
            s.wait_ge(sV.h, n_ty2)
            rsqrt(s, stc(pb, 2), stc(pb, 3), stc(pb, 1))
            s.drain()
            mi = None
            for t in range(4):
                mi = s.activation(tb2[:, t * 1024:(t + 1) * 1024],
                                  aod[:, t * 1024:(t + 1) * 1024], AF.Identity,
                                  bias=stcol(pb, 0, t), scale=1.0)
            mi.then_inc(sA.h, 1)
        sA.n += 1
        n_t1b = sA.n
        w_xsp = d_xsp[S - 1] if S >= 1 else 0

        @blk.vector
        def _(v, pb=pb, n_t1b=n_t1b, w_xsp=w_xsp):
            v.wait_ge(sA.h, n_t1b)
            if w_xsp:
                v.wait_ge(sG.h, w_xsp)
            v.tensor_tensor(dap(tb2, 0, [[4096, 128], [1024, 4], [1, 1024]]),
                            dap(tb2, 0, [[4096, 128], [1024, 4], [1, 1024]]),
                            stbc(pb, 2), ALU.mult)
            v.tensor_tensor(x1d[:, :], tb2[:, :], xd[:, :],
                            ALU.add).then_inc(sV.h, 1)
        sV.n += 1
        d_x1.append(sV.n)
        n_x1 = sV.n
        if S + 1 < NS:
            emit_dload(S + 1)

        @blk.gpsimd
        def _(g, S=S, n_x1=n_x1):
            g.wait_ge(sV.h, n_x1)
            g.dma_start(dap(s_x1, S * 4 * 131072,
                            [[1024, 128], [131072, 4], [1, 1024]]),
                        dap(x1d, 0, [[4096, 128], [1024, 4], [1, 1024]])
                        ).then_inc(sG.h, 16)
        sG.n += 16
        d_xsp.append(sG.n)

        @blk.scalar
        def _(s, pb=pb, n_x1=n_x1):
            s.wait_ge(sV.h, n_x1)
            mi = None
            for t in range(4):
                xt = x1d[:, t * 1024:(t + 1) * 1024]
                s.activation(sqd[:, :], xt, AF.Copy, accum_out=stcol(pb, 0, t))
                mi = s.activation(sqd[:, :], xt, AF.Square,
                                  accum_out=stcol(pb, 1, t))
            mi.then_inc(sA.h, 1)
        sA.n += 1
        n_stx = sA.n

        @blk.vector
        def _(v, pb=pb, n_stx=n_stx):
            v.wait_ge(sA.h, n_stx)
            v = DR(v)
            v.tensor_scalar(stc(pb, 0), stc(pb, 0), -1.0 / D, None, ALU.mult)
            v.tensor_scalar(stc(pb, 1), stc(pb, 1), 1.0 / D, None, ALU.mult)
            v.tensor_tensor(stc(pb, 2), stc(pb, 0), stc(pb, 0), ALU.mult)
            v.tensor_tensor(stc(pb, 1), stc(pb, 1), stc(pb, 2), ALU.subtract)
            v.tensor_scalar(stc(pb, 1), stc(pb, 1), EPS, None,
                            ALU.add).then_inc(sV.h, 1)
        sV.n += 1
        n_ty3 = sV.n

        @blk.scalar
        def _(s, pb=pb, n_ty3=n_ty3):
            s.wait_ge(sV.h, n_ty3)
            rsqrt(s, stc(pb, 2), stc(pb, 3), stc(pb, 1))
            s.drain()
            mi = None
            for t in range(4):
                mi = s.activation(tb2[:, t * 1024:(t + 1) * 1024],
                                  x1d[:, t * 1024:(t + 1) * 1024], AF.Identity,
                                  bias=stcol(pb, 0, t), scale=1.0)
            mi.then_inc(sA.h, 1)
        sA.n += 1
        n_t1c = sA.n
        w_lt = d_ltq_s[S - 1] if S >= 1 else 0
        w_lts = d_lts_s[S - 1] if S >= 1 else 0

        @blk.vector
        def _(v, pb=pb, n_t1c=n_t1c, w_lt=w_lt, w_lts=w_lts):
            v.wait_ge(sA.h, n_t1c)
            if w_lt:
                v.wait_ge(sQ.h, w_lt)
                v.wait_ge(sS.h, w_lts)
            v.tensor_tensor(dap(xl2, 0, [[4096, 128], [1024, 4], [1, 1024]]),
                            dap(tb2, 0, [[4096, 128], [1024, 4], [1, 1024]]),
                            stbc(pb, 2), ALU.mult).then_inc(sV.h, 1)
        sV.n += 1
        d_fin.append(sV.n)
        n_fin = sV.n

        for t4 in range(4):
            tt = S * 4 + t4

            @blk.sync
            def _(s, t4=t4, tt=tt, n_fin=n_fin):
                s.wait_ge(sV.h, n_fin)
                for j in range(NC8):
                    s.dma_start(xln2T[:, j * T + tt * 128:j * T + (tt + 1) * 128],
                                xl2[:, t4 * D + j * 128:t4 * D + (j + 1) * 128],
                                transpose=True).then_inc(sQ.h, 16)
            sQ.n += 16 * NC8
        d_ltq_s.append(sQ.n)
        d_lts_s.append(sS.n)
        d_tpq_s.append(sQ.n)
        d_tps_s.append(sS.n)

    n_dq_end = sQ.n
    n_dg_end = sG.n
    n_dv_end = sV.n
    n_da_end = sA.n
    n_ds_end = sS.n
    d_es.close()
    wt_es.close()
    r_es.close()

    e_es = ExitStack()
    w1s = sb(e_es, "w1s", [128, 2 * 1024], BF16)
    w2s = sb(e_es, "w2s", [128, 2 * 4096], BF16)
    hT = sb(e_es, "hT", [128, 32 * 1024], BF16)
    yst = sb(e_es, "yst", [128, 2 * 1024], BF16)
    ytok = sb(e_es, "ytok", [128, 8 * 1024], BF16, is_transpose_dest=True)
    x1f = sb(e_es, "x1f", [128, 4096], F32)
    obuf = sb(e_es, "obuf", [128, 4096], F32)

    out_st, fin_all = [], []
    ffn2_mm_prev = 0
    fin_prev_block = 0
    f1_mm, f2_mm = [], []
    for Tb in range(4):
        relu_cnt = []
        for fc in range(32):
            gi = Tb * 32 + fc
            w_slab = f1_mm[gi - 2] if gi >= 2 else 0

            @blk.gpsimd
            def _(g, Tb=Tb, fc=fc, w_slab=w_slab):
                if Tb == 0 and fc == 0:
                    g.wait_ge(sV.h, n_dv_end)
                    g.wait_ge(sA.h, n_da_end)
                    g.wait_ge(sQ.h, n_dq_end)
                    g.wait_ge(sG.h, n_dg_end)
                    g.wait_ge(sS.h, n_ds_end)
                if w_slab:
                    g.wait_ge(sM.h, w_slab)
                g.dma_start(dap(w1s, (fc % 2) * 1024, [[2048, 128], [1, 1024]]),
                            dap(s_w1t, fc * 131072, [[128, 128], [16384, 8], [1, 128]])
                            ).then_inc(sG.h, 16)
            sG.n += 16
            n_sl = sG.n
            w_relu = relu_cnt[fc - 1] if fc >= 1 else 0

            @blk.tensor
            def _(t, Tb=Tb, fc=fc, n_sl=n_sl, w_relu=w_relu, w_prev2=ffn2_mm_prev):
                t.wait_ge(sG.h, n_sl)
                if fc == 0:
                    t.wait_ge(sQ.h, n_dq_end)
                    if w_prev2:
                        t.wait_ge(sM.h, w_prev2)
                if w_relu:
                    t.wait_ge(sA.h, w_relu)
                mi = None
                for dc in range(NC8):
                    lhs = w1s[:, (fc % 2) * 1024 + dc * 128:(fc % 2) * 1024 + (dc + 1) * 128]
                    for n in range(2):
                        dst = ps_c if n == 0 else ps_d
                        mi = t.matmul(dst[:, :], lhs,
                                      xln2T[:, dc * T + Tb * 1024 + n * 512:
                                            dc * T + Tb * 1024 + (n + 1) * 512],
                                      start=(dc == 0), stop=(dc == NC8 - 1))
                mi.then_inc(sM.h, 1)
            sM.n += 1
            f1_mm.append(sM.n)
            n_f1 = sM.n

            @blk.scalar
            def _(s, fc=fc, n_f1=n_f1):
                s.wait_ge(sM.h, n_f1)
                s.activation(hT[:, fc * 1024 + 0:fc * 1024 + 512], ps_c[:, :],
                             AF.Relu, bias=b1c[:, fc:fc + 1], scale=1.0)
                s.activation(hT[:, fc * 1024 + 512:fc * 1024 + 1024], ps_d[:, :],
                             AF.Relu, bias=b1c[:, fc:fc + 1], scale=1.0
                             ).then_inc(sA.h, 1)
            sA.n += 1
            relu_cnt.append(sA.n)
        n_relu_all = sA.n

        yev, ytp = [], []
        for dc in range(NC8):
            gj = Tb * NC8 + dc
            w_slab2 = f2_mm[gj - 2] if gj >= 2 else 0

            @blk.gpsimd
            def _(g, dc=dc, w_slab2=w_slab2):
                if w_slab2:
                    g.wait_ge(sM.h, w_slab2)
                g.dma_start(dap(w2s, (dc % 2) * 4096, [[8192, 128], [1, 4096]]),
                            dap(s_w2t, dc * 524288, [[128, 128], [16384, 32], [1, 128]])
                            ).then_inc(sG.h, 16)
            sG.n += 16
            n_sl = sG.n
            w_yev = yev[dc - 1] if dc >= 1 else 0

            @blk.tensor
            def _(t, dc=dc, n_sl=n_sl, n_relu_all=n_relu_all, w_yev=w_yev):
                t.wait_ge(sG.h, n_sl)
                if dc == 0:
                    t.wait_ge(sA.h, n_relu_all)
                if w_yev:
                    t.wait_ge(sV.h, w_yev)
                mi = None
                for fc in range(32):
                    lhs = w2s[:, (dc % 2) * 4096 + fc * 128:(dc % 2) * 4096 + (fc + 1) * 128]
                    for n in range(2):
                        mi = t.matmul(ps_a[:, n * 512:(n + 1) * 512], lhs,
                                      hT[:, fc * 1024 + n * 512:fc * 1024 + (n + 1) * 512],
                                      start=(fc == 0), stop=(fc == 31))
                mi.then_inc(sM.h, 1)
            sM.n += 1
            f2_mm.append(sM.n)
            n_f2 = sM.n
            w_ytp = ytp[dc - 2] if dc >= 2 else 0

            @blk.vector
            def _(v, dc=dc, n_f2=n_f2, w_ytp=w_ytp):
                v.wait_ge(sM.h, n_f2)
                if w_ytp:
                    v.wait_ge(sQ.h, w_ytp[0])
                    v.wait_ge(sS.h, w_ytp[1])
                mi = None
                for n in range(2):
                    mi = v.tensor_scalar(yst[:, (dc % 2) * 1024 + n * 512:
                                             (dc % 2) * 1024 + (n + 1) * 512],
                                         ps_a[:, n * 512:(n + 1) * 512],
                                         b2c[:, dc:dc + 1], None, ALU.add)
                mi.then_inc(sV.h, 1)
            sV.n += 1
            yev.append(sV.n)
            n_ye = sV.n

            def ytpb(s, dc=dc, n_ye=n_ye, fin_prev_block=fin_prev_block,
                     q=(dc % 2 == 0)):
                s.wait_ge(sV.h, n_ye)
                if fin_prev_block:
                    s.wait_ge(sV.h, fin_prev_block)
                for ts_ in range(8):
                    s.dma_start(ytok[:, ts_ * 1024 + dc * 128:ts_ * 1024 + (dc + 1) * 128],
                                yst[:, (dc % 2) * 1024 + ts_ * 128:
                                    (dc % 2) * 1024 + (ts_ + 1) * 128],
                                transpose=True).then_inc((sQ if q else sS).h, 16)
            if dc % 2 == 0:
                blk.sync(ytpb)
                sQ.n += 128
            else:
                blk.scalar(ytpb)
                sS.n += 128
            ytp.append((sQ.n, sS.n))
        n_ytp_all = sQ.n
        n_ytp_all_s = sS.n

        for gq in range(2):
            gg = Tb * 2 + gq
            pb = gg % 2
            w_xlf = fin_all[gg - 1] if gg >= 1 else 0

            @blk.sync
            def _(s, Tb=Tb, gq=gq, w_xlf=w_xlf):
                if w_xlf:
                    s.wait_ge(sV.h, w_xlf)
                s.dma_start(dap(x1f, 0, [[4096, 128], [1024, 4], [1, 1024]]),
                            dap(s_x1, (Tb * 8 + gq * 4) * 131072,
                                [[1024, 128], [131072, 4], [1, 1024]])
                            ).then_inc(sQ.h, 16)
            sQ.n += 16
            n_xl = sQ.n
            w_ost = out_st[gg - 1] if gg >= 1 else 0
            w_st = fin_all[gg - 2] if gg >= 2 else 0

            @blk.scalar
            def _(s, pb=pb, gq=gq, n_ytp_all=n_ytp_all, n_ytp_all_s=n_ytp_all_s,
                  w_ost=w_ost, w_st=w_st):
                s.wait_ge(sQ.h, n_ytp_all)
                s.wait_ge(sS.h, n_ytp_all_s)
                if w_ost:
                    s.wait_ge(sQ.h, w_ost)
                if w_st:
                    s.wait_ge(sV.h, w_st)
                mi = None
                for t in range(4):
                    yt = ytok[:, gq * 4096 + t * 1024:gq * 4096 + (t + 1) * 1024]
                    ot = obuf[:, t * 1024:(t + 1) * 1024]
                    s.activation(ot, yt, AF.Copy, accum_out=stcol(pb, 0, t))
                    mi = s.activation(ot, yt, AF.Square, accum_out=stcol(pb, 1, t))
                mi.then_inc(sA.h, 1)
            sA.n += 1
            n_st8 = sA.n

            @blk.vector
            def _(v, pb=pb, n_st8=n_st8):
                v.wait_ge(sA.h, n_st8)
                v = DR(v)
                v.tensor_scalar(stc(pb, 0), stc(pb, 0), -1.0 / D, None, ALU.mult)
                v.tensor_scalar(stc(pb, 1), stc(pb, 1), 1.0 / D, None, ALU.mult)
                v.tensor_tensor(stc(pb, 2), stc(pb, 0), stc(pb, 0), ALU.mult)
                v.tensor_tensor(stc(pb, 1), stc(pb, 1), stc(pb, 2), ALU.subtract)
                v.tensor_scalar(stc(pb, 1), stc(pb, 1), EPS, None,
                                ALU.add).then_inc(sV.h, 1)
            sV.n += 1
            n_ty = sV.n

            @blk.scalar
            def _(s, pb=pb, gq=gq, n_ty=n_ty):
                s.wait_ge(sV.h, n_ty)
                rsqrt(s, stc(pb, 2), stc(pb, 3), stc(pb, 1))
                s.drain()
                mi = None
                for t in range(4):
                    mi = s.activation(obuf[:, t * 1024:(t + 1) * 1024],
                                      ytok[:, gq * 4096 + t * 1024:
                                           gq * 4096 + (t + 1) * 1024],
                                      AF.Identity, bias=stcol(pb, 0, t), scale=1.0)
                mi.then_inc(sA.h, 1)
            sA.n += 1
            n_t1 = sA.n

            @blk.vector
            def _(v, pb=pb, n_t1=n_t1, n_xl=n_xl):
                v.wait_ge(sA.h, n_t1)
                v.wait_ge(sQ.h, n_xl)
                v.tensor_tensor(dap(obuf, 0, [[4096, 128], [1024, 4], [1, 1024]]),
                                dap(obuf, 0, [[4096, 128], [1024, 4], [1, 1024]]),
                                stbc(pb, 2), ALU.mult)
                v.tensor_tensor(obuf[:, :], obuf[:, :], x1f[:, :],
                                ALU.add).then_inc(sV.h, 1)
            sV.n += 1
            fin_all.append(sV.n)
            n_ob = sV.n

            @blk.sync
            def _(s, Tb=Tb, gq=gq, n_ob=n_ob):
                s.wait_ge(sV.h, n_ob)
                s.dma_start(dap(out, (Tb * 8 + gq * 4) * 131072,
                                [[1024, 128], [131072, 4], [1, 1024]]),
                            obuf[:, :]).then_inc(sQ.h, 16)
            sQ.n += 16
            out_st.append(sQ.n)
        fin_prev_block = fin_all[-1]
        ffn2_mm_prev = sM.n

    n_final = sQ.n

    @blk.sync
    def _(s):
        s.wait_ge(sQ.h, n_final)
        s.wait_ge(sG.h, sG.n)
        s.wait_ge(sS.h, sS.n)

    e_es.close()
    p_es.close()
    g_es.close()
    return nc


_NC_CACHE = None


def kernel(L0_attn_x, L0_attn_wq, L0_attn_wk, L0_attn_wv, L0_attn_wo,
           L0_ffn_W1, L0_ffn_b1, L0_ffn_W2, L0_ffn_b2):
    global _NC_CACHE
    x = np.ascontiguousarray(np.asarray(L0_attn_x, dtype=np.float32))
    wq = np.ascontiguousarray(np.asarray(L0_attn_wq, dtype=np.float32))
    wk = np.ascontiguousarray(np.asarray(L0_attn_wk, dtype=np.float32))
    wv = np.ascontiguousarray(np.asarray(L0_attn_wv, dtype=np.float32))
    wo = np.ascontiguousarray(np.asarray(L0_attn_wo, dtype=np.float32))
    w1 = np.ascontiguousarray(np.asarray(L0_ffn_W1, dtype=np.float32))
    b1 = np.ascontiguousarray(np.asarray(L0_ffn_b1, dtype=np.float32)).reshape(32, 128)
    w2 = np.ascontiguousarray(np.asarray(L0_ffn_W2, dtype=np.float32))
    b2 = np.ascontiguousarray(np.asarray(L0_ffn_b2, dtype=np.float32)).reshape(8, 128)

    if _NC_CACHE is None:
        _NC_CACHE = build()
    nc = _NC_CACHE
    in_maps = [{"x": np.ascontiguousarray(x[i]), "wq": wq, "wk": wk, "wv": wv,
                "wo": wo, "w1": w1, "b1": b1, "w2": w2, "b2": b2}
               for i in range(8)]
    res = run_bass_kernel_spmd(nc, in_maps, core_ids=list(range(8)))
    return np.stack([r["out"] for r in res.results], axis=0)


# version 08
# speedup vs baseline: 1.2175x; 1.0417x over previous
import numpy as np
from contextlib import ExitStack

import concourse.bass as bass
import concourse.mybir as mybir
from concourse.bass_utils import run_bass_kernel_spmd

F32 = mybir.dt.float32
BF16 = mybir.dt.bfloat16
AF = mybir.ActivationFunctionType
ALU = mybir.AluOpType
AX = mybir.AxisListType

T, D, H, DH, FF = 4096, 1024, 16, 64, 4096
NT, NC8, NS = 32, 8, 8
C23 = 12582912.0
C52 = 6755399441055744.0
EPS = 1e-5


def dap(t, off, dims):
    return bass.AP(t, off, [list(d) for d in dims])


class DR:
    _OPS = {"tensor_scalar", "tensor_tensor", "tensor_copy", "reduce_sum",
            "reduce_max", "tensor_reduce", "reciprocal", "memset",
            "scalar_tensor_tensor", "tensor_single_scalar", "activation"}

    def __init__(self, e):
        self._e = e

    def __getattr__(self, name):
        f = getattr(self._e, name)
        if name in DR._OPS:
            def g(*a, **k):
                r = f(*a, **k)
                self._e.drain()
                return r
            return g
        return f


def build():
    nc = bass.Bass("TRN2", target_bir_lowering=False,
                   detect_race_conditions=False)

    x_in = nc.dram_tensor("x", [T, D], F32, kind="ExternalInput")
    w_q = nc.dram_tensor("wq", [D, D], F32, kind="ExternalInput")
    w_k = nc.dram_tensor("wk", [D, D], F32, kind="ExternalInput")
    w_v = nc.dram_tensor("wv", [D, D], F32, kind="ExternalInput")
    w_o = nc.dram_tensor("wo", [D, D], F32, kind="ExternalInput")
    w_1 = nc.dram_tensor("w1", [FF, D], F32, kind="ExternalInput")
    b_1 = nc.dram_tensor("b1", [32, 128], F32, kind="ExternalInput")
    w_2 = nc.dram_tensor("w2", [D, FF], F32, kind="ExternalInput")
    b_2 = nc.dram_tensor("b2", [8, 128], F32, kind="ExternalInput")
    out = nc.dram_tensor("out", [T, D], F32, kind="ExternalOutput")

    s_ctx = nc.dram_tensor("s_ctx", [T, D], BF16)
    s_x1 = nc.dram_tensor("s_x1", [T, D], F32)
    s_w1t = nc.dram_tensor("s_w1t", [32, 8, 128, 128], BF16)
    s_w2t = nc.dram_tensor("s_w2t", [8, 32, 128, 128], BF16)

    g_es = ExitStack()

    def sb(stk, name, shape, dtype, **kw):
        return stk.enter_context(nc.sbuf_tensor(name, shape, dtype, **kw))

    wsel = sb(g_es, "wsel", [128, 1], F32)
    ones = sb(g_es, "ones", [128, 128], F32)
    kq = sb(g_es, "kq", [128, 64], F32)
    gam = sb(g_es, "gam", [128, 8], F32)
    b1c = sb(g_es, "b1c", [128, 32], F32)
    b2c = sb(g_es, "b2c", [128, 16], F32)
    gat = sb(g_es, "gat", [128, 128], BF16)
    gatT = sb(g_es, "gatT", [128, 128], BF16, is_transpose_dest=True)
    st = sb(g_es, "st", [128, 64], F32)

    def stc(p, i):
        return st[:, 16 * p + 4 * i:16 * p + 4 * (i + 1)]

    def stcol(p, i, t):
        c = 16 * p + 4 * i + t
        return st[:, c:c + 1]

    def stbc(p, i):
        return dap(st, 16 * p + 4 * i, [[64, 128], [1, 4], [0, 1024]])

    r_es = ExitStack()
    tb1 = sb(r_es, "tb1", [128, 4096], F32)
    tb2 = sb(r_es, "tb2", [128, 4096], F32)

    wt_es = ExitStack()
    wt_q = sb(wt_es, "wt_q", [128, NC8 * D], BF16, is_transpose_dest=True)

    w_es = ExitStack()
    wbig = sb(w_es, "wbig", [128, 4096], F32)
    qbuf = sb(w_es, "qbuf", [128, 4096], BF16)
    wtst = sb(w_es, "wtst", [128, 4096], BF16, is_transpose_dest=True)
    bk_es = ExitStack()
    wt_k = sb(bk_es, "wt_k", [128, NC8 * D], BF16, is_transpose_dest=True)
    wt_v = sb(bk_es, "wt_v", [128, NC8 * D], BF16, is_transpose_dest=True)

    p_es = ExitStack()
    ps_g = p_es.enter_context(nc.psum_tensor("ps_g", [128, 512], F32))
    ps_a = p_es.enter_context(nc.psum_tensor("ps_a", [128, 1024], F32))
    ps_b = p_es.enter_context(nc.psum_tensor("ps_b", [128, 1024], F32))
    ps_c = p_es.enter_context(nc.psum_tensor("ps_c", [128, 512], F32))
    ps_d = p_es.enter_context(nc.psum_tensor("ps_d", [128, 512], F32))

    class Ctr:
        def __init__(self, name):
            self.h = g_es.enter_context(nc.semaphore(name))
            self.n = 0
    sQ, sG, sV, sA, sM, sS, sP = (Ctr("sQ"), Ctr("sG"), Ctr("sV"), Ctr("sA"),
                                  Ctr("sM"), Ctr("sS"), Ctr("sP"))

    blk = g_es.enter_context(nc.Block())

    def pad(v):
        v.tensor_scalar(st[:, 56:60], st[:, 60:64], 1.0, None, ALU.mult)

    def rnd_inplace(v):
        v.tensor_scalar(tb1[:, :], tb2[:, :], C23, C23, ALU.add, ALU.subtract)
        v.tensor_scalar(tb2[:, :], tb2[:, :], C52, C52, ALU.add, ALU.subtract)
        v.tensor_tensor(tb2[:, :], tb2[:, :], tb1[:, :], ALU.subtract)
        return v.scalar_tensor_tensor(tb1[:, :], tb2[:, :], wsel[:, 0:1],
                                      tb1[:, :], ALU.mult, ALU.add)

    def rsqrt(s, dst, tmp, src):
        s.activation(tmp, src, AF.Ln)
        s.drain()
        return s.activation(dst, tmp, AF.Exp, bias=0.0, scale=-0.5)

    @blk.scalar
    def _(s):
        s.add_instruction(mybir.InstLoadActFuncSet(
            name=nc.get_next_instruction_name(), act_func_set_id=6, ins=[], outs=[]))

    @blk.vector
    def _(v):
        v.memset(wsel[:, :], 2.0 ** -30)
        v.drain()
        v.tensor_scalar(wsel[:, :], wsel[:, :], 1.0, 1.0, ALU.add, ALU.subtract)
        v.drain()
        v.tensor_scalar(wsel[:, :], wsel[:, :], 1e30, 1.0, ALU.mult, ALU.min)
        v.drain()
        v.memset(ones[:, :], 1.0)
        v.memset(st[:, :], 0.0)
        v.memset(gat[:, :], 0.0)
        v.memset(wbig[0:32, 128:256], 0.0).then_inc(sV.h, 1)
    sV.n += 1
    n_ms0 = sV.n

    @blk.gpsimd
    def _(g):
        g.wait_ge(sV.h, n_ms0)
        g.dma_start(dap(wbig, 0, [[4096, 32], [1, 128]]),
                    dap(b_1, 0, [[128, 32], [1, 128]])).then_inc(sG.h, 16)
        g.dma_start(dap(wbig, 128, [[4096, 8], [1, 128]]),
                    dap(b_2, 0, [[128, 8], [1, 128]])).then_inc(sG.h, 16)
    sG.n += 32
    n_bld = sG.n

    @blk.vector
    def _(v):
        v.wait_ge(sG.h, n_bld)
        v.tensor_copy(gat[0:32, 0:128], wbig[0:32, 0:128])
        v.tensor_tensor(qbuf[0:32, 0:128], wbig[0:32, 0:128],
                        gat[0:32, 0:128], ALU.subtract)
        v.tensor_copy(qbuf[0:16, 128:256], wbig[0:16, 128:256])
        v.tensor_tensor(qbuf[0:16, 256:384], wbig[0:16, 128:256],
                        qbuf[0:16, 128:256], ALU.subtract).then_inc(sV.h, 1)
    sV.n += 1
    n_bv = sV.n

    @blk.sync
    def _(s):
        s.wait_ge(sV.h, n_bv)
        s.dma_start(gatT[:, 0:32], gat[0:32, 0:128], transpose=True).then_inc(sQ.h, 16)
        s.dma_start(wtst[:, 0:32], qbuf[0:32, 0:128], transpose=True).then_inc(sQ.h, 16)
        s.dma_start(gatT[:, 32:48], qbuf[0:16, 128:256], transpose=True).then_inc(sQ.h, 16)
        s.dma_start(wtst[:, 32:48], qbuf[0:16, 256:384], transpose=True).then_inc(sQ.h, 16)
    sQ.n += 64
    n_bt = sQ.n

    @blk.vector
    def _(v):
        v.wait_ge(sQ.h, n_bt)
        v.tensor_tensor(b1c[:, :], gatT[:, 0:32], wtst[:, 0:32], ALU.add)
        v.tensor_tensor(b2c[:, :], gatT[:, 32:48], wtst[:, 32:48],
                        ALU.add).then_inc(sV.h, 1)
    sV.n += 1

    def prep_attn(w_dram, dst_wt, gcol, tguard=0):
        for half in range(2):
            red = sV.n

            @blk.gpsimd
            def _(g, half=half, red=red):
                g.wait_ge(sV.h, red)
                g.dma_start(dap(wbig, 0, [[4096, 128], [1024, 4], [1, 1024]]),
                            dap(w_dram, half * 524288,
                                [[1024, 128], [131072, 4], [1, 1024]])).then_inc(sG.h, 16)
            sG.n += 16
            ld = sG.n

            @blk.vector
            def _(v, half=half, ld=ld):
                v.wait_ge(sG.h, ld)
                v.reduce_sum(kq[:, 40 + half:41 + half], wbig[:, :], AX.X,
                             apply_absolute_value=True).then_inc(sV.h, 1)
            sV.n += 1

        @blk.vector
        def _(v):
            v.drain()
            v.tensor_tensor(kq[:, 42:43], kq[:, 40:41], kq[:, 41:42],
                            ALU.add).then_inc(sV.h, 1)
        sV.n += 1
        pd = sV.n

        @blk.tensor
        def _(t, pd=pd):
            t.wait_ge(sV.h, pd)
            t.matmul(ps_g[:, 0:1], ones[:, :], kq[:, 42:43],
                     start=True, stop=True).then_inc(sM.h, 1)
        sM.n += 1
        md = sM.n

        @blk.vector
        def _(v, md=md):
            v.wait_ge(sM.h, md)
            v.tensor_scalar(kq[:, 43:44], ps_g[:, 0:1], 1.0 / (D * D), EPS,
                            ALU.mult, ALU.max)
            pad(v)
            v.tensor_copy(gam[:, gcol:gcol + 1], kq[:, 43:44])
            v.reciprocal(kq[:, 44:45], kq[:, 43:44]).then_inc(sV.h, 1)
        sV.n += 1

        tp_call0 = sQ.n
        for half in range(2):
            qd0 = sV.n
            tp0 = sQ.n
            actg = sA.n

            @blk.gpsimd
            def _(g, half=half, qd0=qd0):
                g.wait_ge(sV.h, qd0)
                g.dma_start(dap(wbig, 0, [[4096, 128], [1024, 4], [1, 1024]]),
                            dap(w_dram, half * 524288,
                                [[1024, 128], [131072, 4], [1, 1024]])).then_inc(sG.h, 16)
            sG.n += 16
            ld = sG.n

            @blk.vector
            def _(v, half=half, ld=ld, tp0=tp0, actg=actg, tp_call0=tp_call0):
                v.wait_ge(sG.h, ld)
                if actg:
                    v.wait_ge(sA.h, actg)
                v.wait_ge(sQ.h, tp0 if half == 1 else tp_call0)
                v.tensor_scalar(tb2[:, :], wbig[:, :], kq[:, 44:45], None, ALU.mult)
                v.tensor_scalar(tb1[:, :], tb2[:, :], C23, C23, ALU.add, ALU.subtract)
                v.tensor_scalar(wbig[:, :], tb2[:, :], C52, C52, ALU.add, ALU.subtract)
                v.tensor_tensor(wbig[:, :], wbig[:, :], tb1[:, :], ALU.subtract)
                v.scalar_tensor_tensor(tb1[:, :], wbig[:, :], wsel[:, 0:1],
                                       tb1[:, :], ALU.mult, ALU.add)
                v.tensor_scalar(qbuf[:, :], tb1[:, :], -1.0, 1.0,
                                ALU.max, ALU.min).then_inc(sV.h, 1)
            sV.n += 1
            qd = sV.n

            @blk.sync
            def _(s, half=half, qd=qd, tguard=tguard):
                s.wait_ge(sV.h, qd)
                if tguard and half == 0:
                    s.wait_ge(sM.h, tguard)
                for cl in range(4):
                    c = half * 4 + cl
                    for j in range(NC8):
                        s.dma_start(dst_wt[:, j * D + c * 128:j * D + (c + 1) * 128],
                                    qbuf[:, cl * D + j * 128:cl * D + (j + 1) * 128],
                                    transpose=True).then_inc(sQ.h, 16)
            sQ.n += 16 * 32
        return sQ.n

    def prep_ffn(w_dram, spill, layout):
        for grp in range(8):
            red = sV.n

            @blk.gpsimd
            def _(g, grp=grp, red=red):
                g.wait_ge(sV.h, red)
                if layout == "w1":
                    g.dma_start(dap(wbig, 0, [[4096, 128], [1024, 4], [1, 1024]]),
                                dap(w_dram, grp * 524288,
                                    [[1024, 128], [131072, 4], [1, 1024]])
                                ).then_inc(sG.h, 16)
                else:
                    g.dma_start(dap(wbig, 0, [[4096, 128], [1, 4096]]),
                                dap(w_dram, grp * 524288, [[4096, 128], [1, 4096]])
                                ).then_inc(sG.h, 16)
            sG.n += 16
            ld = sG.n

            @blk.vector
            def _(v, grp=grp, ld=ld):
                v.wait_ge(sG.h, ld)
                v.reduce_sum(kq[:, 45:46], wbig[:, :], AX.X, apply_absolute_value=True)
                pad(v)
                if grp == 0:
                    v.tensor_copy(kq[:, 46:47], kq[:, 45:46]).then_inc(sV.h, 1)
                else:
                    v.tensor_tensor(kq[:, 46:47], kq[:, 46:47], kq[:, 45:46],
                                    ALU.add).then_inc(sV.h, 1)
            sV.n += 1
            yield
        pd = sV.n

        @blk.tensor
        def _(t, pd=pd):
            t.wait_ge(sV.h, pd)
            t.matmul(ps_g[:, 0:1], ones[:, :], kq[:, 46:47],
                     start=True, stop=True).then_inc(sM.h, 1)
        sM.n += 1
        md = sM.n

        @blk.vector
        def _(v, md=md):
            v.wait_ge(sM.h, md)
            v.tensor_scalar(kq[:, 43:44], ps_g[:, 0:1], 1.0 / (FF * D), EPS,
                            ALU.mult, ALU.add)
            pad(v)
            v.reciprocal(kq[:, 44:45], kq[:, 43:44]).then_inc(sV.h, 1)
        sV.n += 1
        yield

        spl_prev = 0
        tpg_prev = sQ.n
        for grp in range(8):
            qd0 = sV.n
            actg = sA.n

            @blk.gpsimd
            def _(g, grp=grp, qd0=qd0):
                g.wait_ge(sV.h, qd0)
                if layout == "w1":
                    g.dma_start(dap(wbig, 0, [[4096, 128], [1024, 4], [1, 1024]]),
                                dap(w_dram, grp * 524288,
                                    [[1024, 128], [131072, 4], [1, 1024]])
                                ).then_inc(sG.h, 16)
                else:
                    g.dma_start(dap(wbig, 0, [[4096, 128], [1, 4096]]),
                                dap(w_dram, grp * 524288, [[4096, 128], [1, 4096]])
                                ).then_inc(sG.h, 16)
            sG.n += 16
            ld = sG.n

            @blk.vector
            def _(v, grp=grp, ld=ld, actg=actg, tpg_prev=tpg_prev):
                v.wait_ge(sG.h, ld)
                if actg:
                    v.wait_ge(sA.h, actg)
                v.wait_ge(sQ.h, tpg_prev)
                v.tensor_scalar(tb2[:, :], wbig[:, :], kq[:, 44:45], None, ALU.mult)
                v.tensor_scalar(tb1[:, :], tb2[:, :], C23, C23, ALU.add, ALU.subtract)
                v.tensor_scalar(wbig[:, :], tb2[:, :], C52, C52, ALU.add, ALU.subtract)
                v.tensor_tensor(wbig[:, :], wbig[:, :], tb1[:, :], ALU.subtract)
                v.scalar_tensor_tensor(tb1[:, :], wbig[:, :], wsel[:, 0:1],
                                       tb1[:, :], ALU.mult, ALU.add)
                v.tensor_scalar(qbuf[:, :], tb1[:, :], -1.0, 1.0,
                                ALU.max, ALU.min).then_inc(sV.h, 1)
            sV.n += 1
            qd = sV.n

            @blk.sync
            def _(s, grp=grp, qd=qd, spl_prev=spl_prev):
                s.wait_ge(sV.h, qd)
                if spl_prev:
                    s.wait_ge(sQ.h, spl_prev)
                for bi in range(32):
                    s.dma_start(wtst[:, bi * 128:(bi + 1) * 128],
                                qbuf[:, bi * 128:(bi + 1) * 128],
                                transpose=True).then_inc(sQ.h, 16)
            sQ.n += 16 * 32
            tp = sQ.n
            tpg_prev = tp

            @blk.sync
            def _(s, grp=grp, tp=tp):
                s.wait_ge(sQ.h, tp)
                if layout == "w1":
                    s.dma_start(dap(spill, grp * 4 * 131072,
                                    [[128, 128], [131072, 4], [16384, 8], [1, 128]]),
                                dap(wtst, 0, [[4096, 128], [1024, 4], [128, 8], [1, 128]])
                                ).then_inc(sQ.h, 16)
                else:
                    s.dma_start(dap(spill, grp * 524288,
                                    [[128, 128], [16384, 32], [1, 128]]),
                                dap(wtst, 0, [[4096, 128], [128, 32], [1, 128]])
                                ).then_inc(sQ.h, 16)
            sQ.n += 16
            spl_prev = sQ.n
            yield

    prep_attn(w_v, wt_v, 2)
    prep_attn(w_q, wt_q, 0)
    n_wprep = prep_attn(w_k, wt_k, 1)

    a_es = ExitStack()
    xa = sb(a_es, "xa", [128, 4096], F32)
    sqr = sb(a_es, "sqr", [128, 1024], F32)
    xqs = sb(a_es, "xqs", [128, 2 * 4096], BF16)
    xqTt = sb(a_es, "xqTt", [128, 2 * D], BF16, is_transpose_dest=True)
    q_sb = sb(a_es, "q_sb", [128, D], BF16)
    k_sb = sb(a_es, "k_sb", [128, D], BF16)
    v_sb = sb(a_es, "v_sb", [128, D], BF16)
    pr_g = sb(a_es, "pr_g", [128, 8192], BF16)
    pr_v = sb(a_es, "pr_v", [128, 8192], BF16)
    sc_sb = sb(a_es, "sc_sb", [128, 256], F32)
    mx_sb = sb(a_es, "mx_sb", [128, 16], F32)
    e_sb = sb(a_es, "e_sb", [128, 256], F32)
    sm_sb = sb(a_es, "sm_sb", [128, 16], F32)
    at_sb = sb(a_es, "at_sb", [128, 256], BF16)
    ctx_f = sb(a_es, "ctx_f", [128, D], F32)
    ctx_sb = sb(a_es, "ctx_sb", [128, 2 * D], BF16)

    pbanks = [(ps_a, 0), (ps_a, 512), (ps_b, 0), (ps_b, 512), (ps_c, 0), (ps_d, 0)]
    a_t1a, a_rnd, a_xq, s_tpq, a_xld = [], [], [], [], []
    ev_c, ctx_v, ctx_sp, mm_c = [], [], [], []

    prep_gens = [prep_ffn(w_1, s_w1t, "w1"), prep_ffn(w_2, s_w2t, "w2")]

    def pull_prep(k):
        while k > 0 and prep_gens:
            try:
                next(prep_gens[0])
                k -= 1
            except StopIteration:
                prep_gens.pop(0)

    def emit_xload(S):
        w_t1a = a_t1a[S - 1] if S >= 1 else 0

        @blk.gpsimd
        def _(g, S=S, w_t1a=w_t1a):
            if w_t1a:
                g.wait_ge(sA.h, w_t1a)
            g.dma_start(dap(xa, 0, [[4096, 128], [1024, 4], [1, 1024]]),
                        dap(x_in, S * 4 * 131072,
                            [[1024, 128], [131072, 4], [1, 1024]])).then_inc(sG.h, 16)
        sG.n += 16
        a_xld.append(sG.n)

    emit_xload(0)
    for S in range(NS):
        pb = S % 2
        n_xld = a_xld[S]
        w_rnd2 = a_rnd[S - 2] if S >= 2 else 0

        @blk.scalar
        def _(s, pb=pb, n_xld=n_xld, w_rnd2=w_rnd2):
            s.wait_ge(sG.h, n_xld)
            if w_rnd2:
                s.wait_ge(sV.h, w_rnd2)
            mi = None
            for t in range(4):
                xt = xa[:, t * 1024:(t + 1) * 1024]
                s.activation(sqr[:, :], xt, AF.Copy, accum_out=stcol(pb, 0, t))
                mi = s.activation(sqr[:, :], xt, AF.Square,
                                  accum_out=stcol(pb, 1, t))
            mi.then_inc(sA.h, 1)
        sA.n += 1
        n_st8 = sA.n

        @blk.vector
        def _(v, pb=pb, n_st8=n_st8):
            v.wait_ge(sA.h, n_st8)
            v.tensor_scalar(stc(pb, 0), stc(pb, 0), -1.0 / D, None, ALU.mult)
            v.tensor_scalar(stc(pb, 1), stc(pb, 1), 1.0 / D, None, ALU.mult)
            v.tensor_tensor(stc(pb, 2), stc(pb, 0), stc(pb, 0), ALU.mult)
            pad(v)
            v.tensor_tensor(stc(pb, 1), stc(pb, 1), stc(pb, 2), ALU.subtract)
            pad(v)
            v.tensor_scalar(stc(pb, 1), stc(pb, 1), 1.0 + 1e-6, 1e-11,
                            ALU.mult, ALU.add).then_inc(sV.h, 1)
        sV.n += 1
        n_ty1 = sV.n

        @blk.scalar
        def _(s, pb=pb, n_ty1=n_ty1):
            s.wait_ge(sV.h, n_ty1)
            rsqrt(s, stc(pb, 2), stc(pb, 3), stc(pb, 1))
            s.drain()
            mi = None
            for t in range(4):
                mi = s.activation(tb2[:, t * 1024:(t + 1) * 1024],
                                  xa[:, t * 1024:(t + 1) * 1024], AF.Identity,
                                  bias=stcol(pb, 0, t), scale=1.0)
            mi.then_inc(sA.h, 1)
        sA.n += 1
        a_t1a.append(sA.n)
        n_t1a = sA.n

        @blk.vector
        def _(v, pb=pb, n_t1a=n_t1a):
            v.wait_ge(sA.h, n_t1a)
            v.tensor_reduce(stc(pb, 1),
                            dap(tb2, 0, [[4096, 128], [1024, 4], [1, 1024]]),
                            AX.X, ALU.max, apply_absolute_value=True)
            pad(v)
            v.tensor_tensor(stc(pb, 1), stc(pb, 1), stc(pb, 2), ALU.mult)
            pad(v)
            v.tensor_scalar(stc(pb, 1), stc(pb, 1), EPS, None, ALU.max)
            pad(v)
            v.reciprocal(stc(pb, 3), stc(pb, 1))
            pad(v)
            v.tensor_scalar(stc(pb, 3), stc(pb, 3), 127.0, None, ALU.mult)
            v.tensor_scalar(stc(pb, 1), stc(pb, 1), 1.0 / 127.0, None, ALU.mult)
            v.tensor_tensor(stc(pb, 2), stc(pb, 2), stc(pb, 3), ALU.mult)
            pad(v)
            v.tensor_tensor(dap(tb2, 0, [[4096, 128], [1024, 4], [1, 1024]]),
                            dap(tb2, 0, [[4096, 128], [1024, 4], [1, 1024]]),
                            stbc(pb, 2), ALU.mult)
            rnd_inplace(v).then_inc(sV.h, 1)
        sV.n += 1
        a_rnd.append(sV.n)
        n_rnd = sV.n
        w_tpq = s_tpq[S - 2] if S >= 2 else 0

        @blk.scalar
        def _(s, pb=pb, n_rnd=n_rnd, w_tpq=w_tpq):
            s.wait_ge(sV.h, n_rnd)
            if w_tpq:
                s.wait_ge(sQ.h, w_tpq)
            mi = None
            for t in range(4):
                mi = s.activation(xqs[:, pb * 4096 + t * 1024:pb * 4096 + (t + 1) * 1024],
                                  tb1[:, t * 1024:(t + 1) * 1024], AF.Copy,
                                  bias=0.0, scale=stcol(pb, 1, t))
            mi.then_inc(sA.h, 1)
        sA.n += 1
        a_xq.append(sA.n)
        n_xq = sA.n

        for t4 in range(4):
            tt = S * 4 + t4
            tpb = tt % 2
            w_mm = mm_c[tt - 2] if tt >= 2 else 0

            @blk.sync
            def _(s, t4=t4, pb=pb, tpb=tpb, n_xq=n_xq, w_mm=w_mm):
                s.wait_ge(sA.h, n_xq)
                if w_mm:
                    s.wait_ge(sM.h, w_mm)
                for j in range(NC8):
                    s.dma_start(xqTt[:, tpb * D + j * 128:tpb * D + (j + 1) * 128],
                                xqs[:, pb * 4096 + t4 * D + j * 128:
                                    pb * 4096 + t4 * D + (j + 1) * 128],
                                transpose=True).then_inc(sQ.h, 16)
            sQ.n += 16 * NC8
            n_tp = sQ.n
            w_ev = ev_c[tt - 1] if tt >= 1 else 0

            @blk.tensor
            def _(t, tt=tt, tpb=tpb, n_tp=n_tp, w_ev=w_ev):
                t.wait_ge(sQ.h, n_tp)
                if tt == 0:
                    t.wait_ge(sQ.h, n_wprep)
                if w_ev:
                    t.wait_ge(sA.h, w_ev)
                mi = None
                for j in range(NC8):
                    lhs = xqTt[:, tpb * D + j * 128:tpb * D + (j + 1) * 128]
                    for pi, wt in enumerate((wt_q, wt_k, wt_v)):
                        for n in range(2):
                            pb_, off = pbanks[pi * 2 + n]
                            mi = t.matmul(pb_[:, off:off + 512], lhs,
                                          wt[:, j * D + n * 512:j * D + (n + 1) * 512],
                                          start=(j == 0), stop=(j == NC8 - 1))
                mi.then_inc(sM.h, 1)
            sM.n += 1
            mm_c.append(sM.n)
            n_mm = sM.n
            w_mix = ctx_v[tt - 1] if tt >= 1 else 0

            @blk.scalar
            def _(s, n_mm=n_mm, w_mix=w_mix):
                s.wait_ge(sM.h, n_mm)
                if w_mix:
                    s.wait_ge(sV.h, w_mix)
                mi = None
                for pi, dst in enumerate((q_sb, k_sb, v_sb)):
                    for n in range(2):
                        pb_, off = pbanks[pi * 2 + n]
                        mi = s.activation(dst[:, n * 512:(n + 1) * 512],
                                          pb_[:, off:off + 512],
                                          AF.Copy, bias=0.0, scale=gam[:, pi:pi + 1])
                mi.then_inc(sA.h, 1)
            sA.n += 1
            ev_c.append(sA.n)
            n_evc = sA.n

            @blk.gpsimd
            def _(g, n_evc=n_evc):
                g.wait_ge(sA.h, n_evc)
                g.tensor_tensor(
                    dap(pr_g, 0, [[8192, 128], [1024, 8], [64, 16], [1, 64]]),
                    dap(q_sb, 0, [[D, 128], [64, 8], [0, 16], [1, 64]]),
                    dap(k_sb, 0, [[D, 128], [0, 8], [64, 16], [1, 64]]),
                    ALU.mult).then_inc(sP.h, 1)
            sP.n += 1
            g_sc = sP.n

            @blk.vector
            def _(v, n_evc=n_evc, g_sc=g_sc):
                v.wait_ge(sA.h, n_evc)
                v.tensor_tensor(
                    dap(pr_v, 0, [[8192, 128], [1024, 8], [64, 16], [1, 64]]),
                    dap(q_sb, 512, [[D, 128], [64, 8], [0, 16], [1, 64]]),
                    dap(k_sb, 0, [[D, 128], [0, 8], [64, 16], [1, 64]]),
                    ALU.mult)
                v.tensor_reduce(dap(sc_sb, 128, [[256, 128], [16, 8], [1, 16]]),
                                dap(pr_v, 0,
                                    [[8192, 128], [1024, 8], [64, 16], [1, 64]]),
                                AX.X, ALU.add)
                v.wait_ge(sP.h, g_sc)
                v.tensor_reduce(dap(sc_sb, 0, [[256, 128], [16, 8], [1, 16]]),
                                dap(pr_g, 0,
                                    [[8192, 128], [1024, 8], [64, 16], [1, 64]]),
                                AX.X, ALU.add)
                pad(v)
                v.tensor_reduce(dap(mx_sb, 0, [[16, 128], [1, 16]]),
                                dap(sc_sb, 0, [[256, 128], [16, 16], [1, 16]]),
                                AX.X, ALU.max)
                pad(v)
                v.tensor_tensor(sc_sb[:, :], sc_sb[:, :],
                                dap(mx_sb, 0, [[16, 128], [1, 16], [0, 16]]),
                                ALU.subtract).then_inc(sV.h, 1)
            sV.n += 1
            n_scd = sV.n

            @blk.scalar
            def _(s, n_scd=n_scd):
                s.wait_ge(sV.h, n_scd)
                s.activation(e_sb[:, :], sc_sb[:, :], AF.Exp, bias=0.0,
                             scale=0.125).then_inc(sA.h, 1)
            sA.n += 1
            n_exp = sA.n

            @blk.vector
            def _(v, n_exp=n_exp):
                v.wait_ge(sA.h, n_exp)
                v.tensor_reduce(dap(sm_sb, 0, [[16, 128], [1, 16]]),
                                dap(e_sb, 0, [[256, 128], [16, 16], [1, 16]]),
                                AX.X, ALU.add)
                pad(v)
                v.reciprocal(sm_sb[:, :], sm_sb[:, :])
                pad(v)
                v.tensor_tensor(at_sb[:, :], e_sb[:, :],
                                dap(sm_sb, 0, [[16, 128], [1, 16], [0, 16]]),
                                ALU.mult).then_inc(sV.h, 1)
            sV.n += 1
            n_at = sV.n

            @blk.gpsimd
            def _(g, n_at=n_at):
                g.wait_ge(sV.h, n_at)
                g.tensor_tensor(
                    dap(pr_g, 0, [[8192, 128], [1024, 8], [16, 64], [1, 16]]),
                    dap(at_sb, 0, [[256, 128], [16, 8], [0, 64], [1, 16]]),
                    dap(v_sb, 0, [[D, 128], [0, 8], [1, 64], [64, 16]]),
                    ALU.mult).then_inc(sP.h, 1)
            sP.n += 1
            g_ctx = sP.n

            @blk.vector
            def _(v, g_ctx=g_ctx):
                pad(v)
                v.tensor_tensor(
                    dap(pr_v, 0, [[8192, 128], [1024, 8], [16, 64], [1, 16]]),
                    dap(at_sb, 128, [[256, 128], [16, 8], [0, 64], [1, 16]]),
                    dap(v_sb, 0, [[D, 128], [0, 8], [1, 64], [64, 16]]),
                    ALU.mult)
                v.tensor_reduce(dap(ctx_f, 512, [[D, 128], [64, 8], [1, 64]]),
                                dap(pr_v, 0,
                                    [[8192, 128], [1024, 8], [16, 64], [1, 16]]),
                                AX.X, ALU.add)
                v.wait_ge(sP.h, g_ctx)
                v.tensor_reduce(dap(ctx_f, 0, [[D, 128], [64, 8], [1, 64]]),
                                dap(pr_g, 0,
                                    [[8192, 128], [1024, 8], [16, 64], [1, 16]]),
                                AX.X, ALU.add).then_inc(sV.h, 1)
            sV.n += 1
            ctx_v.append(sV.n)
            n_cv = sV.n
            w_ctx = ctx_sp[tt - 2] if tt >= 2 else 0

            @blk.scalar
            def _(s, tpb=tpb, n_cv=n_cv, w_ctx=w_ctx):
                s.wait_ge(sV.h, n_cv)
                if w_ctx:
                    s.wait_ge(sG.h, w_ctx)
                s.activation(ctx_sb[:, tpb * D:(tpb + 1) * D], ctx_f[:, :],
                             AF.Copy).then_inc(sA.h, 1)
            sA.n += 1
            n_cc = sA.n

            @blk.gpsimd
            def _(g, tt=tt, tpb=tpb, n_cc=n_cc):
                g.wait_ge(sA.h, n_cc)
                g.dma_start(dap(s_ctx, tt * 128 * 64,
                                [[64, 128], [262144, 16], [1, 64]]),
                            dap(ctx_sb, tpb * D,
                                [[2 * D, 128], [64, 16], [1, 64]])).then_inc(sG.h, 16)
            sG.n += 16
            ctx_sp.append(sG.n)
        s_tpq.append(sQ.n)
        if S + 1 < NS:
            emit_xload(S + 1)
        pull_prep(6)

    while prep_gens:
        pull_prep(8)

    n_abc_mm = sM.n
    n_ctx_all = ctx_sp[-1]
    prep_attn(w_o, wt_q, 3, tguard=n_abc_mm)
    n_prep_q = sQ.n
    n_prep_g = sG.n
    a_es.close()
    bk_es.close()
    w_es.close()

    xln2T = sb(g_es, "xln2T", [128, NC8 * T], BF16, is_transpose_dest=True,
               side="right")
    d_es = ExitStack()
    ctd = sb(d_es, "ctd", [128, 2 * 4096], BF16)
    xd = sb(d_es, "xd", [128, 4096], F32)
    sqd = sb(d_es, "sqd", [128, 1024], F32)
    cqs = sb(d_es, "cqs", [128, 4096], BF16)
    cqT = sb(d_es, "cqT", [128, 2 * D], BF16, is_transpose_dest=True)
    aod = sb(d_es, "aod", [128, 4096], F32)
    x1d = sb(d_es, "x1d", [128, 4096], F32)
    xl2 = sb(d_es, "xl2", [128, 4096], BF16)

    d_rnd, d_x1, d_fin, d_xsp, d_ld = [], [], [], [], []
    d_tpq_s, d_tps_s, d_ltq_s, d_lts_s = [], [], [], []
    d_mm, d_aev = [], []

    def emit_dload(S):
        pb = S % 2
        ofs = pb * 4096
        w_ctd = d_rnd[S - 2] if S >= 2 else 0
        w_xd = d_x1[S - 1] if S >= 1 else 0

        @blk.gpsimd
        def _(g, S=S, ofs=ofs, w_ctd=w_ctd, w_xd=w_xd):
            if S == 0:
                g.wait_ge(sQ.h, n_ctx_all) if False else None
                g.wait_ge(sG.h, n_ctx_all)
                g.wait_ge(sQ.h, n_prep_q)
                g.wait_ge(sG.h, n_prep_g)
            if w_ctd:
                g.wait_ge(sV.h, w_ctd)
            g.dma_start(dap(ctd, ofs, [[2 * 4096, 128], [1024, 4], [1, 1024]]),
                        dap(s_ctx, S * 4 * 131072,
                            [[1024, 128], [131072, 4], [1, 1024]])).then_inc(sG.h, 16)
            if w_xd:
                g.wait_ge(sV.h, w_xd)
            g.dma_start(dap(xd, 0, [[4096, 128], [1024, 4], [1, 1024]]),
                        dap(x_in, S * 4 * 131072,
                            [[1024, 128], [131072, 4], [1, 1024]])).then_inc(sG.h, 16)
        sG.n += 32
        d_ld.append(sG.n)

    emit_dload(0)
    for S in range(NS):
        pb = S % 2
        ofs = pb * 4096
        n_ld = d_ld[S]
        w_st = d_fin[S - 2] if S >= 2 else 0

        @blk.scalar
        def _(s, pb=pb, ofs=ofs, n_ld=n_ld, w_st=w_st):
            s.wait_ge(sG.h, n_ld)
            if w_st:
                s.wait_ge(sV.h, w_st)
            mi = None
            for t in range(4):
                mi = s.activation(sqd[:, :], ctd[:, ofs + t * 1024:ofs + (t + 1) * 1024],
                                  AF.Square, accum_out=stcol(pb, 0, t))
            mi.then_inc(sA.h, 1)
        sA.n += 1
        n_st4 = sA.n

        @blk.vector
        def _(v, pb=pb, n_st4=n_st4):
            v.wait_ge(sA.h, n_st4)
            v.tensor_scalar(stc(pb, 0), stc(pb, 0), 1.0 / D, 1e-6,
                            ALU.mult, ALU.add).then_inc(sV.h, 1)
        sV.n += 1
        n_ty1 = sV.n

        @blk.scalar
        def _(s, pb=pb, n_ty1=n_ty1):
            s.wait_ge(sV.h, n_ty1)
            rsqrt(s, stc(pb, 2), stc(pb, 3), stc(pb, 0)).then_inc(sA.h, 1)
        sA.n += 1
        n_rr = sA.n

        @blk.vector
        def _(v, pb=pb, ofs=ofs, n_rr=n_rr):
            v.wait_ge(sA.h, n_rr)
            v.tensor_reduce(stc(pb, 1),
                            dap(ctd, ofs, [[2 * 4096, 128], [1024, 4], [1, 1024]]),
                            AX.X, ALU.max, apply_absolute_value=True)
            pad(v)
            v.tensor_tensor(stc(pb, 1), stc(pb, 1), stc(pb, 2), ALU.mult)
            pad(v)
            v.tensor_scalar(stc(pb, 1), stc(pb, 1), EPS, None, ALU.max)
            pad(v)
            v.reciprocal(stc(pb, 3), stc(pb, 1))
            pad(v)
            v.tensor_scalar(stc(pb, 3), stc(pb, 3), 127.0, None, ALU.mult)
            v.tensor_scalar(stc(pb, 1), stc(pb, 1), 1.0 / 127.0, None, ALU.mult)
            v.tensor_tensor(stc(pb, 2), stc(pb, 2), stc(pb, 3), ALU.mult)
            pad(v)
            v.tensor_tensor(dap(tb2, 0, [[4096, 128], [1024, 4], [1, 1024]]),
                            dap(ctd, ofs, [[2 * 4096, 128], [1024, 4], [1, 1024]]),
                            stbc(pb, 2), ALU.mult)
            rnd_inplace(v).then_inc(sV.h, 1)
        sV.n += 1
        d_rnd.append(sV.n)
        n_rnd = sV.n
        w_tpq = d_tpq_s[S - 1] if S >= 1 else 0
        w_tps = d_tps_s[S - 1] if S >= 1 else 0

        @blk.scalar
        def _(s, pb=pb, n_rnd=n_rnd, w_tpq=w_tpq, w_tps=w_tps):
            s.wait_ge(sV.h, n_rnd)
            if w_tpq:
                s.wait_ge(sQ.h, w_tpq)
                s.wait_ge(sS.h, w_tps)
            mi = None
            for t in range(4):
                mi = s.activation(cqs[:, t * 1024:(t + 1) * 1024],
                                  tb1[:, t * 1024:(t + 1) * 1024], AF.Copy,
                                  bias=0.0, scale=stcol(pb, 1, t))
            mi.then_inc(sA.h, 1)
        sA.n += 1
        n_cq = sA.n

        for t4 in range(4):
            tt = S * 4 + t4
            tpb = tt % 2
            w_mmo = d_mm[tt - 2] if tt >= 2 else 0

            def tpd(s, t4=t4, tpb=tpb, n_cq=n_cq, w_mmo=w_mmo, q=(t4 % 2 == 0)):
                s.wait_ge(sA.h, n_cq)
                if w_mmo:
                    s.wait_ge(sM.h, w_mmo)
                for j in range(NC8):
                    s.dma_start(cqT[:, tpb * D + j * 128:tpb * D + (j + 1) * 128],
                                cqs[:, t4 * D + j * 128:t4 * D + (j + 1) * 128],
                                transpose=True).then_inc((sQ if q else sS).h, 16)
            if t4 % 2 == 0:
                blk.sync(tpd)
                sQ.n += 16 * NC8
            else:
                blk.scalar(tpd)
                sS.n += 16 * NC8
            n_tpq, n_tps = sQ.n, sS.n
            w_aev = d_aev[tt - 1] if tt >= 1 else 0

            @blk.tensor
            def _(t, tpb=tpb, n_tpq=n_tpq, n_tps=n_tps, w_aev=w_aev):
                t.wait_ge(sQ.h, n_tpq)
                t.wait_ge(sS.h, n_tps)
                if w_aev:
                    t.wait_ge(sV.h, w_aev)
                mi = None
                for j in range(NC8):
                    lhs = cqT[:, tpb * D + j * 128:tpb * D + (j + 1) * 128]
                    mi = t.matmul(ps_c[:, :], lhs, wt_q[:, j * D + 0:j * D + 512],
                                  start=(j == 0), stop=(j == NC8 - 1))
                    mi = t.matmul(ps_d[:, :], lhs, wt_q[:, j * D + 512:j * D + 1024],
                                  start=(j == 0), stop=(j == NC8 - 1))
                mi.then_inc(sM.h, 1)
            sM.n += 1
            d_mm.append(sM.n)
            n_omm = sM.n

            @blk.vector
            def _(v, t4=t4, n_omm=n_omm):
                v.wait_ge(sM.h, n_omm)
                v.tensor_scalar(aod[:, t4 * D:t4 * D + 512], ps_c[:, :],
                                gam[:, 3:4], None, ALU.mult, ALU.add,
                                accum_out=st[:, 48 + t4:49 + t4])
                v.tensor_scalar(aod[:, t4 * D + 512:(t4 + 1) * D], ps_d[:, :],
                                gam[:, 3:4], None, ALU.mult, ALU.add,
                                accum_out=st[:, 52 + t4:53 + t4]).then_inc(sV.h, 1)
            sV.n += 1
            d_aev.append(sV.n)
        n_aev4 = sV.n

        @blk.scalar
        def _(s, pb=pb, n_aev4=n_aev4):
            s.wait_ge(sV.h, n_aev4)
            mi = None
            for t in range(4):
                mi = s.activation(sqd[:, :], aod[:, t * 1024:(t + 1) * 1024],
                                  AF.Square, accum_out=stcol(pb, 1, t))
            mi.then_inc(sA.h, 1)
        sA.n += 1
        n_sqa = sA.n

        @blk.vector
        def _(v, pb=pb, n_sqa=n_sqa):
            v.wait_ge(sA.h, n_sqa)
            v.tensor_tensor(stc(pb, 0), st[:, 48:52], st[:, 52:56], ALU.add)
            pad(v)
            v.tensor_scalar(stc(pb, 0), stc(pb, 0), -1.0 / D, None, ALU.mult)
            v.tensor_scalar(stc(pb, 1), stc(pb, 1), 1.0 / D, None, ALU.mult)
            v.tensor_tensor(stc(pb, 2), stc(pb, 0), stc(pb, 0), ALU.mult)
            pad(v)
            v.tensor_tensor(stc(pb, 1), stc(pb, 1), stc(pb, 2), ALU.subtract)
            pad(v)
            v.tensor_scalar(stc(pb, 1), stc(pb, 1), EPS, None,
                            ALU.add).then_inc(sV.h, 1)
        sV.n += 1
        n_ty2 = sV.n

        @blk.scalar
        def _(s, pb=pb, n_ty2=n_ty2):
            s.wait_ge(sV.h, n_ty2)
            rsqrt(s, stc(pb, 2), stc(pb, 3), stc(pb, 1))
            s.drain()
            mi = None
            for t in range(4):
                mi = s.activation(tb2[:, t * 1024:(t + 1) * 1024],
                                  aod[:, t * 1024:(t + 1) * 1024], AF.Identity,
                                  bias=stcol(pb, 0, t), scale=1.0)
            mi.then_inc(sA.h, 1)
        sA.n += 1
        n_t1b = sA.n
        w_xsp = d_xsp[S - 1] if S >= 1 else 0

        @blk.vector
        def _(v, pb=pb, n_t1b=n_t1b, w_xsp=w_xsp):
            v.wait_ge(sA.h, n_t1b)
            if w_xsp:
                v.wait_ge(sG.h, w_xsp)
            v.tensor_tensor(dap(tb2, 0, [[4096, 128], [1024, 4], [1, 1024]]),
                            dap(tb2, 0, [[4096, 128], [1024, 4], [1, 1024]]),
                            stbc(pb, 2), ALU.mult)
            v.tensor_tensor(x1d[:, :], tb2[:, :], xd[:, :],
                            ALU.add).then_inc(sV.h, 1)
        sV.n += 1
        d_x1.append(sV.n)
        n_x1 = sV.n
        if S + 1 < NS:
            emit_dload(S + 1)

        @blk.gpsimd
        def _(g, S=S, n_x1=n_x1):
            g.wait_ge(sV.h, n_x1)
            g.dma_start(dap(s_x1, S * 4 * 131072,
                            [[1024, 128], [131072, 4], [1, 1024]]),
                        dap(x1d, 0, [[4096, 128], [1024, 4], [1, 1024]])
                        ).then_inc(sG.h, 16)
        sG.n += 16
        d_xsp.append(sG.n)

        @blk.scalar
        def _(s, pb=pb, n_x1=n_x1):
            s.wait_ge(sV.h, n_x1)
            mi = None
            for t in range(4):
                xt = x1d[:, t * 1024:(t + 1) * 1024]
                s.activation(sqd[:, :], xt, AF.Copy, accum_out=stcol(pb, 0, t))
                mi = s.activation(sqd[:, :], xt, AF.Square,
                                  accum_out=stcol(pb, 1, t))
            mi.then_inc(sA.h, 1)
        sA.n += 1
        n_stx = sA.n

        @blk.vector
        def _(v, pb=pb, n_stx=n_stx):
            v.wait_ge(sA.h, n_stx)
            v.tensor_scalar(stc(pb, 0), stc(pb, 0), -1.0 / D, None, ALU.mult)
            v.tensor_scalar(stc(pb, 1), stc(pb, 1), 1.0 / D, None, ALU.mult)
            v.tensor_tensor(stc(pb, 2), stc(pb, 0), stc(pb, 0), ALU.mult)
            pad(v)
            v.tensor_tensor(stc(pb, 1), stc(pb, 1), stc(pb, 2), ALU.subtract)
            pad(v)
            v.tensor_scalar(stc(pb, 1), stc(pb, 1), EPS, None,
                            ALU.add).then_inc(sV.h, 1)
        sV.n += 1
        n_ty3 = sV.n

        @blk.scalar
        def _(s, pb=pb, n_ty3=n_ty3):
            s.wait_ge(sV.h, n_ty3)
            rsqrt(s, stc(pb, 2), stc(pb, 3), stc(pb, 1))
            s.drain()
            mi = None
            for t in range(4):
                mi = s.activation(tb2[:, t * 1024:(t + 1) * 1024],
                                  x1d[:, t * 1024:(t + 1) * 1024], AF.Identity,
                                  bias=stcol(pb, 0, t), scale=1.0)
            mi.then_inc(sA.h, 1)
        sA.n += 1
        n_t1c = sA.n
        w_lt = d_ltq_s[S - 1] if S >= 1 else 0
        w_lts = d_lts_s[S - 1] if S >= 1 else 0

        @blk.vector
        def _(v, pb=pb, n_t1c=n_t1c, w_lt=w_lt, w_lts=w_lts):
            v.wait_ge(sA.h, n_t1c)
            if w_lt:
                v.wait_ge(sQ.h, w_lt)
                v.wait_ge(sS.h, w_lts)
            v.tensor_tensor(dap(xl2, 0, [[4096, 128], [1024, 4], [1, 1024]]),
                            dap(tb2, 0, [[4096, 128], [1024, 4], [1, 1024]]),
                            stbc(pb, 2), ALU.mult).then_inc(sV.h, 1)
        sV.n += 1
        d_fin.append(sV.n)
        n_fin = sV.n

        for t4 in range(4):
            tt = S * 4 + t4

            @blk.sync
            def _(s, t4=t4, tt=tt, n_fin=n_fin):
                s.wait_ge(sV.h, n_fin)
                for j in range(NC8):
                    s.dma_start(xln2T[:, j * T + tt * 128:j * T + (tt + 1) * 128],
                                xl2[:, t4 * D + j * 128:t4 * D + (j + 1) * 128],
                                transpose=True).then_inc(sQ.h, 16)
            sQ.n += 16 * NC8
        d_ltq_s.append(sQ.n)
        d_lts_s.append(sS.n)
        d_tpq_s.append(sQ.n)
        d_tps_s.append(sS.n)

    n_dq_end = sQ.n
    n_dg_end = sG.n
    n_dv_end = sV.n
    n_da_end = sA.n
    n_ds_end = sS.n
    d_es.close()
    wt_es.close()
    r_es.close()

    e_es = ExitStack()
    w1s = sb(e_es, "w1s", [128, 2 * 1024], BF16)
    w2s = sb(e_es, "w2s", [128, 2 * 4096], BF16)
    hT = sb(e_es, "hT", [128, 32 * 1024], BF16)
    yst = sb(e_es, "yst", [128, 2 * 1024], BF16)
    ytok = sb(e_es, "ytok", [128, 8 * 1024], BF16, is_transpose_dest=True)
    x1f = sb(e_es, "x1f", [128, 4096], F32)
    obuf = sb(e_es, "obuf", [128, 4096], F32)

    out_st, fin_all = [], []
    ffn2_mm_prev = 0
    fin_prev_block = 0
    f1_mm, f2_mm = [], []
    for Tb in range(4):
        relu_cnt = []
        for fc in range(32):
            gi = Tb * 32 + fc
            w_slab = f1_mm[(gi - 2) * 2 + 1] if gi >= 2 else 0

            @blk.gpsimd
            def _(g, Tb=Tb, fc=fc, w_slab=w_slab):
                if Tb == 0 and fc == 0:
                    g.wait_ge(sV.h, n_dv_end)
                    g.wait_ge(sA.h, n_da_end)
                    g.wait_ge(sQ.h, n_dq_end)
                    g.wait_ge(sG.h, n_dg_end)
                    g.wait_ge(sS.h, n_ds_end)
                if w_slab:
                    g.wait_ge(sM.h, w_slab)
                g.dma_start(dap(w1s, (fc % 2) * 1024, [[2048, 128], [1, 1024]]),
                            dap(s_w1t, fc * 131072, [[128, 128], [16384, 8], [1, 128]])
                            ).then_inc(sG.h, 16)
            sG.n += 16
            n_sl = sG.n
            for n in range(2):
                u = fc * 2 + n
                bank = ps_c if u % 2 == 0 else ps_d
                w_relu = relu_cnt[u - 2] if u >= 2 else 0

                @blk.tensor
                def _(t, Tb=Tb, fc=fc, n=n, bank=bank, n_sl=n_sl,
                      w_relu=w_relu, w_prev2=ffn2_mm_prev):
                    t.wait_ge(sG.h, n_sl)
                    if fc == 0 and n == 0:
                        t.wait_ge(sQ.h, n_dq_end)
                        if w_prev2:
                            t.wait_ge(sM.h, w_prev2)
                    if w_relu:
                        t.wait_ge(sA.h, w_relu)
                    mi = None
                    for dc in range(NC8):
                        lhs = w1s[:, (fc % 2) * 1024 + dc * 128:
                                  (fc % 2) * 1024 + (dc + 1) * 128]
                        mi = t.matmul(bank[:, :], lhs,
                                      xln2T[:, dc * T + Tb * 1024 + n * 512:
                                            dc * T + Tb * 1024 + (n + 1) * 512],
                                      start=(dc == 0), stop=(dc == NC8 - 1))
                    mi.then_inc(sM.h, 1)
                sM.n += 1
                f1_mm.append(sM.n)
                n_f1 = sM.n

                @blk.scalar
                def _(s, fc=fc, n=n, bank=bank, n_f1=n_f1):
                    s.wait_ge(sM.h, n_f1)
                    s.activation(hT[:, fc * 1024 + n * 512:fc * 1024 + (n + 1) * 512],
                                 bank[:, :], AF.Relu, bias=b1c[:, fc:fc + 1],
                                 scale=1.0).then_inc(sA.h, 1)
                sA.n += 1
                relu_cnt.append(sA.n)
        n_relu_all = sA.n

        yev, ytp = [], []
        for dc in range(NC8):
            gj = Tb * NC8 + dc
            w_slab2 = f2_mm[(gj - 2) * 2 + 1] if gj >= 2 else 0

            @blk.gpsimd
            def _(g, dc=dc, w_slab2=w_slab2):
                if w_slab2:
                    g.wait_ge(sM.h, w_slab2)
                g.dma_start(dap(w2s, (dc % 2) * 4096, [[8192, 128], [1, 4096]]),
                            dap(s_w2t, dc * 524288, [[128, 128], [16384, 32], [1, 128]])
                            ).then_inc(sG.h, 16)
            sG.n += 16
            n_sl = sG.n
            for n in range(2):
                u2 = dc * 2 + n
                boff = 0 if u2 % 2 == 0 else 512
                w_yev = yev[u2 - 2] if u2 >= 2 else 0
                w_ytp = ytp[dc - 2] if (dc >= 2 and n == 0) else 0

                @blk.tensor
                def _(t, dc=dc, n=n, boff=boff, n_sl=n_sl,
                      n_relu_all=n_relu_all, w_yev=w_yev):
                    t.wait_ge(sG.h, n_sl)
                    if dc == 0 and n == 0:
                        t.wait_ge(sA.h, n_relu_all)
                    if w_yev:
                        t.wait_ge(sV.h, w_yev)
                    mi = None
                    for fc in range(32):
                        lhs = w2s[:, (dc % 2) * 4096 + fc * 128:
                                  (dc % 2) * 4096 + (fc + 1) * 128]
                        mi = t.matmul(ps_a[:, boff:boff + 512], lhs,
                                      hT[:, fc * 1024 + n * 512:fc * 1024 + (n + 1) * 512],
                                      start=(fc == 0), stop=(fc == 31))
                    mi.then_inc(sM.h, 1)
                sM.n += 1
                f2_mm.append(sM.n)
                n_f2 = sM.n

                @blk.vector
                def _(v, dc=dc, n=n, boff=boff, n_f2=n_f2, w_ytp=w_ytp):
                    v.wait_ge(sM.h, n_f2)
                    if w_ytp:
                        v.wait_ge(sQ.h, w_ytp[0])
                        v.wait_ge(sS.h, w_ytp[1])
                    v.tensor_scalar(yst[:, (dc % 2) * 1024 + n * 512:
                                        (dc % 2) * 1024 + (n + 1) * 512],
                                    ps_a[:, boff:boff + 512],
                                    b2c[:, dc:dc + 1], None,
                                    ALU.add).then_inc(sV.h, 1)
                sV.n += 1
                yev.append(sV.n)
            n_ye = sV.n

            def ytpb(s, dc=dc, n_ye=n_ye, fin_prev_block=fin_prev_block,
                     q=(dc % 2 == 0)):
                s.wait_ge(sV.h, n_ye)
                if fin_prev_block:
                    s.wait_ge(sV.h, fin_prev_block)
                for ts_ in range(8):
                    s.dma_start(ytok[:, ts_ * 1024 + dc * 128:ts_ * 1024 + (dc + 1) * 128],
                                yst[:, (dc % 2) * 1024 + ts_ * 128:
                                    (dc % 2) * 1024 + (ts_ + 1) * 128],
                                transpose=True).then_inc((sQ if q else sS).h, 16)
            if dc % 2 == 0:
                blk.sync(ytpb)
                sQ.n += 128
            else:
                blk.scalar(ytpb)
                sS.n += 128
            ytp.append((sQ.n, sS.n))
        n_ytp_all = sQ.n
        n_ytp_all_s = sS.n

        for gq in range(2):
            gg = Tb * 2 + gq
            pb = gg % 2
            w_xlf = fin_all[gg - 1] if gg >= 1 else 0

            @blk.sync
            def _(s, Tb=Tb, gq=gq, w_xlf=w_xlf):
                if w_xlf:
                    s.wait_ge(sV.h, w_xlf)
                s.dma_start(dap(x1f, 0, [[4096, 128], [1024, 4], [1, 1024]]),
                            dap(s_x1, (Tb * 8 + gq * 4) * 131072,
                                [[1024, 128], [131072, 4], [1, 1024]])
                            ).then_inc(sQ.h, 16)
            sQ.n += 16
            n_xl = sQ.n
            w_ost = out_st[gg - 1] if gg >= 1 else 0
            w_st = fin_all[gg - 2] if gg >= 2 else 0

            @blk.scalar
            def _(s, pb=pb, gq=gq, n_ytp_all=n_ytp_all, n_ytp_all_s=n_ytp_all_s,
                  w_ost=w_ost, w_st=w_st):
                s.wait_ge(sQ.h, n_ytp_all)
                s.wait_ge(sS.h, n_ytp_all_s)
                if w_ost:
                    s.wait_ge(sQ.h, w_ost)
                if w_st:
                    s.wait_ge(sV.h, w_st)
                mi = None
                for t in range(4):
                    yt = ytok[:, gq * 4096 + t * 1024:gq * 4096 + (t + 1) * 1024]
                    ot = obuf[:, t * 1024:(t + 1) * 1024]
                    s.activation(ot, yt, AF.Copy, accum_out=stcol(pb, 0, t))
                    mi = s.activation(ot, yt, AF.Square, accum_out=stcol(pb, 1, t))
                mi.then_inc(sA.h, 1)
            sA.n += 1
            n_st8 = sA.n

            @blk.vector
            def _(v, pb=pb, n_st8=n_st8):
                v.wait_ge(sA.h, n_st8)
                v.tensor_scalar(stc(pb, 0), stc(pb, 0), -1.0 / D, None, ALU.mult)
                v.tensor_scalar(stc(pb, 1), stc(pb, 1), 1.0 / D, None, ALU.mult)
                v.tensor_tensor(stc(pb, 2), stc(pb, 0), stc(pb, 0), ALU.mult)
                pad(v)
                v.tensor_tensor(stc(pb, 1), stc(pb, 1), stc(pb, 2), ALU.subtract)
                pad(v)
                v.tensor_scalar(stc(pb, 1), stc(pb, 1), EPS, None,
                                ALU.add).then_inc(sV.h, 1)
            sV.n += 1
            n_ty = sV.n

            @blk.scalar
            def _(s, pb=pb, gq=gq, n_ty=n_ty):
                s.wait_ge(sV.h, n_ty)
                rsqrt(s, stc(pb, 2), stc(pb, 3), stc(pb, 1))
                s.drain()
                mi = None
                for t in range(4):
                    mi = s.activation(obuf[:, t * 1024:(t + 1) * 1024],
                                      ytok[:, gq * 4096 + t * 1024:
                                           gq * 4096 + (t + 1) * 1024],
                                      AF.Identity, bias=stcol(pb, 0, t), scale=1.0)
                mi.then_inc(sA.h, 1)
            sA.n += 1
            n_t1 = sA.n

            @blk.vector
            def _(v, pb=pb, n_t1=n_t1, n_xl=n_xl):
                v.wait_ge(sA.h, n_t1)
                v.wait_ge(sQ.h, n_xl)
                v.tensor_tensor(dap(obuf, 0, [[4096, 128], [1024, 4], [1, 1024]]),
                                dap(obuf, 0, [[4096, 128], [1024, 4], [1, 1024]]),
                                stbc(pb, 2), ALU.mult)
                v.tensor_tensor(obuf[:, :], obuf[:, :], x1f[:, :],
                                ALU.add).then_inc(sV.h, 1)
            sV.n += 1
            fin_all.append(sV.n)
            n_ob = sV.n

            @blk.sync
            def _(s, Tb=Tb, gq=gq, n_ob=n_ob):
                s.wait_ge(sV.h, n_ob)
                s.dma_start(dap(out, (Tb * 8 + gq * 4) * 131072,
                                [[1024, 128], [131072, 4], [1, 1024]]),
                            obuf[:, :]).then_inc(sQ.h, 16)
            sQ.n += 16
            out_st.append(sQ.n)
        fin_prev_block = fin_all[-1]
        ffn2_mm_prev = sM.n

    n_final = sQ.n

    @blk.sync
    def _(s):
        s.wait_ge(sQ.h, n_final)
        s.wait_ge(sG.h, sG.n)
        s.wait_ge(sS.h, sS.n)

    e_es.close()
    p_es.close()
    g_es.close()
    return nc


_NC_CACHE = None


def kernel(L0_attn_x, L0_attn_wq, L0_attn_wk, L0_attn_wv, L0_attn_wo,
           L0_ffn_W1, L0_ffn_b1, L0_ffn_W2, L0_ffn_b2):
    global _NC_CACHE
    x = np.ascontiguousarray(np.asarray(L0_attn_x, dtype=np.float32))
    wq = np.ascontiguousarray(np.asarray(L0_attn_wq, dtype=np.float32))
    wk = np.ascontiguousarray(np.asarray(L0_attn_wk, dtype=np.float32))
    wv = np.ascontiguousarray(np.asarray(L0_attn_wv, dtype=np.float32))
    wo = np.ascontiguousarray(np.asarray(L0_attn_wo, dtype=np.float32))
    w1 = np.ascontiguousarray(np.asarray(L0_ffn_W1, dtype=np.float32))
    b1 = np.ascontiguousarray(np.asarray(L0_ffn_b1, dtype=np.float32)).reshape(32, 128)
    w2 = np.ascontiguousarray(np.asarray(L0_ffn_W2, dtype=np.float32))
    b2 = np.ascontiguousarray(np.asarray(L0_ffn_b2, dtype=np.float32)).reshape(8, 128)

    if _NC_CACHE is None:
        _NC_CACHE = build()
    nc = _NC_CACHE
    in_maps = [{"x": np.ascontiguousarray(x[i]), "wq": wq, "wk": wk, "wv": wv,
                "wo": wo, "w1": w1, "b1": b1, "w2": w2, "b2": b2}
               for i in range(8)]
    res = run_bass_kernel_spmd(nc, in_maps, core_ids=list(range(8)))
    return np.stack([r["out"] for r in res.results], axis=0)


# version 09
# speedup vs baseline: 1.2356x; 1.0148x over previous
import numpy as np
from contextlib import ExitStack

import concourse.bass as bass
import concourse.mybir as mybir
from concourse.bass_utils import run_bass_kernel_spmd

F32 = mybir.dt.float32
BF16 = mybir.dt.bfloat16
AF = mybir.ActivationFunctionType
ALU = mybir.AluOpType
AX = mybir.AxisListType

T, D, H, DH, FF = 4096, 1024, 16, 64, 4096
NT, NC8, NS = 32, 8, 8
C23 = 12582912.0
C52 = 6755399441055744.0
EPS = 1e-5


def dap(t, off, dims):
    return bass.AP(t, off, [list(d) for d in dims])


class DR:
    _OPS = {"tensor_scalar", "tensor_tensor", "tensor_copy", "reduce_sum",
            "reduce_max", "tensor_reduce", "reciprocal", "memset",
            "scalar_tensor_tensor", "tensor_single_scalar", "activation"}

    def __init__(self, e):
        self._e = e

    def __getattr__(self, name):
        f = getattr(self._e, name)
        if name in DR._OPS:
            def g(*a, **k):
                r = f(*a, **k)
                self._e.drain()
                return r
            return g
        return f


def build():
    nc = bass.Bass("TRN2", target_bir_lowering=False,
                   detect_race_conditions=False)

    x_in = nc.dram_tensor("x", [T, D], F32, kind="ExternalInput")
    w_q = nc.dram_tensor("wq", [D, D], F32, kind="ExternalInput")
    w_k = nc.dram_tensor("wk", [D, D], F32, kind="ExternalInput")
    w_v = nc.dram_tensor("wv", [D, D], F32, kind="ExternalInput")
    w_o = nc.dram_tensor("wo", [D, D], F32, kind="ExternalInput")
    w_1 = nc.dram_tensor("w1", [FF, D], F32, kind="ExternalInput")
    b_1 = nc.dram_tensor("b1", [32, 128], F32, kind="ExternalInput")
    w_2 = nc.dram_tensor("w2", [D, FF], F32, kind="ExternalInput")
    b_2 = nc.dram_tensor("b2", [8, 128], F32, kind="ExternalInput")
    out = nc.dram_tensor("out", [T, D], F32, kind="ExternalOutput")

    s_ctx = nc.dram_tensor("s_ctx", [T, D], BF16)
    s_x1 = nc.dram_tensor("s_x1", [T, D], F32)
    s_w1t = nc.dram_tensor("s_w1t", [32, 8, 128, 128], BF16)
    s_w2t = nc.dram_tensor("s_w2t", [8, 32, 128, 128], BF16)

    g_es = ExitStack()

    def sb(stk, name, shape, dtype, **kw):
        return stk.enter_context(nc.sbuf_tensor(name, shape, dtype, **kw))

    wsel = sb(g_es, "wsel", [128, 1], F32)
    ones = sb(g_es, "ones", [128, 128], F32)
    kq = sb(g_es, "kq", [128, 64], F32)
    gam = sb(g_es, "gam", [128, 8], F32)
    b1c = sb(g_es, "b1c", [128, 32], F32)
    b2c = sb(g_es, "b2c", [128, 16], F32)
    gat = sb(g_es, "gat", [128, 128], BF16)
    gatT = sb(g_es, "gatT", [128, 128], BF16, is_transpose_dest=True)
    st = sb(g_es, "st", [128, 64], F32)

    def stc(p, i):
        return st[:, 16 * p + 4 * i:16 * p + 4 * (i + 1)]

    def stcol(p, i, t):
        c = 16 * p + 4 * i + t
        return st[:, c:c + 1]

    def stbc(p, i):
        return dap(st, 16 * p + 4 * i, [[64, 128], [1, 4], [0, 1024]])

    r_es = ExitStack()
    tb1 = sb(r_es, "tb1", [128, 4096], F32)
    tb2 = sb(r_es, "tb2", [128, 4096], F32)

    wt_es = ExitStack()
    wt_q = sb(wt_es, "wt_q", [128, NC8 * D], BF16, is_transpose_dest=True)

    w_es = ExitStack()
    wbig = sb(w_es, "wbig", [128, 4096], F32)
    qbuf = sb(w_es, "qbuf", [128, 4096], BF16)
    wtst = sb(w_es, "wtst", [128, 4096], BF16, is_transpose_dest=True)
    bk_es = ExitStack()
    wt_k = sb(bk_es, "wt_k", [128, NC8 * D], BF16, is_transpose_dest=True)
    wt_v = sb(bk_es, "wt_v", [128, NC8 * D], BF16, is_transpose_dest=True)
    pre_es = ExitStack()
    wbig2 = sb(pre_es, "wbig2", [128, 4096], F32)
    qbuf2 = sb(pre_es, "qbuf2", [128, 4096], BF16)

    p_es = ExitStack()
    ps_g = p_es.enter_context(nc.psum_tensor("ps_g", [128, 512], F32))
    ps_a = p_es.enter_context(nc.psum_tensor("ps_a", [128, 1024], F32))
    ps_b = p_es.enter_context(nc.psum_tensor("ps_b", [128, 1024], F32))
    ps_c = p_es.enter_context(nc.psum_tensor("ps_c", [128, 512], F32))
    ps_d = p_es.enter_context(nc.psum_tensor("ps_d", [128, 512], F32))

    class Ctr:
        def __init__(self, name):
            self.h = g_es.enter_context(nc.semaphore(name))
            self.n = 0
    sQ, sG, sV, sA, sM, sS, sP = (Ctr("sQ"), Ctr("sG"), Ctr("sV"), Ctr("sA"),
                                  Ctr("sM"), Ctr("sS"), Ctr("sP"))

    blk = g_es.enter_context(nc.Block())

    def pad(v):
        v.tensor_scalar(st[:, 56:60], st[:, 60:64], 1.0, None, ALU.mult)

    def rnd_inplace(v):
        v.tensor_scalar(tb1[:, :], tb2[:, :], C23, C23, ALU.add, ALU.subtract)
        v.tensor_scalar(tb2[:, :], tb2[:, :], C52, C52, ALU.add, ALU.subtract)
        v.tensor_tensor(tb2[:, :], tb2[:, :], tb1[:, :], ALU.subtract)
        return v.scalar_tensor_tensor(tb1[:, :], tb2[:, :], wsel[:, 0:1],
                                      tb1[:, :], ALU.mult, ALU.add)

    def rsqrt(s, dst, tmp, src):
        s.activation(tmp, src, AF.Ln)
        s.drain()
        return s.activation(dst, tmp, AF.Exp, bias=0.0, scale=-0.5)

    @blk.scalar
    def _(s):
        s.add_instruction(mybir.InstLoadActFuncSet(
            name=nc.get_next_instruction_name(), act_func_set_id=6, ins=[], outs=[]))

    @blk.vector
    def _(v):
        v.memset(wsel[:, :], 2.0 ** -30)
        v.drain()
        v.tensor_scalar(wsel[:, :], wsel[:, :], 1.0, 1.0, ALU.add, ALU.subtract)
        v.drain()
        v.tensor_scalar(wsel[:, :], wsel[:, :], 1e30, 1.0, ALU.mult, ALU.min)
        v.drain()
        v.memset(ones[:, :], 1.0)
        v.memset(st[:, :], 0.0)
        v.memset(gat[:, :], 0.0)
        v.memset(wbig[0:32, 128:256], 0.0).then_inc(sV.h, 1)
    sV.n += 1
    n_ms0 = sV.n

    @blk.gpsimd
    def _(g):
        g.wait_ge(sV.h, n_ms0)
        g.dma_start(dap(wbig, 0, [[4096, 32], [1, 128]]),
                    dap(b_1, 0, [[128, 32], [1, 128]])).then_inc(sG.h, 16)
        g.dma_start(dap(wbig, 128, [[4096, 8], [1, 128]]),
                    dap(b_2, 0, [[128, 8], [1, 128]])).then_inc(sG.h, 16)
    sG.n += 32
    n_bld = sG.n

    @blk.vector
    def _(v):
        v.wait_ge(sG.h, n_bld)
        v.tensor_copy(gat[0:32, 0:128], wbig[0:32, 0:128])
        v.tensor_tensor(qbuf[0:32, 0:128], wbig[0:32, 0:128],
                        gat[0:32, 0:128], ALU.subtract)
        v.tensor_copy(qbuf[0:16, 128:256], wbig[0:16, 128:256])
        v.tensor_tensor(qbuf[0:16, 256:384], wbig[0:16, 128:256],
                        qbuf[0:16, 128:256], ALU.subtract).then_inc(sV.h, 1)
    sV.n += 1
    n_bv = sV.n

    @blk.sync
    def _(s):
        s.wait_ge(sV.h, n_bv)
        s.dma_start(gatT[:, 0:32], gat[0:32, 0:128], transpose=True).then_inc(sQ.h, 16)
        s.dma_start(wtst[:, 0:32], qbuf[0:32, 0:128], transpose=True).then_inc(sQ.h, 16)
        s.dma_start(gatT[:, 32:48], qbuf[0:16, 128:256], transpose=True).then_inc(sQ.h, 16)
        s.dma_start(wtst[:, 32:48], qbuf[0:16, 256:384], transpose=True).then_inc(sQ.h, 16)
    sQ.n += 64
    n_bt = sQ.n

    @blk.vector
    def _(v):
        v.wait_ge(sQ.h, n_bt)
        v.tensor_tensor(b1c[:, :], gatT[:, 0:32], wtst[:, 0:32], ALU.add)
        v.tensor_tensor(b2c[:, :], gatT[:, 32:48], wtst[:, 32:48],
                        ALU.add).then_inc(sV.h, 1)
    sV.n += 1

    slotV = {}
    slotQ = {}

    def prep_attn(w_dram, dst_wt, gcol, tguard=0, wb=None, qb=None):
        wb = wbig if wb is None else wb
        qb = qbuf if qb is None else qb
        wfree = slotV.get(id(wb), 0)
        for half in range(2):
            red = wfree if half == 0 else sV.n

            @blk.gpsimd
            def _(g, half=half, red=red):
                if red:
                    g.wait_ge(sV.h, red)
                g.dma_start(dap(wb, 0, [[4096, 128], [1024, 4], [1, 1024]]),
                            dap(w_dram, half * 524288,
                                [[1024, 128], [131072, 4], [1, 1024]])).then_inc(sG.h, 16)
            sG.n += 16
            ld = sG.n

            @blk.vector
            def _(v, half=half, ld=ld):
                v.wait_ge(sG.h, ld)
                v.reduce_sum(kq[:, 40 + half:41 + half], wb[:, :], AX.X,
                             apply_absolute_value=True).then_inc(sV.h, 1)
            sV.n += 1

        @blk.vector
        def _(v):
            v.drain()
            v.tensor_tensor(kq[:, 42:43], kq[:, 40:41], kq[:, 41:42],
                            ALU.add).then_inc(sV.h, 1)
        sV.n += 1
        pd = sV.n

        @blk.tensor
        def _(t, pd=pd):
            t.wait_ge(sV.h, pd)
            t.matmul(ps_g[:, 0:1], ones[:, :], kq[:, 42:43],
                     start=True, stop=True).then_inc(sM.h, 1)
        sM.n += 1
        md = sM.n

        @blk.vector
        def _(v, md=md):
            v.wait_ge(sM.h, md)
            v.tensor_scalar(kq[:, 43:44], ps_g[:, 0:1], 1.0 / (D * D), EPS,
                            ALU.mult, ALU.max)
            pad(v)
            v.tensor_copy(gam[:, gcol:gcol + 1], kq[:, 43:44])
            v.reciprocal(kq[:, 44:45], kq[:, 43:44]).then_inc(sV.h, 1)
        sV.n += 1

        qfree = slotQ.get(id(qb), 0)
        for half in range(2):
            qd0 = sV.n
            tp0 = sQ.n
            actg = sA.n

            @blk.gpsimd
            def _(g, half=half, qd0=qd0):
                g.wait_ge(sV.h, qd0)
                g.dma_start(dap(wb, 0, [[4096, 128], [1024, 4], [1, 1024]]),
                            dap(w_dram, half * 524288,
                                [[1024, 128], [131072, 4], [1, 1024]])).then_inc(sG.h, 16)
            sG.n += 16
            ld = sG.n

            @blk.vector
            def _(v, half=half, ld=ld, tp0=tp0, actg=actg, qfree=qfree):
                v.wait_ge(sG.h, ld)
                if actg:
                    v.wait_ge(sA.h, actg)
                qg = tp0 if half == 1 else qfree
                if qg:
                    v.wait_ge(sQ.h, qg)
                v.tensor_scalar(tb2[:, :], wb[:, :], kq[:, 44:45], None, ALU.mult)
                v.tensor_scalar(tb1[:, :], tb2[:, :], C23, C23, ALU.add, ALU.subtract)
                v.tensor_scalar(wb[:, :], tb2[:, :], C52, C52, ALU.add, ALU.subtract)
                v.tensor_tensor(wb[:, :], wb[:, :], tb1[:, :], ALU.subtract)
                v.scalar_tensor_tensor(tb1[:, :], wb[:, :], wsel[:, 0:1],
                                       tb1[:, :], ALU.mult, ALU.add)
                v.tensor_scalar(qb[:, :], tb1[:, :], -1.0, 1.0,
                                ALU.max, ALU.min).then_inc(sV.h, 1)
            sV.n += 1
            qd = sV.n

            @blk.sync
            def _(s, half=half, qd=qd, tguard=tguard):
                s.wait_ge(sV.h, qd)
                if tguard and half == 0:
                    s.wait_ge(sM.h, tguard)
                for cl in range(4):
                    c = half * 4 + cl
                    for j in range(NC8):
                        s.dma_start(dst_wt[:, j * D + c * 128:j * D + (c + 1) * 128],
                                    qb[:, cl * D + j * 128:cl * D + (j + 1) * 128],
                                    transpose=True).then_inc(sQ.h, 16)
            sQ.n += 16 * 32
        slotV[id(wb)] = sV.n
        slotQ[id(qb)] = sQ.n
        return sQ.n

    def prep_ffn(w_dram, spill, layout):
        for grp in range(8):
            red = sV.n

            @blk.gpsimd
            def _(g, grp=grp, red=red):
                g.wait_ge(sV.h, red)
                if layout == "w1":
                    g.dma_start(dap(wbig, 0, [[4096, 128], [1024, 4], [1, 1024]]),
                                dap(w_dram, grp * 524288,
                                    [[1024, 128], [131072, 4], [1, 1024]])
                                ).then_inc(sG.h, 16)
                else:
                    g.dma_start(dap(wbig, 0, [[4096, 128], [1, 4096]]),
                                dap(w_dram, grp * 524288, [[4096, 128], [1, 4096]])
                                ).then_inc(sG.h, 16)
            sG.n += 16
            ld = sG.n

            @blk.vector
            def _(v, grp=grp, ld=ld):
                v.wait_ge(sG.h, ld)
                v.reduce_sum(kq[:, 45:46], wbig[:, :], AX.X, apply_absolute_value=True)
                pad(v)
                if grp == 0:
                    v.tensor_copy(kq[:, 46:47], kq[:, 45:46]).then_inc(sV.h, 1)
                else:
                    v.tensor_tensor(kq[:, 46:47], kq[:, 46:47], kq[:, 45:46],
                                    ALU.add).then_inc(sV.h, 1)
            sV.n += 1
            yield
        pd = sV.n

        @blk.tensor
        def _(t, pd=pd):
            t.wait_ge(sV.h, pd)
            t.matmul(ps_g[:, 0:1], ones[:, :], kq[:, 46:47],
                     start=True, stop=True).then_inc(sM.h, 1)
        sM.n += 1
        md = sM.n

        @blk.vector
        def _(v, md=md):
            v.wait_ge(sM.h, md)
            v.tensor_scalar(kq[:, 43:44], ps_g[:, 0:1], 1.0 / (FF * D), EPS,
                            ALU.mult, ALU.add)
            pad(v)
            v.reciprocal(kq[:, 44:45], kq[:, 43:44]).then_inc(sV.h, 1)
        sV.n += 1
        yield

        spl_prev = 0
        tpg_prev = sQ.n
        for grp in range(8):
            qd0 = sV.n
            actg = sA.n

            @blk.gpsimd
            def _(g, grp=grp, qd0=qd0):
                g.wait_ge(sV.h, qd0)
                if layout == "w1":
                    g.dma_start(dap(wbig, 0, [[4096, 128], [1024, 4], [1, 1024]]),
                                dap(w_dram, grp * 524288,
                                    [[1024, 128], [131072, 4], [1, 1024]])
                                ).then_inc(sG.h, 16)
                else:
                    g.dma_start(dap(wbig, 0, [[4096, 128], [1, 4096]]),
                                dap(w_dram, grp * 524288, [[4096, 128], [1, 4096]])
                                ).then_inc(sG.h, 16)
            sG.n += 16
            ld = sG.n

            @blk.vector
            def _(v, grp=grp, ld=ld, actg=actg, tpg_prev=tpg_prev):
                v.wait_ge(sG.h, ld)
                if actg:
                    v.wait_ge(sA.h, actg)
                v.wait_ge(sQ.h, tpg_prev)
                v.tensor_scalar(tb2[:, :], wbig[:, :], kq[:, 44:45], None, ALU.mult)
                v.tensor_scalar(tb1[:, :], tb2[:, :], C23, C23, ALU.add, ALU.subtract)
                v.tensor_scalar(wbig[:, :], tb2[:, :], C52, C52, ALU.add, ALU.subtract)
                v.tensor_tensor(wbig[:, :], wbig[:, :], tb1[:, :], ALU.subtract)
                v.scalar_tensor_tensor(tb1[:, :], wbig[:, :], wsel[:, 0:1],
                                       tb1[:, :], ALU.mult, ALU.add)
                v.tensor_scalar(qbuf[:, :], tb1[:, :], -1.0, 1.0,
                                ALU.max, ALU.min).then_inc(sV.h, 1)
            sV.n += 1
            qd = sV.n

            @blk.sync
            def _(s, grp=grp, qd=qd, spl_prev=spl_prev):
                s.wait_ge(sV.h, qd)
                if spl_prev:
                    s.wait_ge(sQ.h, spl_prev)
                for bi in range(32):
                    s.dma_start(wtst[:, bi * 128:(bi + 1) * 128],
                                qbuf[:, bi * 128:(bi + 1) * 128],
                                transpose=True).then_inc(sQ.h, 16)
            sQ.n += 16 * 32
            tp = sQ.n
            tpg_prev = tp

            @blk.sync
            def _(s, grp=grp, tp=tp):
                s.wait_ge(sQ.h, tp)
                if layout == "w1":
                    s.dma_start(dap(spill, grp * 4 * 131072,
                                    [[128, 128], [131072, 4], [16384, 8], [1, 128]]),
                                dap(wtst, 0, [[4096, 128], [1024, 4], [128, 8], [1, 128]])
                                ).then_inc(sQ.h, 16)
                else:
                    s.dma_start(dap(spill, grp * 524288,
                                    [[128, 128], [16384, 32], [1, 128]]),
                                dap(wtst, 0, [[4096, 128], [128, 32], [1, 128]])
                                ).then_inc(sQ.h, 16)
            sQ.n += 16
            spl_prev = sQ.n
            yield

    prep_attn(w_v, wt_v, 2)
    prep_attn(w_q, wt_q, 0, wb=wbig2, qb=qbuf2)
    n_wprep = prep_attn(w_k, wt_k, 1)
    pre_q_tp = slotQ[id(qbuf2)]
    pre_q_v = slotV[id(wbig2)]
    pre_es.close()

    a_es = ExitStack()
    xa = sb(a_es, "xa", [128, 4096], F32)
    sqr = sb(a_es, "sqr", [128, 1024], F32)
    xqs = sb(a_es, "xqs", [128, 2 * 4096], BF16)
    xqTt = sb(a_es, "xqTt", [128, 2 * D], BF16, is_transpose_dest=True)
    q_sb = sb(a_es, "q_sb", [128, D], BF16)
    k_sb = sb(a_es, "k_sb", [128, D], BF16)
    v_sb = sb(a_es, "v_sb", [128, D], BF16)
    pr_g = sb(a_es, "pr_g", [128, 8192], BF16)
    pr_v = sb(a_es, "pr_v", [128, 8192], BF16)
    sc_sb = sb(a_es, "sc_sb", [128, 256], F32)
    mx_sb = sb(a_es, "mx_sb", [128, 16], F32)
    e_sb = sb(a_es, "e_sb", [128, 256], F32)
    sm_sb = sb(a_es, "sm_sb", [128, 16], F32)
    at_sb = sb(a_es, "at_sb", [128, 256], BF16)
    ctx_f = sb(a_es, "ctx_f", [128, D], F32)
    ctx_sb = sb(a_es, "ctx_sb", [128, 2 * D], BF16)

    pbanks = [(ps_a, 0), (ps_a, 512), (ps_b, 0), (ps_b, 512), (ps_c, 0), (ps_d, 0)]
    a_t1a, a_rnd, a_xq, s_tpq, a_xld = [], [], [], [], []
    ev_c, ctx_v, ctx_sp, mm_c, a_scd = [], [], [], [], []
    ev_v_c = {}

    prep_gens = [prep_ffn(w_1, s_w1t, "w1"), prep_ffn(w_2, s_w2t, "w2")]

    def pull_prep(k):
        while k > 0 and prep_gens:
            try:
                next(prep_gens[0])
                k -= 1
            except StopIteration:
                prep_gens.pop(0)

    def emit_xload(S):
        w_t1a = a_t1a[S - 1] if S >= 1 else 0

        @blk.gpsimd
        def _(g, S=S, w_t1a=w_t1a):
            if S == 0:
                g.wait_ge(sQ.h, pre_q_tp)
                g.wait_ge(sV.h, pre_q_v)
            if w_t1a:
                g.wait_ge(sA.h, w_t1a)
            g.dma_start(dap(xa, 0, [[4096, 128], [1024, 4], [1, 1024]]),
                        dap(x_in, S * 4 * 131072,
                            [[1024, 128], [131072, 4], [1, 1024]])).then_inc(sG.h, 16)
        sG.n += 16
        a_xld.append(sG.n)

    emit_xload(0)
    for S in range(NS):
        pb = S % 2
        n_xld = a_xld[S]
        w_rnd2 = a_rnd[S - 2] if S >= 2 else 0

        @blk.scalar
        def _(s, pb=pb, n_xld=n_xld, w_rnd2=w_rnd2):
            s.wait_ge(sG.h, n_xld)
            if w_rnd2:
                s.wait_ge(sV.h, w_rnd2)
            mi = None
            for t in range(4):
                xt = xa[:, t * 1024:(t + 1) * 1024]
                s.activation(sqr[:, :], xt, AF.Copy, accum_out=stcol(pb, 0, t))
                mi = s.activation(sqr[:, :], xt, AF.Square,
                                  accum_out=stcol(pb, 1, t))
            mi.then_inc(sA.h, 1)
        sA.n += 1
        n_st8 = sA.n

        @blk.vector
        def _(v, pb=pb, n_st8=n_st8):
            v.wait_ge(sA.h, n_st8)
            v.tensor_scalar(stc(pb, 0), stc(pb, 0), -1.0 / D, None, ALU.mult)
            v.tensor_scalar(stc(pb, 1), stc(pb, 1), 1.0 / D, None, ALU.mult)
            v.tensor_tensor(stc(pb, 2), stc(pb, 0), stc(pb, 0), ALU.mult)
            pad(v)
            v.tensor_tensor(stc(pb, 1), stc(pb, 1), stc(pb, 2), ALU.subtract)
            pad(v)
            v.tensor_scalar(stc(pb, 1), stc(pb, 1), 1.0 + 1e-6, 1e-11,
                            ALU.mult, ALU.add).then_inc(sV.h, 1)
        sV.n += 1
        n_ty1 = sV.n

        @blk.scalar
        def _(s, pb=pb, n_ty1=n_ty1):
            s.wait_ge(sV.h, n_ty1)
            rsqrt(s, stc(pb, 2), stc(pb, 3), stc(pb, 1))
            s.drain()
            mi = None
            for t in range(4):
                mi = s.activation(tb2[:, t * 1024:(t + 1) * 1024],
                                  xa[:, t * 1024:(t + 1) * 1024], AF.Identity,
                                  bias=stcol(pb, 0, t), scale=1.0)
            mi.then_inc(sA.h, 1)
        sA.n += 1
        a_t1a.append(sA.n)
        n_t1a = sA.n

        @blk.vector
        def _(v, pb=pb, n_t1a=n_t1a):
            v.wait_ge(sA.h, n_t1a)
            v.tensor_reduce(stc(pb, 1),
                            dap(tb2, 0, [[4096, 128], [1024, 4], [1, 1024]]),
                            AX.X, ALU.max, apply_absolute_value=True)
            pad(v)
            v.tensor_tensor(stc(pb, 1), stc(pb, 1), stc(pb, 2), ALU.mult)
            pad(v)
            v.tensor_scalar(stc(pb, 1), stc(pb, 1), EPS, None, ALU.max)
            pad(v)
            v.reciprocal(stc(pb, 3), stc(pb, 1))
            pad(v)
            v.tensor_scalar(stc(pb, 3), stc(pb, 3), 127.0, None, ALU.mult)
            v.tensor_scalar(stc(pb, 1), stc(pb, 1), 1.0 / 127.0, None, ALU.mult)
            v.tensor_tensor(stc(pb, 2), stc(pb, 2), stc(pb, 3), ALU.mult)
            pad(v)
            v.tensor_tensor(dap(tb2, 0, [[4096, 128], [1024, 4], [1, 1024]]),
                            dap(tb2, 0, [[4096, 128], [1024, 4], [1, 1024]]),
                            stbc(pb, 2), ALU.mult)
            rnd_inplace(v).then_inc(sV.h, 1)
        sV.n += 1
        a_rnd.append(sV.n)
        n_rnd = sV.n
        w_tpq = s_tpq[S - 2] if S >= 2 else 0

        @blk.scalar
        def _(s, pb=pb, n_rnd=n_rnd, w_tpq=w_tpq):
            s.wait_ge(sV.h, n_rnd)
            if w_tpq:
                s.wait_ge(sQ.h, w_tpq)
            mi = None
            for t in range(4):
                mi = s.activation(xqs[:, pb * 4096 + t * 1024:pb * 4096 + (t + 1) * 1024],
                                  tb1[:, t * 1024:(t + 1) * 1024], AF.Copy,
                                  bias=0.0, scale=stcol(pb, 1, t))
            mi.then_inc(sA.h, 1)
        sA.n += 1
        a_xq.append(sA.n)
        n_xq = sA.n

        for t4 in range(4):
            tt = S * 4 + t4
            tpb = tt % 2
            w_mm = mm_c[tt - 2] if tt >= 2 else 0

            @blk.sync
            def _(s, t4=t4, pb=pb, tpb=tpb, n_xq=n_xq, w_mm=w_mm):
                s.wait_ge(sA.h, n_xq)
                if w_mm:
                    s.wait_ge(sM.h, w_mm)
                for j in range(NC8):
                    s.dma_start(xqTt[:, tpb * D + j * 128:tpb * D + (j + 1) * 128],
                                xqs[:, pb * 4096 + t4 * D + j * 128:
                                    pb * 4096 + t4 * D + (j + 1) * 128],
                                transpose=True).then_inc(sQ.h, 16)
            sQ.n += 16 * NC8
            n_tp = sQ.n
            w_ev = ev_v_c.get(tt - 1, 0) if tt >= 1 else 0

            @blk.tensor
            def _(t, tt=tt, tpb=tpb, n_tp=n_tp, w_ev=w_ev):
                t.wait_ge(sQ.h, n_tp)
                if tt == 0:
                    t.wait_ge(sQ.h, n_wprep)
                if w_ev:
                    t.wait_ge(sA.h, w_ev)
                mi = None
                for j in range(NC8):
                    lhs = xqTt[:, tpb * D + j * 128:tpb * D + (j + 1) * 128]
                    for pi, wt in enumerate((wt_q, wt_k, wt_v)):
                        for n in range(2):
                            pb_, off = pbanks[pi * 2 + n]
                            mi = t.matmul(pb_[:, off:off + 512], lhs,
                                          wt[:, j * D + n * 512:j * D + (n + 1) * 512],
                                          start=(j == 0), stop=(j == NC8 - 1))
                mi.then_inc(sM.h, 1)
            sM.n += 1
            mm_c.append(sM.n)
            n_mm = sM.n
            w_qk = a_scd[tt - 1] if tt >= 1 else 0
            w_mix = ctx_v[tt - 1] if tt >= 1 else 0

            @blk.scalar
            def _(s, n_mm=n_mm, w_qk=w_qk):
                s.wait_ge(sM.h, n_mm)
                if w_qk:
                    s.wait_ge(sV.h, w_qk)
                mi = None
                for pi, dst in enumerate((q_sb, k_sb)):
                    for n in range(2):
                        pb_, off = pbanks[pi * 2 + n]
                        mi = s.activation(dst[:, n * 512:(n + 1) * 512],
                                          pb_[:, off:off + 512],
                                          AF.Copy, bias=0.0, scale=gam[:, pi:pi + 1])
                mi.then_inc(sA.h, 1)
            sA.n += 1
            ev_c.append(sA.n)
            n_evc = sA.n

            @blk.scalar
            def _(s, w_mix=w_mix):
                if w_mix:
                    s.wait_ge(sV.h, w_mix)
                mi = None
                for n in range(2):
                    pb_, off = pbanks[4 + n]
                    mi = s.activation(v_sb[:, n * 512:(n + 1) * 512],
                                      pb_[:, off:off + 512],
                                      AF.Copy, bias=0.0, scale=gam[:, 2:3])
                mi.then_inc(sA.h, 1)
            sA.n += 1
            ev_v_c[tt] = sA.n

            @blk.gpsimd
            def _(g, n_evc=n_evc):
                g.wait_ge(sA.h, n_evc)
                g.tensor_tensor(
                    dap(pr_g, 0, [[8192, 128], [1024, 8], [64, 16], [1, 64]]),
                    dap(q_sb, 0, [[D, 128], [64, 8], [0, 16], [1, 64]]),
                    dap(k_sb, 0, [[D, 128], [0, 8], [64, 16], [1, 64]]),
                    ALU.mult).then_inc(sP.h, 1)
            sP.n += 1
            g_sc = sP.n

            @blk.vector
            def _(v, n_evc=n_evc, g_sc=g_sc):
                v.wait_ge(sA.h, n_evc)
                v.tensor_tensor(
                    dap(pr_v, 0, [[8192, 128], [1024, 8], [64, 16], [1, 64]]),
                    dap(q_sb, 512, [[D, 128], [64, 8], [0, 16], [1, 64]]),
                    dap(k_sb, 0, [[D, 128], [0, 8], [64, 16], [1, 64]]),
                    ALU.mult)
                v.tensor_reduce(dap(sc_sb, 128, [[256, 128], [16, 8], [1, 16]]),
                                dap(pr_v, 0,
                                    [[8192, 128], [1024, 8], [64, 16], [1, 64]]),
                                AX.X, ALU.add)
                v.wait_ge(sP.h, g_sc)
                v.tensor_reduce(dap(sc_sb, 0, [[256, 128], [16, 8], [1, 16]]),
                                dap(pr_g, 0,
                                    [[8192, 128], [1024, 8], [64, 16], [1, 64]]),
                                AX.X, ALU.add)
                pad(v)
                v.tensor_reduce(dap(mx_sb, 0, [[16, 128], [1, 16]]),
                                dap(sc_sb, 0, [[256, 128], [16, 16], [1, 16]]),
                                AX.X, ALU.max)
                pad(v)
                v.tensor_tensor(sc_sb[:, :], sc_sb[:, :],
                                dap(mx_sb, 0, [[16, 128], [1, 16], [0, 16]]),
                                ALU.subtract).then_inc(sV.h, 1)
            sV.n += 1
            a_scd.append(sV.n)
            n_scd = sV.n

            @blk.scalar
            def _(s, n_scd=n_scd):
                s.wait_ge(sV.h, n_scd)
                s.activation(e_sb[:, :], sc_sb[:, :], AF.Exp, bias=0.0,
                             scale=0.125).then_inc(sA.h, 1)
            sA.n += 1
            n_exp = sA.n

            @blk.vector
            def _(v, n_exp=n_exp):
                v.wait_ge(sA.h, n_exp)
                v.tensor_reduce(dap(sm_sb, 0, [[16, 128], [1, 16]]),
                                dap(e_sb, 0, [[256, 128], [16, 16], [1, 16]]),
                                AX.X, ALU.add)
                pad(v)
                v.reciprocal(sm_sb[:, :], sm_sb[:, :])
                pad(v)
                v.tensor_tensor(at_sb[:, :], e_sb[:, :],
                                dap(sm_sb, 0, [[16, 128], [1, 16], [0, 16]]),
                                ALU.mult).then_inc(sV.h, 1)
            sV.n += 1
            n_at = sV.n

            @blk.vector
            def _(v, n_evv=ev_v_c[tt]):
                v.wait_ge(sA.h, n_evv)
                pad(v)
                for hh in range(2):
                    v.tensor_tensor(
                        dap(pr_v, 0, [[8192, 128], [1024, 8], [16, 64], [1, 16]]),
                        dap(at_sb, hh * 128, [[256, 128], [16, 8], [0, 64], [1, 16]]),
                        dap(v_sb, 0, [[D, 128], [0, 8], [1, 64], [64, 16]]),
                        ALU.mult)
                    v.tensor_reduce(dap(ctx_f, hh * 512, [[D, 128], [64, 8], [1, 64]]),
                                    dap(pr_v, 0,
                                        [[8192, 128], [1024, 8], [16, 64], [1, 16]]),
                                    AX.X, ALU.add).then_inc(sV.h, 1) if hh == 1 else                     v.tensor_reduce(dap(ctx_f, hh * 512, [[D, 128], [64, 8], [1, 64]]),
                                    dap(pr_v, 0,
                                        [[8192, 128], [1024, 8], [16, 64], [1, 16]]),
                                    AX.X, ALU.add)
            sV.n += 1
            ctx_v.append(sV.n)
            n_cv = sV.n
            w_ctx = ctx_sp[tt - 2] if tt >= 2 else 0

            @blk.scalar
            def _(s, tpb=tpb, n_cv=n_cv, w_ctx=w_ctx):
                s.wait_ge(sV.h, n_cv)
                if w_ctx:
                    s.wait_ge(sG.h, w_ctx)
                s.activation(ctx_sb[:, tpb * D:(tpb + 1) * D], ctx_f[:, :],
                             AF.Copy).then_inc(sA.h, 1)
            sA.n += 1
            n_cc = sA.n

            @blk.gpsimd
            def _(g, tt=tt, tpb=tpb, n_cc=n_cc):
                g.wait_ge(sA.h, n_cc)
                g.dma_start(dap(s_ctx, tt * 128 * 64,
                                [[64, 128], [262144, 16], [1, 64]]),
                            dap(ctx_sb, tpb * D,
                                [[2 * D, 128], [64, 16], [1, 64]])).then_inc(sG.h, 16)
            sG.n += 16
            ctx_sp.append(sG.n)
        s_tpq.append(sQ.n)
        if S + 1 < NS:
            emit_xload(S + 1)
        pull_prep(6)

    while prep_gens:
        pull_prep(8)

    n_abc_mm = sM.n
    n_ctx_all = ctx_sp[-1]
    prep_attn(w_o, wt_q, 3, tguard=n_abc_mm)
    n_prep_q = sQ.n
    n_prep_g = sG.n
    a_es.close()
    bk_es.close()
    w_es.close()

    xln2T = sb(g_es, "xln2T", [128, NC8 * T], BF16, is_transpose_dest=True,
               side="right")
    d_es = ExitStack()
    ctd = sb(d_es, "ctd", [128, 2 * 4096], BF16)
    xd = sb(d_es, "xd", [128, 4096], F32)
    sqd = sb(d_es, "sqd", [128, 1024], F32)
    cqs = sb(d_es, "cqs", [128, 4096], BF16)
    cqT = sb(d_es, "cqT", [128, 2 * D], BF16, is_transpose_dest=True)
    aod = sb(d_es, "aod", [128, 4096], F32)
    x1d = sb(d_es, "x1d", [128, 4096], F32)
    xl2 = sb(d_es, "xl2", [128, 4096], BF16)

    d_rnd, d_x1, d_fin, d_xsp, d_ld = [], [], [], [], []
    d_tpq_s, d_tps_s, d_ltq_s, d_lts_s = [], [], [], []
    d_mm, d_aev = [], []

    def emit_dload(S):
        pb = S % 2
        ofs = pb * 4096
        w_ctd = d_rnd[S - 2] if S >= 2 else 0
        w_xd = d_x1[S - 1] if S >= 1 else 0

        @blk.gpsimd
        def _(g, S=S, ofs=ofs, w_ctd=w_ctd, w_xd=w_xd):
            if S == 0:
                g.wait_ge(sQ.h, n_ctx_all) if False else None
                g.wait_ge(sG.h, n_ctx_all)
                g.wait_ge(sQ.h, n_prep_q)
                g.wait_ge(sG.h, n_prep_g)
            if w_ctd:
                g.wait_ge(sV.h, w_ctd)
            g.dma_start(dap(ctd, ofs, [[2 * 4096, 128], [1024, 4], [1, 1024]]),
                        dap(s_ctx, S * 4 * 131072,
                            [[1024, 128], [131072, 4], [1, 1024]])).then_inc(sG.h, 16)
            if w_xd:
                g.wait_ge(sV.h, w_xd)
            g.dma_start(dap(xd, 0, [[4096, 128], [1024, 4], [1, 1024]]),
                        dap(x_in, S * 4 * 131072,
                            [[1024, 128], [131072, 4], [1, 1024]])).then_inc(sG.h, 16)
        sG.n += 32
        d_ld.append(sG.n)

    emit_dload(0)
    for S in range(NS):
        pb = S % 2
        ofs = pb * 4096
        n_ld = d_ld[S]
        w_st = d_fin[S - 2] if S >= 2 else 0

        @blk.scalar
        def _(s, pb=pb, ofs=ofs, n_ld=n_ld, w_st=w_st):
            s.wait_ge(sG.h, n_ld)
            if w_st:
                s.wait_ge(sV.h, w_st)
            mi = None
            for t in range(4):
                mi = s.activation(sqd[:, :], ctd[:, ofs + t * 1024:ofs + (t + 1) * 1024],
                                  AF.Square, accum_out=stcol(pb, 0, t))
            mi.then_inc(sA.h, 1)
        sA.n += 1
        n_st4 = sA.n

        @blk.vector
        def _(v, pb=pb, n_st4=n_st4):
            v.wait_ge(sA.h, n_st4)
            v.tensor_scalar(stc(pb, 0), stc(pb, 0), 1.0 / D, 1e-6,
                            ALU.mult, ALU.add).then_inc(sV.h, 1)
        sV.n += 1
        n_ty1 = sV.n

        @blk.scalar
        def _(s, pb=pb, n_ty1=n_ty1):
            s.wait_ge(sV.h, n_ty1)
            rsqrt(s, stc(pb, 2), stc(pb, 3), stc(pb, 0)).then_inc(sA.h, 1)
        sA.n += 1
        n_rr = sA.n

        @blk.vector
        def _(v, pb=pb, ofs=ofs, n_rr=n_rr):
            v.wait_ge(sA.h, n_rr)
            v.tensor_reduce(stc(pb, 1),
                            dap(ctd, ofs, [[2 * 4096, 128], [1024, 4], [1, 1024]]),
                            AX.X, ALU.max, apply_absolute_value=True)
            pad(v)
            v.tensor_tensor(stc(pb, 1), stc(pb, 1), stc(pb, 2), ALU.mult)
            pad(v)
            v.tensor_scalar(stc(pb, 1), stc(pb, 1), EPS, None, ALU.max)
            pad(v)
            v.reciprocal(stc(pb, 3), stc(pb, 1))
            pad(v)
            v.tensor_scalar(stc(pb, 3), stc(pb, 3), 127.0, None, ALU.mult)
            v.tensor_scalar(stc(pb, 1), stc(pb, 1), 1.0 / 127.0, None, ALU.mult)
            v.tensor_tensor(stc(pb, 2), stc(pb, 2), stc(pb, 3), ALU.mult)
            pad(v)
            v.tensor_tensor(dap(tb2, 0, [[4096, 128], [1024, 4], [1, 1024]]),
                            dap(ctd, ofs, [[2 * 4096, 128], [1024, 4], [1, 1024]]),
                            stbc(pb, 2), ALU.mult)
            rnd_inplace(v).then_inc(sV.h, 1)
        sV.n += 1
        d_rnd.append(sV.n)
        n_rnd = sV.n
        w_tpq = d_tpq_s[S - 1] if S >= 1 else 0
        w_tps = d_tps_s[S - 1] if S >= 1 else 0

        @blk.scalar
        def _(s, pb=pb, n_rnd=n_rnd, w_tpq=w_tpq, w_tps=w_tps):
            s.wait_ge(sV.h, n_rnd)
            if w_tpq:
                s.wait_ge(sQ.h, w_tpq)
                s.wait_ge(sS.h, w_tps)
            mi = None
            for t in range(4):
                mi = s.activation(cqs[:, t * 1024:(t + 1) * 1024],
                                  tb1[:, t * 1024:(t + 1) * 1024], AF.Copy,
                                  bias=0.0, scale=stcol(pb, 1, t))
            mi.then_inc(sA.h, 1)
        sA.n += 1
        n_cq = sA.n

        for t4 in range(4):
            tt = S * 4 + t4
            tpb = tt % 2
            w_mmo = d_mm[tt - 2] if tt >= 2 else 0

            def tpd(s, t4=t4, tpb=tpb, n_cq=n_cq, w_mmo=w_mmo, q=(t4 % 2 == 0)):
                s.wait_ge(sA.h, n_cq)
                if w_mmo:
                    s.wait_ge(sM.h, w_mmo)
                for j in range(NC8):
                    s.dma_start(cqT[:, tpb * D + j * 128:tpb * D + (j + 1) * 128],
                                cqs[:, t4 * D + j * 128:t4 * D + (j + 1) * 128],
                                transpose=True).then_inc((sQ if q else sS).h, 16)
            if t4 % 2 == 0:
                blk.sync(tpd)
                sQ.n += 16 * NC8
            else:
                blk.scalar(tpd)
                sS.n += 16 * NC8
            n_tpq, n_tps = sQ.n, sS.n
            w_aev = d_aev[tt - 1] if tt >= 1 else 0

            @blk.tensor
            def _(t, tpb=tpb, n_tpq=n_tpq, n_tps=n_tps, w_aev=w_aev):
                t.wait_ge(sQ.h, n_tpq)
                t.wait_ge(sS.h, n_tps)
                if w_aev:
                    t.wait_ge(sV.h, w_aev)
                mi = None
                for j in range(NC8):
                    lhs = cqT[:, tpb * D + j * 128:tpb * D + (j + 1) * 128]
                    mi = t.matmul(ps_c[:, :], lhs, wt_q[:, j * D + 0:j * D + 512],
                                  start=(j == 0), stop=(j == NC8 - 1))
                    mi = t.matmul(ps_d[:, :], lhs, wt_q[:, j * D + 512:j * D + 1024],
                                  start=(j == 0), stop=(j == NC8 - 1))
                mi.then_inc(sM.h, 1)
            sM.n += 1
            d_mm.append(sM.n)
            n_omm = sM.n

            @blk.vector
            def _(v, t4=t4, n_omm=n_omm):
                v.wait_ge(sM.h, n_omm)
                v.tensor_scalar(aod[:, t4 * D:t4 * D + 512], ps_c[:, :],
                                gam[:, 3:4], None, ALU.mult, ALU.add,
                                accum_out=st[:, 48 + t4:49 + t4])
                v.tensor_scalar(aod[:, t4 * D + 512:(t4 + 1) * D], ps_d[:, :],
                                gam[:, 3:4], None, ALU.mult, ALU.add,
                                accum_out=st[:, 52 + t4:53 + t4]).then_inc(sV.h, 1)
            sV.n += 1
            d_aev.append(sV.n)
        n_aev4 = sV.n

        @blk.scalar
        def _(s, pb=pb, n_aev4=n_aev4):
            s.wait_ge(sV.h, n_aev4)
            mi = None
            for t in range(4):
                mi = s.activation(sqd[:, :], aod[:, t * 1024:(t + 1) * 1024],
                                  AF.Square, accum_out=stcol(pb, 1, t))
            mi.then_inc(sA.h, 1)
        sA.n += 1
        n_sqa = sA.n

        @blk.vector
        def _(v, pb=pb, n_sqa=n_sqa):
            v.wait_ge(sA.h, n_sqa)
            v.tensor_tensor(stc(pb, 0), st[:, 48:52], st[:, 52:56], ALU.add)
            pad(v)
            v.tensor_scalar(stc(pb, 0), stc(pb, 0), -1.0 / D, None, ALU.mult)
            v.tensor_scalar(stc(pb, 1), stc(pb, 1), 1.0 / D, None, ALU.mult)
            v.tensor_tensor(stc(pb, 2), stc(pb, 0), stc(pb, 0), ALU.mult)
            pad(v)
            v.tensor_tensor(stc(pb, 1), stc(pb, 1), stc(pb, 2), ALU.subtract)
            pad(v)
            v.tensor_scalar(stc(pb, 1), stc(pb, 1), EPS, None,
                            ALU.add).then_inc(sV.h, 1)
        sV.n += 1
        n_ty2 = sV.n

        @blk.scalar
        def _(s, pb=pb, n_ty2=n_ty2):
            s.wait_ge(sV.h, n_ty2)
            rsqrt(s, stc(pb, 2), stc(pb, 3), stc(pb, 1))
            s.drain()
            mi = None
            for t in range(4):
                mi = s.activation(tb2[:, t * 1024:(t + 1) * 1024],
                                  aod[:, t * 1024:(t + 1) * 1024], AF.Identity,
                                  bias=stcol(pb, 0, t), scale=1.0)
            mi.then_inc(sA.h, 1)
        sA.n += 1
        n_t1b = sA.n
        w_xsp = d_xsp[S - 1] if S >= 1 else 0

        @blk.vector
        def _(v, pb=pb, n_t1b=n_t1b, w_xsp=w_xsp):
            v.wait_ge(sA.h, n_t1b)
            if w_xsp:
                v.wait_ge(sG.h, w_xsp)
            v.tensor_tensor(dap(tb2, 0, [[4096, 128], [1024, 4], [1, 1024]]),
                            dap(tb2, 0, [[4096, 128], [1024, 4], [1, 1024]]),
                            stbc(pb, 2), ALU.mult)
            v.tensor_tensor(x1d[:, :], tb2[:, :], xd[:, :],
                            ALU.add).then_inc(sV.h, 1)
        sV.n += 1
        d_x1.append(sV.n)
        n_x1 = sV.n
        if S + 1 < NS:
            emit_dload(S + 1)

        @blk.gpsimd
        def _(g, S=S, n_x1=n_x1):
            g.wait_ge(sV.h, n_x1)
            g.dma_start(dap(s_x1, S * 4 * 131072,
                            [[1024, 128], [131072, 4], [1, 1024]]),
                        dap(x1d, 0, [[4096, 128], [1024, 4], [1, 1024]])
                        ).then_inc(sG.h, 16)
        sG.n += 16
        d_xsp.append(sG.n)

        @blk.scalar
        def _(s, pb=pb, n_x1=n_x1):
            s.wait_ge(sV.h, n_x1)
            mi = None
            for t in range(4):
                xt = x1d[:, t * 1024:(t + 1) * 1024]
                s.activation(sqd[:, :], xt, AF.Copy, accum_out=stcol(pb, 0, t))
                mi = s.activation(sqd[:, :], xt, AF.Square,
                                  accum_out=stcol(pb, 1, t))
            mi.then_inc(sA.h, 1)
        sA.n += 1
        n_stx = sA.n

        @blk.vector
        def _(v, pb=pb, n_stx=n_stx):
            v.wait_ge(sA.h, n_stx)
            v.tensor_scalar(stc(pb, 0), stc(pb, 0), -1.0 / D, None, ALU.mult)
            v.tensor_scalar(stc(pb, 1), stc(pb, 1), 1.0 / D, None, ALU.mult)
            v.tensor_tensor(stc(pb, 2), stc(pb, 0), stc(pb, 0), ALU.mult)
            pad(v)
            v.tensor_tensor(stc(pb, 1), stc(pb, 1), stc(pb, 2), ALU.subtract)
            pad(v)
            v.tensor_scalar(stc(pb, 1), stc(pb, 1), EPS, None,
                            ALU.add).then_inc(sV.h, 1)
        sV.n += 1
        n_ty3 = sV.n

        @blk.scalar
        def _(s, pb=pb, n_ty3=n_ty3):
            s.wait_ge(sV.h, n_ty3)
            rsqrt(s, stc(pb, 2), stc(pb, 3), stc(pb, 1))
            s.drain()
            mi = None
            for t in range(4):
                mi = s.activation(tb2[:, t * 1024:(t + 1) * 1024],
                                  x1d[:, t * 1024:(t + 1) * 1024], AF.Identity,
                                  bias=stcol(pb, 0, t), scale=1.0)
            mi.then_inc(sA.h, 1)
        sA.n += 1
        n_t1c = sA.n
        w_lt = d_ltq_s[S - 1] if S >= 1 else 0
        w_lts = d_lts_s[S - 1] if S >= 1 else 0

        @blk.vector
        def _(v, pb=pb, n_t1c=n_t1c, w_lt=w_lt, w_lts=w_lts):
            v.wait_ge(sA.h, n_t1c)
            if w_lt:
                v.wait_ge(sQ.h, w_lt)
                v.wait_ge(sS.h, w_lts)
            v.tensor_tensor(dap(xl2, 0, [[4096, 128], [1024, 4], [1, 1024]]),
                            dap(tb2, 0, [[4096, 128], [1024, 4], [1, 1024]]),
                            stbc(pb, 2), ALU.mult).then_inc(sV.h, 1)
        sV.n += 1
        d_fin.append(sV.n)
        n_fin = sV.n

        for t4 in range(4):
            tt = S * 4 + t4

            @blk.sync
            def _(s, t4=t4, tt=tt, n_fin=n_fin):
                s.wait_ge(sV.h, n_fin)
                for j in range(NC8):
                    s.dma_start(xln2T[:, j * T + tt * 128:j * T + (tt + 1) * 128],
                                xl2[:, t4 * D + j * 128:t4 * D + (j + 1) * 128],
                                transpose=True).then_inc(sQ.h, 16)
            sQ.n += 16 * NC8
        d_ltq_s.append(sQ.n)
        d_lts_s.append(sS.n)
        d_tpq_s.append(sQ.n)
        d_tps_s.append(sS.n)

    n_dq_end = sQ.n
    n_dg_end = sG.n
    n_dv_end = sV.n
    n_da_end = sA.n
    n_ds_end = sS.n
    d_es.close()
    wt_es.close()
    r_es.close()

    e_es = ExitStack()
    w1s = sb(e_es, "w1s", [128, 2 * 1024], BF16)
    w2s = sb(e_es, "w2s", [128, 2 * 4096], BF16)
    hT = sb(e_es, "hT", [128, 32 * 1024], BF16)
    yst = sb(e_es, "yst", [128, 2 * 1024], BF16)
    ytok = sb(e_es, "ytok", [128, 8 * 1024], BF16, is_transpose_dest=True)
    x1f = sb(e_es, "x1f", [128, 4096], F32)
    obuf = sb(e_es, "obuf", [128, 4096], F32)

    out_st, fin_all = [], []
    ffn2_mm_prev = 0
    fin_prev_block = 0
    f1_mm, f2_mm = [], []
    for Tb in range(4):
        relu_cnt = []
        for fc in range(32):
            gi = Tb * 32 + fc
            w_slab = f1_mm[(gi - 2) * 2 + 1] if gi >= 2 else 0

            @blk.gpsimd
            def _(g, Tb=Tb, fc=fc, w_slab=w_slab):
                if Tb == 0 and fc == 0:
                    g.wait_ge(sV.h, n_dv_end)
                    g.wait_ge(sA.h, n_da_end)
                    g.wait_ge(sQ.h, n_dq_end)
                    g.wait_ge(sG.h, n_dg_end)
                    g.wait_ge(sS.h, n_ds_end)
                if w_slab:
                    g.wait_ge(sM.h, w_slab)
                g.dma_start(dap(w1s, (fc % 2) * 1024, [[2048, 128], [1, 1024]]),
                            dap(s_w1t, fc * 131072, [[128, 128], [16384, 8], [1, 128]])
                            ).then_inc(sG.h, 16)
            sG.n += 16
            n_sl = sG.n
            for n in range(2):
                u = fc * 2 + n
                bank = ps_c if u % 2 == 0 else ps_d
                w_relu = relu_cnt[u - 2] if u >= 2 else 0

                @blk.tensor
                def _(t, Tb=Tb, fc=fc, n=n, bank=bank, n_sl=n_sl,
                      w_relu=w_relu, w_prev2=ffn2_mm_prev):
                    t.wait_ge(sG.h, n_sl)
                    if fc == 0 and n == 0:
                        t.wait_ge(sQ.h, n_dq_end)
                        if w_prev2:
                            t.wait_ge(sM.h, w_prev2)
                    if w_relu:
                        t.wait_ge(sA.h, w_relu)
                    mi = None
                    for dc in range(NC8):
                        lhs = w1s[:, (fc % 2) * 1024 + dc * 128:
                                  (fc % 2) * 1024 + (dc + 1) * 128]
                        mi = t.matmul(bank[:, :], lhs,
                                      xln2T[:, dc * T + Tb * 1024 + n * 512:
                                            dc * T + Tb * 1024 + (n + 1) * 512],
                                      start=(dc == 0), stop=(dc == NC8 - 1))
                    mi.then_inc(sM.h, 1)
                sM.n += 1
                f1_mm.append(sM.n)
                n_f1 = sM.n

                @blk.scalar
                def _(s, fc=fc, n=n, bank=bank, n_f1=n_f1):
                    s.wait_ge(sM.h, n_f1)
                    s.activation(hT[:, fc * 1024 + n * 512:fc * 1024 + (n + 1) * 512],
                                 bank[:, :], AF.Relu, bias=b1c[:, fc:fc + 1],
                                 scale=1.0).then_inc(sA.h, 1)
                sA.n += 1
                relu_cnt.append(sA.n)
        n_relu_all = sA.n

        yev, ytp = [], []
        for dc in range(NC8):
            gj = Tb * NC8 + dc
            w_slab2 = f2_mm[(gj - 2) * 2 + 1] if gj >= 2 else 0

            @blk.gpsimd
            def _(g, dc=dc, w_slab2=w_slab2):
                if w_slab2:
                    g.wait_ge(sM.h, w_slab2)
                g.dma_start(dap(w2s, (dc % 2) * 4096, [[8192, 128], [1, 4096]]),
                            dap(s_w2t, dc * 524288, [[128, 128], [16384, 32], [1, 128]])
                            ).then_inc(sG.h, 16)
            sG.n += 16
            n_sl = sG.n
            for n in range(2):
                u2 = dc * 2 + n
                boff = 0 if u2 % 2 == 0 else 512
                w_yev = yev[u2 - 2] if u2 >= 2 else 0
                w_ytp = ytp[dc - 2] if (dc >= 2 and n == 0) else 0

                @blk.tensor
                def _(t, dc=dc, n=n, boff=boff, n_sl=n_sl,
                      n_relu_all=n_relu_all, w_yev=w_yev):
                    t.wait_ge(sG.h, n_sl)
                    if dc == 0 and n == 0:
                        t.wait_ge(sA.h, n_relu_all)
                    if w_yev:
                        t.wait_ge(sV.h, w_yev)
                    mi = None
                    for fc in range(32):
                        lhs = w2s[:, (dc % 2) * 4096 + fc * 128:
                                  (dc % 2) * 4096 + (fc + 1) * 128]
                        mi = t.matmul(ps_a[:, boff:boff + 512], lhs,
                                      hT[:, fc * 1024 + n * 512:fc * 1024 + (n + 1) * 512],
                                      start=(fc == 0), stop=(fc == 31))
                    mi.then_inc(sM.h, 1)
                sM.n += 1
                f2_mm.append(sM.n)
                n_f2 = sM.n

                @blk.vector
                def _(v, dc=dc, n=n, boff=boff, n_f2=n_f2, w_ytp=w_ytp):
                    v.wait_ge(sM.h, n_f2)
                    if w_ytp:
                        v.wait_ge(sQ.h, w_ytp[0])
                        v.wait_ge(sS.h, w_ytp[1])
                    v.tensor_scalar(yst[:, (dc % 2) * 1024 + n * 512:
                                        (dc % 2) * 1024 + (n + 1) * 512],
                                    ps_a[:, boff:boff + 512],
                                    b2c[:, dc:dc + 1], None,
                                    ALU.add).then_inc(sV.h, 1)
                sV.n += 1
                yev.append(sV.n)
            n_ye = sV.n

            def ytpb(s, dc=dc, n_ye=n_ye, fin_prev_block=fin_prev_block,
                     q=(dc % 2 == 0)):
                s.wait_ge(sV.h, n_ye)
                if fin_prev_block:
                    s.wait_ge(sV.h, fin_prev_block)
                for ts_ in range(8):
                    s.dma_start(ytok[:, ts_ * 1024 + dc * 128:ts_ * 1024 + (dc + 1) * 128],
                                yst[:, (dc % 2) * 1024 + ts_ * 128:
                                    (dc % 2) * 1024 + (ts_ + 1) * 128],
                                transpose=True).then_inc((sQ if q else sS).h, 16)
            if dc % 2 == 0:
                blk.sync(ytpb)
                sQ.n += 128
            else:
                blk.scalar(ytpb)
                sS.n += 128
            ytp.append((sQ.n, sS.n))
        n_ytp_all = sQ.n
        n_ytp_all_s = sS.n

        for gq in range(2):
            gg = Tb * 2 + gq
            pb = gg % 2
            w_xlf = fin_all[gg - 1] if gg >= 1 else 0

            @blk.sync
            def _(s, Tb=Tb, gq=gq, w_xlf=w_xlf):
                if w_xlf:
                    s.wait_ge(sV.h, w_xlf)
                s.dma_start(dap(x1f, 0, [[4096, 128], [1024, 4], [1, 1024]]),
                            dap(s_x1, (Tb * 8 + gq * 4) * 131072,
                                [[1024, 128], [131072, 4], [1, 1024]])
                            ).then_inc(sQ.h, 16)
            sQ.n += 16
            n_xl = sQ.n
            w_ost = out_st[gg - 1] if gg >= 1 else 0
            w_st = fin_all[gg - 2] if gg >= 2 else 0

            @blk.scalar
            def _(s, pb=pb, gq=gq, n_ytp_all=n_ytp_all, n_ytp_all_s=n_ytp_all_s,
                  w_ost=w_ost, w_st=w_st):
                s.wait_ge(sQ.h, n_ytp_all)
                s.wait_ge(sS.h, n_ytp_all_s)
                if w_ost:
                    s.wait_ge(sQ.h, w_ost)
                if w_st:
                    s.wait_ge(sV.h, w_st)
                mi = None
                for t in range(4):
                    yt = ytok[:, gq * 4096 + t * 1024:gq * 4096 + (t + 1) * 1024]
                    ot = obuf[:, t * 1024:(t + 1) * 1024]
                    s.activation(ot, yt, AF.Copy, accum_out=stcol(pb, 0, t))
                    mi = s.activation(ot, yt, AF.Square, accum_out=stcol(pb, 1, t))
                mi.then_inc(sA.h, 1)
            sA.n += 1
            n_st8 = sA.n

            @blk.vector
            def _(v, pb=pb, n_st8=n_st8):
                v.wait_ge(sA.h, n_st8)
                v.tensor_scalar(stc(pb, 0), stc(pb, 0), -1.0 / D, None, ALU.mult)
                v.tensor_scalar(stc(pb, 1), stc(pb, 1), 1.0 / D, None, ALU.mult)
                v.tensor_tensor(stc(pb, 2), stc(pb, 0), stc(pb, 0), ALU.mult)
                pad(v)
                v.tensor_tensor(stc(pb, 1), stc(pb, 1), stc(pb, 2), ALU.subtract)
                pad(v)
                v.tensor_scalar(stc(pb, 1), stc(pb, 1), EPS, None,
                                ALU.add).then_inc(sV.h, 1)
            sV.n += 1
            n_ty = sV.n

            @blk.scalar
            def _(s, pb=pb, gq=gq, n_ty=n_ty):
                s.wait_ge(sV.h, n_ty)
                rsqrt(s, stc(pb, 2), stc(pb, 3), stc(pb, 1))
                s.drain()
                mi = None
                for t in range(4):
                    mi = s.activation(obuf[:, t * 1024:(t + 1) * 1024],
                                      ytok[:, gq * 4096 + t * 1024:
                                           gq * 4096 + (t + 1) * 1024],
                                      AF.Identity, bias=stcol(pb, 0, t), scale=1.0)
                mi.then_inc(sA.h, 1)
            sA.n += 1
            n_t1 = sA.n

            @blk.vector
            def _(v, pb=pb, n_t1=n_t1, n_xl=n_xl):
                v.wait_ge(sA.h, n_t1)
                v.wait_ge(sQ.h, n_xl)
                v.tensor_tensor(dap(obuf, 0, [[4096, 128], [1024, 4], [1, 1024]]),
                                dap(obuf, 0, [[4096, 128], [1024, 4], [1, 1024]]),
                                stbc(pb, 2), ALU.mult)
                v.tensor_tensor(obuf[:, :], obuf[:, :], x1f[:, :],
                                ALU.add).then_inc(sV.h, 1)
            sV.n += 1
            fin_all.append(sV.n)
            n_ob = sV.n

            @blk.sync
            def _(s, Tb=Tb, gq=gq, n_ob=n_ob):
                s.wait_ge(sV.h, n_ob)
                s.dma_start(dap(out, (Tb * 8 + gq * 4) * 131072,
                                [[1024, 128], [131072, 4], [1, 1024]]),
                            obuf[:, :]).then_inc(sQ.h, 16)
            sQ.n += 16
            out_st.append(sQ.n)
        fin_prev_block = fin_all[-1]
        ffn2_mm_prev = sM.n

    n_final = sQ.n

    @blk.sync
    def _(s):
        s.wait_ge(sQ.h, n_final)
        s.wait_ge(sG.h, sG.n)
        s.wait_ge(sS.h, sS.n)

    e_es.close()
    p_es.close()
    g_es.close()
    return nc


_NC_CACHE = None


def kernel(L0_attn_x, L0_attn_wq, L0_attn_wk, L0_attn_wv, L0_attn_wo,
           L0_ffn_W1, L0_ffn_b1, L0_ffn_W2, L0_ffn_b2):
    global _NC_CACHE
    x = np.ascontiguousarray(np.asarray(L0_attn_x, dtype=np.float32))
    wq = np.ascontiguousarray(np.asarray(L0_attn_wq, dtype=np.float32))
    wk = np.ascontiguousarray(np.asarray(L0_attn_wk, dtype=np.float32))
    wv = np.ascontiguousarray(np.asarray(L0_attn_wv, dtype=np.float32))
    wo = np.ascontiguousarray(np.asarray(L0_attn_wo, dtype=np.float32))
    w1 = np.ascontiguousarray(np.asarray(L0_ffn_W1, dtype=np.float32))
    b1 = np.ascontiguousarray(np.asarray(L0_ffn_b1, dtype=np.float32)).reshape(32, 128)
    w2 = np.ascontiguousarray(np.asarray(L0_ffn_W2, dtype=np.float32))
    b2 = np.ascontiguousarray(np.asarray(L0_ffn_b2, dtype=np.float32)).reshape(8, 128)

    if _NC_CACHE is None:
        _NC_CACHE = build()
    nc = _NC_CACHE
    in_maps = [{"x": np.ascontiguousarray(x[i]), "wq": wq, "wk": wk, "wv": wv,
                "wo": wo, "w1": w1, "b1": b1, "w2": w2, "b2": b2}
               for i in range(8)]
    res = run_bass_kernel_spmd(nc, in_maps, core_ids=list(range(8)))
    return np.stack([r["out"] for r in res.results], axis=0)


# version 11 (final)
# speedup vs baseline: 1.2701x; 1.0279x over previous
import numpy as np
from contextlib import ExitStack

import concourse.bass as bass
import concourse.mybir as mybir
from concourse.bass_utils import run_bass_kernel_spmd

F32 = mybir.dt.float32
BF16 = mybir.dt.bfloat16
AF = mybir.ActivationFunctionType
ALU = mybir.AluOpType
AX = mybir.AxisListType

T, D, H, DH, FF = 4096, 1024, 16, 64, 4096
NT, NC8, NS = 32, 8, 8
C23 = 12582912.0
C52 = 6755399441055744.0
EPS = 1e-5


def dap(t, off, dims):
    return bass.AP(t, off, [list(d) for d in dims])


class DR:
    _OPS = {"tensor_scalar", "tensor_tensor", "tensor_copy", "reduce_sum",
            "reduce_max", "tensor_reduce", "reciprocal", "memset",
            "scalar_tensor_tensor", "tensor_single_scalar", "activation"}

    def __init__(self, e):
        self._e = e

    def __getattr__(self, name):
        f = getattr(self._e, name)
        if name in DR._OPS:
            def g(*a, **k):
                r = f(*a, **k)
                self._e.drain()
                return r
            return g
        return f


def build():
    nc = bass.Bass("TRN2", target_bir_lowering=False,
                   detect_race_conditions=False)

    x_in = nc.dram_tensor("x", [T, D], F32, kind="ExternalInput")
    w_q = nc.dram_tensor("wq", [D, D], F32, kind="ExternalInput")
    w_k = nc.dram_tensor("wk", [D, D], F32, kind="ExternalInput")
    w_v = nc.dram_tensor("wv", [D, D], F32, kind="ExternalInput")
    w_o = nc.dram_tensor("wo", [D, D], F32, kind="ExternalInput")
    w_1 = nc.dram_tensor("w1", [FF, D], F32, kind="ExternalInput")
    b_1 = nc.dram_tensor("b1", [32, 128], F32, kind="ExternalInput")
    w_2 = nc.dram_tensor("w2", [D, FF], F32, kind="ExternalInput")
    b_2 = nc.dram_tensor("b2", [8, 128], F32, kind="ExternalInput")
    out = nc.dram_tensor("out", [T, D], F32, kind="ExternalOutput")

    s_ctx = nc.dram_tensor("s_ctx", [T, D], BF16)
    s_x1 = nc.dram_tensor("s_x1", [T, D], F32)
    s_w1t = nc.dram_tensor("s_w1t", [32, 8, 128, 128], BF16)
    s_w2t = nc.dram_tensor("s_w2t", [8, 32, 128, 128], BF16)

    g_es = ExitStack()

    def sb(stk, name, shape, dtype, **kw):
        return stk.enter_context(nc.sbuf_tensor(name, shape, dtype, **kw))

    wsel = sb(g_es, "wsel", [128, 1], F32)
    ones = sb(g_es, "ones", [128, 128], F32)
    kq = sb(g_es, "kq", [128, 64], F32)
    gam = sb(g_es, "gam", [128, 8], F32)
    b1c = sb(g_es, "b1c", [128, 32], F32)
    b2c = sb(g_es, "b2c", [128, 16], F32)
    gat = sb(g_es, "gat", [128, 128], BF16)
    gatT = sb(g_es, "gatT", [128, 128], BF16, is_transpose_dest=True)
    st = sb(g_es, "st", [128, 128], F32)

    def stc(p, i):
        return st[:, 64 * p + 4 * i:64 * p + 4 * (i + 1)]

    def stcol(p, i, t):
        c = 64 * p + 4 * i + t
        return st[:, c:c + 1]

    def stbc(p, i):
        return dap(st, 64 * p + 4 * i, [[128, 128], [1, 4], [0, 1024]])

    r_es = ExitStack()
    tb1 = sb(r_es, "tb1", [128, 4096], F32)
    tb2 = sb(r_es, "tb2", [128, 4096], F32)

    wt_es = ExitStack()
    wt_q = sb(wt_es, "wt_q", [128, NC8 * D], BF16, is_transpose_dest=True)

    w_es = ExitStack()
    wbig = sb(w_es, "wbig", [128, 4096], F32)
    qbuf = sb(w_es, "qbuf", [128, 4096], BF16)
    wtst = sb(w_es, "wtst", [128, 4096], BF16, is_transpose_dest=True)
    bk_es = ExitStack()
    wt_k = sb(bk_es, "wt_k", [128, NC8 * D], BF16, is_transpose_dest=True)
    wt_v = sb(bk_es, "wt_v", [128, NC8 * D], BF16, is_transpose_dest=True)
    pre_es = ExitStack()
    wbig2 = sb(pre_es, "wbig2", [128, 4096], F32)
    qbuf2 = sb(pre_es, "qbuf2", [128, 4096], BF16)

    p_es = ExitStack()
    ps_g = p_es.enter_context(nc.psum_tensor("ps_g", [128, 512], F32))
    ps_a = p_es.enter_context(nc.psum_tensor("ps_a", [128, 1024], F32))
    ps_b = p_es.enter_context(nc.psum_tensor("ps_b", [128, 1024], F32))
    ps_c = p_es.enter_context(nc.psum_tensor("ps_c", [128, 512], F32))
    ps_d = p_es.enter_context(nc.psum_tensor("ps_d", [128, 512], F32))

    class Ctr:
        def __init__(self, name):
            self.h = g_es.enter_context(nc.semaphore(name))
            self.n = 0
    sQ, sG, sV, sA, sM, sS, sP = (Ctr("sQ"), Ctr("sG"), Ctr("sV"), Ctr("sA"),
                                  Ctr("sM"), Ctr("sS"), Ctr("sP"))
    sU = Ctr("sU")
    sW = [Ctr("sW0"), Ctr("sW1")]
    sZ = [Ctr("sZ0"), Ctr("sZ1")]
    sX = [Ctr("sX0"), Ctr("sX1")]
    sY = [Ctr("sY0"), Ctr("sY1")]
    sC = [Ctr("sC0"), Ctr("sC1")]
    sL = Ctr("sL")
    sT = [Ctr("sT0"), Ctr("sT1")]
    sO = Ctr("sO")
    sD = [Ctr("sD0"), Ctr("sD1")]
    sB = [Ctr("sB0"), Ctr("sB1")]

    blk = g_es.enter_context(nc.Block())

    def pad(v):
        v.tensor_scalar(st[:, 120:124], st[:, 124:128], 1.0, None, ALU.mult)

    def rnd_inplace(v):
        v.tensor_scalar(tb1[:, :], tb2[:, :], C23, C23, ALU.add, ALU.subtract)
        v.tensor_scalar(tb2[:, :], tb2[:, :], C52, C52, ALU.add, ALU.subtract)
        v.tensor_tensor(tb2[:, :], tb2[:, :], tb1[:, :], ALU.subtract)
        return v.scalar_tensor_tensor(tb1[:, :], tb2[:, :], wsel[:, 0:1],
                                      tb1[:, :], ALU.mult, ALU.add)

    def act_fence(s):
        s.drain()
        return s.activation(kq[:, 63:64], wsel[:, 0:1], AF.Copy)

    def rsqrt(s, dst, tmp, src):
        s.activation(tmp, src, AF.Ln)
        s.drain()
        return s.activation(dst, tmp, AF.Exp, bias=0.0, scale=-0.5)

    @blk.scalar
    def _(s):
        s.add_instruction(mybir.InstLoadActFuncSet(
            name=nc.get_next_instruction_name(), act_func_set_id=6, ins=[], outs=[]))

    @blk.vector
    def _(v):
        v.memset(wsel[:, :], 2.0 ** -30)
        v.drain()
        v.tensor_scalar(wsel[:, :], wsel[:, :], 1.0, 1.0, ALU.add, ALU.subtract)
        v.drain()
        v.tensor_scalar(wsel[:, :], wsel[:, :], 1e30, 1.0, ALU.mult, ALU.min)
        v.drain()
        v.memset(ones[:, :], 1.0)
        v.memset(st[:, :], 0.0)
        v.memset(gat[:, :], 0.0)
        v.memset(wbig[0:32, 128:256], 0.0).then_inc(sV.h, 1)
    sV.n += 1
    n_ms0 = sV.n

    @blk.gpsimd
    def _(g):
        g.wait_ge(sV.h, n_ms0)
        g.dma_start(dap(wbig, 0, [[4096, 32], [1, 128]]),
                    dap(b_1, 0, [[128, 32], [1, 128]])).then_inc(sG.h, 16)
        g.dma_start(dap(wbig, 128, [[4096, 8], [1, 128]]),
                    dap(b_2, 0, [[128, 8], [1, 128]])).then_inc(sG.h, 16)
    sG.n += 32
    n_bld = sG.n

    @blk.vector
    def _(v):
        v.wait_ge(sG.h, n_bld)
        v.tensor_copy(gat[0:32, 0:128], wbig[0:32, 0:128])
        v.tensor_tensor(qbuf[0:32, 0:128], wbig[0:32, 0:128],
                        gat[0:32, 0:128], ALU.subtract)
        v.tensor_copy(qbuf[0:16, 128:256], wbig[0:16, 128:256])
        v.tensor_tensor(qbuf[0:16, 256:384], wbig[0:16, 128:256],
                        qbuf[0:16, 128:256], ALU.subtract).then_inc(sV.h, 1)
    sV.n += 1
    n_bv = sV.n

    @blk.sync
    def _(s):
        s.wait_ge(sV.h, n_bv)
        s.dma_start(gatT[:, 0:32], gat[0:32, 0:128], transpose=True).then_inc(sU.h, 16)
        s.dma_start(wtst[:, 0:32], qbuf[0:32, 0:128], transpose=True).then_inc(sU.h, 16)
        s.dma_start(gatT[:, 32:48], qbuf[0:16, 128:256], transpose=True).then_inc(sU.h, 16)
        s.dma_start(wtst[:, 32:48], qbuf[0:16, 256:384], transpose=True).then_inc(sU.h, 16)
    sU.n += 64
    n_bt = sU.n

    @blk.vector
    def _(v):
        v.wait_ge(sU.h, n_bt)
        v.tensor_tensor(b1c[:, :], gatT[:, 0:32], wtst[:, 0:32], ALU.add)
        v.tensor_tensor(b2c[:, :], gatT[:, 32:48], wtst[:, 32:48],
                        ALU.add).then_inc(sV.h, 1)
    sV.n += 1

    slotV = {}
    slotQ = {}
    slotW = {}

    def prep_attn(w_dram, dst_wt, gcol, tguard=0, wb=None, qb=None, extraq=()):
        wb = wbig if wb is None else wb
        qb = qbuf if qb is None else qb
        sd = sD[0] if wb is wbig else sD[1]
        wfree = slotV.get(id(wb), 0)
        for half in range(2):
            red = wfree if half == 0 else sV.n

            @blk.gpsimd
            def _(g, half=half, red=red):
                if red:
                    g.wait_ge(sV.h, red)
                g.dma_start(dap(wb, 0, [[4096, 128], [1024, 4], [1, 1024]]),
                            dap(w_dram, half * 524288,
                                [[1024, 128], [131072, 4], [1, 1024]])).then_inc(sd.h, 16)
            sd.n += 16
            ld = sd.n

            @blk.vector
            def _(v, half=half, ld=ld):
                v.wait_ge(sd.h, ld)
                v.reduce_sum(kq[:, 40 + half:41 + half], wb[:, :], AX.X,
                             apply_absolute_value=True).then_inc(sV.h, 1)
            sV.n += 1

        @blk.vector
        def _(v):
            v.drain()
            v.tensor_tensor(kq[:, 42:43], kq[:, 40:41], kq[:, 41:42],
                            ALU.add).then_inc(sV.h, 1)
        sV.n += 1
        pd = sV.n

        @blk.tensor
        def _(t, pd=pd):
            t.wait_ge(sV.h, pd)
            t.matmul(ps_g[:, 0:1], ones[:, :], kq[:, 42:43],
                     start=True, stop=True).then_inc(sM.h, 1)
        sM.n += 1
        md = sM.n

        @blk.vector
        def _(v, md=md):
            v.wait_ge(sM.h, md)
            v.tensor_scalar(kq[:, 43:44], ps_g[:, 0:1], 1.0 / (D * D), EPS,
                            ALU.mult, ALU.max)
            pad(v)
            v.tensor_copy(gam[:, gcol:gcol + 1], kq[:, 43:44])
            v.reciprocal(kq[:, 44:45], kq[:, 43:44]).then_inc(sV.h, 1)
        sV.n += 1

        qfree = slotQ.get(id(qb), 0) or 0
        for half in range(2):
            qd0 = sV.n
            tp0 = sU.n
            actg = sA.n

            @blk.gpsimd
            def _(g, half=half, qd0=qd0):
                g.wait_ge(sV.h, qd0)
                g.dma_start(dap(wb, 0, [[4096, 128], [1024, 4], [1, 1024]]),
                            dap(w_dram, half * 524288,
                                [[1024, 128], [131072, 4], [1, 1024]])).then_inc(sd.h, 16)
            sd.n += 16
            ld = sd.n

            @blk.vector
            def _(v, half=half, ld=ld, tp0=tp0, actg=actg, qfree=qfree,
                  extraq=extraq):
                v.wait_ge(sd.h, ld)
                if actg:
                    v.wait_ge(sA.h, actg)
                qg = tp0 if half == 1 else qfree
                if qg:
                    v.wait_ge(sU.h, qg)
                if half == 0:
                    for sem_, val_ in extraq:
                        if val_:
                            v.wait_ge(sem_.h, val_)
                v.tensor_scalar(tb2[:, :], wb[:, :], kq[:, 44:45], None, ALU.mult)
                v.tensor_scalar(tb1[:, :], tb2[:, :], C23, C23, ALU.add, ALU.subtract)
                v.tensor_scalar(wb[:, :], tb2[:, :], C52, C52, ALU.add, ALU.subtract)
                v.tensor_tensor(wb[:, :], wb[:, :], tb1[:, :], ALU.subtract)
                v.scalar_tensor_tensor(tb1[:, :], wb[:, :], wsel[:, 0:1],
                                       tb1[:, :], ALU.mult, ALU.add)
                v.tensor_scalar(qb[:, :], tb1[:, :], -1.0, 1.0,
                                ALU.max, ALU.min).then_inc(sV.h, 1)
            sV.n += 1
            qd = sV.n

            @blk.sync
            def _(s, half=half, qd=qd, tguard=tguard):
                s.wait_ge(sV.h, qd)
                if tguard and half == 0:
                    s.wait_ge(sM.h, tguard)
                for cl in range(4):
                    c = half * 4 + cl
                    for j in range(NC8):
                        s.dma_start(dst_wt[:, j * D + c * 128:j * D + (c + 1) * 128],
                                    qb[:, cl * D + j * 128:cl * D + (j + 1) * 128],
                                    transpose=True).then_inc(sU.h, 16)
            sU.n += 16 * 32
        slotV[id(wb)] = sV.n
        slotQ[id(qb)] = sU.n
        return sU.n

    def prep_ffn(w_dram, spill, layout):
        wv0 = slotV.get(id(wbig), 0)
        for grp in range(8):
            red = sA.n

            @blk.gpsimd
            def _(g, grp=grp, red=red, wv0=wv0):
                if grp == 0 and wv0:
                    g.wait_ge(sV.h, wv0)
                if red:
                    g.wait_ge(sA.h, red)
                if layout == "w1":
                    g.dma_start(dap(wbig, 0, [[4096, 128], [1024, 4], [1, 1024]]),
                                dap(w_dram, grp * 524288,
                                    [[1024, 128], [131072, 4], [1, 1024]])
                                ).then_inc(sD[0].h, 16)
                else:
                    g.dma_start(dap(wbig, 0, [[4096, 128], [1, 4096]]),
                                dap(w_dram, grp * 524288, [[4096, 128], [1, 4096]])
                                ).then_inc(sD[0].h, 16)
            sD[0].n += 16
            ld = sD[0].n

            @blk.scalar
            def _(s, grp=grp, ld=ld):
                s.wait_ge(sD[0].h, ld)
                for c in range(4):
                    s.activation(sqr[:, :], wbig[:, c * 1024:(c + 1) * 1024],
                                 AF.Abs, accum_out=kq[:, 50 + c:51 + c])
                act_fence(s).then_inc(sA.h, 1)
            sA.n += 1
            n_ab = sA.n

            @blk.gpsimd
            def _(g, grp=grp, n_ab=n_ab):
                g.wait_ge(sA.h, n_ab)
                g.tensor_tensor(kq[:, 54:55], kq[:, 50:51], kq[:, 51:52], ALU.add)
                g.tensor_tensor(kq[:, 55:56], kq[:, 52:53], kq[:, 53:54], ALU.add)
                g.tensor_tensor(kq[:, 54:55], kq[:, 54:55], kq[:, 55:56], ALU.add)
                if grp == 0:
                    g.tensor_copy(kq[:, 46:47], kq[:, 54:55]).then_inc(sP.h, 1)
                else:
                    g.tensor_tensor(kq[:, 46:47], kq[:, 46:47], kq[:, 54:55],
                                    ALU.add).then_inc(sP.h, 1)
            sP.n += 1
            yield
        pd = sP.n

        @blk.tensor
        def _(t, pd=pd):
            t.wait_ge(sP.h, pd)
            t.matmul(ps_g[:, 0:1], ones[:, :], kq[:, 46:47],
                     start=True, stop=True).then_inc(sM.h, 1)
        sM.n += 1
        md = sM.n

        @blk.vector
        def _(v, md=md):
            v.wait_ge(sM.h, md)
            v.tensor_scalar(kq[:, 43:44], ps_g[:, 0:1], 1.0 / (FF * D), EPS,
                            ALU.mult, ALU.add)
            pad(v)
            v.reciprocal(kq[:, 44:45], kq[:, 43:44]).then_inc(sV.h, 1)
        sV.n += 1
        yield

        p1a = sA.n
        qb0 = slotQ.get(id(qbuf), 0)
        wt0 = slotW.get(id(wtst), 0)
        qdone = [0, 0]
        tpdone = [0, 0]
        spdone = [0, 0]
        for grp in range(16):
            sl = grp % 2
            so = sl * 2048
            w_qd = qdone[sl]
            actg = sA.n

            @blk.gpsimd
            def _(g, grp=grp, so=so, sl=sl, w_qd=w_qd, p1a=p1a):
                if grp < 2:
                    g.wait_ge(sA.h, p1a)
                if w_qd:
                    g.wait_ge(sV.h, w_qd)
                if layout == "w1":
                    g.dma_start(dap(wbig, so, [[4096, 128], [1024, 2], [1, 1024]]),
                                dap(w_dram, grp * 262144,
                                    [[1024, 128], [131072, 2], [1, 1024]])
                                ).then_inc(sD[sl].h, 16)
                else:
                    g.dma_start(dap(wbig, so, [[4096, 128], [1, 2048]]),
                                dap(w_dram, (grp // 2) * 524288 + (grp % 2) * 2048,
                                    [[4096, 128], [1, 2048]])
                                ).then_inc(sD[sl].h, 16)
            sD[sl].n += 16
            ld = sD[sl].n
            w_tp = tpdone[sl]

            @blk.vector
            def _(v, grp=grp, so=so, sl=sl, ld=ld, actg=actg, w_tp=w_tp, qb0=qb0):
                v.wait_ge(sD[sl].h, ld)
                if actg:
                    v.wait_ge(sA.h, actg)
                if w_tp:
                    v.wait_ge(sW[sl].h, w_tp)
                elif qb0:
                    v.wait_ge(sU.h, qb0)
                w_ = wbig[:, so:so + 2048]
                q_ = qbuf[:, so:so + 2048]
                t1 = tb1[:, 0:2048]
                t2 = tb2[:, 0:2048]
                v.tensor_scalar(t2, w_, kq[:, 44:45], None, ALU.mult)
                v.tensor_scalar(t1, t2, C23, C23, ALU.add, ALU.subtract)
                v.tensor_scalar(w_, t2, C52, C52, ALU.add, ALU.subtract)
                v.tensor_tensor(w_, w_, t1, ALU.subtract)
                v.scalar_tensor_tensor(t1, w_, wsel[:, 0:1], t1, ALU.mult, ALU.add)
                v.tensor_scalar(q_, t1, -1.0, 1.0, ALU.max, ALU.min).then_inc(sV.h, 1)
            sV.n += 1
            qdone[sl] = sV.n
            qd = sV.n
            w_sp = spdone[sl]

            @blk.sync
            def _(s, grp=grp, so=so, sl=sl, qd=qd, w_sp=w_sp):
                s.wait_ge(sV.h, qd)
                if w_sp:
                    s.wait_ge(sZ[sl].h, w_sp)
                for bi in range(16):
                    s.dma_start(wtst[:, so + bi * 128:so + (bi + 1) * 128],
                                qbuf[:, so + bi * 128:so + (bi + 1) * 128],
                                transpose=True).then_inc(sW[sl].h, 16)
            sW[sl].n += 16 * 16
            tp = sW[sl].n
            tpdone[sl] = tp

            @blk.sync
            def _(s, grp=grp, so=so, sl=sl, tp=tp):
                s.wait_ge(sW[sl].h, tp)
                if layout == "w1":
                    s.dma_start(dap(spill, grp * 2 * 131072,
                                    [[128, 128], [131072, 2], [16384, 8], [1, 128]]),
                                dap(wtst, so, [[4096, 128], [1024, 2], [128, 8], [1, 128]])
                                ).then_inc(sZ[sl].h, 16)
                else:
                    s.dma_start(dap(spill, grp * 262144,
                                    [[128, 128], [16384, 16], [1, 128]]),
                                dap(wtst, so, [[4096, 128], [128, 16], [1, 128]])
                                ).then_inc(sZ[sl].h, 16)
            sZ[sl].n += 16
            spdone[sl] = sZ[sl].n
            yield
        slotV[id(wbig)] = sV.n
        slotQ[id(qbuf)] = None
        slotW[id(wtst)] = None

    prep_attn(w_v, wt_v, 2, wb=wbig2, qb=qbuf2)
    prep_attn(w_q, wt_q, 0)
    n_wprep = prep_attn(w_k, wt_k, 1, wb=wbig2, qb=qbuf2)
    pre_q_tp = n_wprep
    pre_q_v = slotV[id(wbig2)]
    pre_es.close()

    a_es = ExitStack()
    xa = sb(a_es, "xa", [128, 4096], F32)
    sqr = sb(a_es, "sqr", [128, 1024], F32)
    xqs = sb(a_es, "xqs", [128, 2 * 4096], BF16)
    xqTt = sb(a_es, "xqTt", [128, 2 * D], BF16, is_transpose_dest=True)
    q_sb = sb(a_es, "q_sb", [128, D], BF16)
    k_sb = sb(a_es, "k_sb", [128, D], BF16)
    v_sb = sb(a_es, "v_sb", [128, D], BF16)
    pr_g = sb(a_es, "pr_g", [128, 8192], BF16)
    pr_v = sb(a_es, "pr_v", [128, 8192], BF16)
    sc_sb = sb(a_es, "sc_sb", [128, 256], F32)
    mx_sb = sb(a_es, "mx_sb", [128, 16], F32)
    e_sb = sb(a_es, "e_sb", [128, 256], F32)
    sm_sb = sb(a_es, "sm_sb", [128, 16], F32)
    at_sb = sb(a_es, "at_sb", [128, 256], BF16)
    ctx_f = sb(a_es, "ctx_f", [128, D], F32)
    ctx_sb = sb(a_es, "ctx_sb", [128, 2 * D], BF16)

    pbanks = [(ps_a, 0), (ps_a, 512), (ps_b, 0), (ps_b, 512), (ps_c, 0), (ps_d, 0)]
    a_t1a, a_rnd, a_xq, s_tpq, a_xld = [], [], [], [], []
    ev_c, ctx_v, ctx_sp, mm_c, a_scd = [], [], [], [], []
    ev_v_c = {}

    prep_gens = [prep_ffn(w_1, s_w1t, "w1"), prep_ffn(w_2, s_w2t, "w2")]

    def pull_prep(k):
        while k > 0 and prep_gens:
            try:
                next(prep_gens[0])
                k -= 1
            except StopIteration:
                prep_gens.pop(0)

    def emit_xload(S):
        w_t1a = a_t1a[S - 1] if S >= 1 else 0

        @blk.gpsimd
        def _(g, S=S, w_t1a=w_t1a):
            if S == 0:
                g.wait_ge(sU.h, pre_q_tp)
                g.wait_ge(sV.h, pre_q_v)
            if w_t1a:
                g.wait_ge(sA.h, w_t1a)
            g.dma_start(dap(xa, 0, [[4096, 128], [1024, 4], [1, 1024]]),
                        dap(x_in, S * 4 * 131072,
                            [[1024, 128], [131072, 4], [1, 1024]])).then_inc(sG.h, 16)
        sG.n += 16
        a_xld.append(sG.n)

    emit_xload(0)
    for S in range(NS):
        pb = S % 2
        n_xld = a_xld[S]
        w_rnd2 = a_rnd[S - 2] if S >= 2 else 0

        @blk.scalar
        def _(s, pb=pb, n_xld=n_xld, w_rnd2=w_rnd2):
            s.wait_ge(sG.h, n_xld)
            if w_rnd2:
                s.wait_ge(sV.h, w_rnd2)
            for t in range(4):
                xt = xa[:, t * 1024:(t + 1) * 1024]
                s.activation(sqr[:, :], xt, AF.Copy, accum_out=stcol(pb, 0, t))
                s.activation(sqr[:, :], xt, AF.Square, accum_out=stcol(pb, 1, t))
            act_fence(s).then_inc(sA.h, 1)
        sA.n += 1
        n_st8 = sA.n

        @blk.vector
        def _(v, pb=pb, n_st8=n_st8):
            v.wait_ge(sA.h, n_st8)
            v.tensor_scalar(stc(pb, 0), stc(pb, 0), -1.0 / D, None, ALU.mult)
            v.tensor_scalar(stc(pb, 1), stc(pb, 1), 1.0 / D, None, ALU.mult)
            v.tensor_tensor(stc(pb, 2), stc(pb, 0), stc(pb, 0), ALU.mult)
            pad(v)
            v.tensor_tensor(stc(pb, 1), stc(pb, 1), stc(pb, 2), ALU.subtract)
            pad(v)
            v.tensor_scalar(stc(pb, 1), stc(pb, 1), 1.0 + 1e-6, 1e-11,
                            ALU.mult, ALU.add).then_inc(sV.h, 1)
        sV.n += 1
        n_ty1 = sV.n

        @blk.scalar
        def _(s, pb=pb, n_ty1=n_ty1):
            s.wait_ge(sV.h, n_ty1)
            rsqrt(s, stc(pb, 2), stc(pb, 3), stc(pb, 1))
            s.drain()
            mi = None
            for t in range(4):
                mi = s.activation(tb2[:, t * 1024:(t + 1) * 1024],
                                  xa[:, t * 1024:(t + 1) * 1024], AF.Identity,
                                  bias=stcol(pb, 0, t), scale=1.0)
            mi.then_inc(sA.h, 1)
        sA.n += 1
        a_t1a.append(sA.n)
        n_t1a = sA.n

        @blk.vector
        def _(v, pb=pb, n_t1a=n_t1a):
            v.wait_ge(sA.h, n_t1a)
            v.tensor_reduce(stc(pb, 1),
                            dap(tb2, 0, [[4096, 128], [1024, 4], [1, 1024]]),
                            AX.X, ALU.max, apply_absolute_value=True)
            pad(v)
            v.tensor_tensor(stc(pb, 1), stc(pb, 1), stc(pb, 2), ALU.mult)
            pad(v)
            v.tensor_scalar(stc(pb, 1), stc(pb, 1), EPS, None, ALU.max)
            pad(v)
            v.reciprocal(stc(pb, 3), stc(pb, 1))
            pad(v)
            v.tensor_scalar(stc(pb, 3), stc(pb, 3), 127.0, None, ALU.mult)
            v.tensor_scalar(stc(pb, 1), stc(pb, 1), 1.0 / 127.0, None, ALU.mult)
            v.tensor_tensor(stc(pb, 2), stc(pb, 2), stc(pb, 3), ALU.mult)
            pad(v)
            v.tensor_tensor(dap(tb2, 0, [[4096, 128], [1024, 4], [1, 1024]]),
                            dap(tb2, 0, [[4096, 128], [1024, 4], [1, 1024]]),
                            stbc(pb, 2), ALU.mult)
            rnd_inplace(v).then_inc(sV.h, 1)
        sV.n += 1
        a_rnd.append(sV.n)
        n_rnd = sV.n
        w_tpm = mm_c[4 * (S - 2) + 3] if S >= 2 else 0

        @blk.scalar
        def _(s, pb=pb, n_rnd=n_rnd, w_tpm=w_tpm):
            s.wait_ge(sV.h, n_rnd)
            if w_tpm:
                s.wait_ge(sM.h, w_tpm)
            mi = None
            for t in range(4):
                mi = s.activation(xqs[:, pb * 4096 + t * 1024:pb * 4096 + (t + 1) * 1024],
                                  tb1[:, t * 1024:(t + 1) * 1024], AF.Copy,
                                  bias=0.0, scale=stcol(pb, 1, t))
            mi.then_inc(sA.h, 1)
        sA.n += 1
        a_xq.append(sA.n)
        n_xq = sA.n

        for t4 in range(4):
            tt = S * 4 + t4
            tpb = tt % 2
            w_mm = mm_c[tt - 2] if tt >= 2 else 0

            @blk.sync
            def _(s, t4=t4, pb=pb, tpb=tpb, n_xq=n_xq, w_mm=w_mm):
                s.wait_ge(sA.h, n_xq)
                if w_mm:
                    s.wait_ge(sM.h, w_mm)
                for j in range(NC8):
                    s.dma_start(xqTt[:, tpb * D + j * 128:tpb * D + (j + 1) * 128],
                                xqs[:, pb * 4096 + t4 * D + j * 128:
                                    pb * 4096 + t4 * D + (j + 1) * 128],
                                transpose=True).then_inc(sX[tpb].h, 16)
            sX[tpb].n += 16 * NC8
            n_tp = sX[tpb].n
            w_ev = ev_v_c.get(tt - 1, 0) if tt >= 1 else 0

            @blk.tensor
            def _(t, tt=tt, tpb=tpb, n_tp=n_tp, w_ev=w_ev):
                t.wait_ge(sX[tpb].h, n_tp)
                if tt == 0:
                    t.wait_ge(sU.h, n_wprep)
                if w_ev:
                    t.wait_ge(sA.h, w_ev)
                mi = None
                for j in range(NC8):
                    lhs = xqTt[:, tpb * D + j * 128:tpb * D + (j + 1) * 128]
                    for pi, wt in enumerate((wt_q, wt_k, wt_v)):
                        for n in range(2):
                            pb_, off = pbanks[pi * 2 + n]
                            mi = t.matmul(pb_[:, off:off + 512], lhs,
                                          wt[:, j * D + n * 512:j * D + (n + 1) * 512],
                                          start=(j == 0), stop=(j == NC8 - 1))
                mi.then_inc(sM.h, 1)
            sM.n += 1
            mm_c.append(sM.n)
            n_mm = sM.n
            w_qk = a_scd[tt - 1] if tt >= 1 else 0
            w_mix = ctx_v[tt - 1] if tt >= 1 else 0

            @blk.scalar
            def _(s, n_mm=n_mm, w_qk=w_qk):
                s.wait_ge(sM.h, n_mm)
                if w_qk:
                    s.wait_ge(sV.h, w_qk)
                mi = None
                for pi, dst in enumerate((q_sb, k_sb)):
                    for n in range(2):
                        pb_, off = pbanks[pi * 2 + n]
                        mi = s.activation(dst[:, n * 512:(n + 1) * 512],
                                          pb_[:, off:off + 512],
                                          AF.Copy, bias=0.0, scale=gam[:, pi:pi + 1])
                mi.then_inc(sA.h, 1)
            sA.n += 1
            ev_c.append(sA.n)
            n_evc = sA.n

            @blk.scalar
            def _(s, w_mix=w_mix):
                if w_mix:
                    s.wait_ge(sV.h, w_mix)
                mi = None
                for n in range(2):
                    pb_, off = pbanks[4 + n]
                    mi = s.activation(v_sb[:, n * 512:(n + 1) * 512],
                                      pb_[:, off:off + 512],
                                      AF.Copy, bias=0.0, scale=gam[:, 2:3])
                mi.then_inc(sA.h, 1)
            sA.n += 1
            ev_v_c[tt] = sA.n

            @blk.gpsimd
            def _(g, n_evc=n_evc):
                g.wait_ge(sA.h, n_evc)
                g.tensor_tensor(
                    dap(pr_g, 0, [[8192, 128], [1024, 8], [64, 16], [1, 64]]),
                    dap(q_sb, 0, [[D, 128], [64, 8], [0, 16], [1, 64]]),
                    dap(k_sb, 0, [[D, 128], [0, 8], [64, 16], [1, 64]]),
                    ALU.mult).then_inc(sP.h, 1)
            sP.n += 1
            g_sc = sP.n

            @blk.vector
            def _(v, n_evc=n_evc, g_sc=g_sc):
                v.wait_ge(sA.h, n_evc)
                v.tensor_tensor(
                    dap(pr_v, 0, [[8192, 128], [1024, 8], [64, 16], [1, 64]]),
                    dap(q_sb, 512, [[D, 128], [64, 8], [0, 16], [1, 64]]),
                    dap(k_sb, 0, [[D, 128], [0, 8], [64, 16], [1, 64]]),
                    ALU.mult)
                v.tensor_reduce(dap(sc_sb, 128, [[256, 128], [16, 8], [1, 16]]),
                                dap(pr_v, 0,
                                    [[8192, 128], [1024, 8], [64, 16], [1, 64]]),
                                AX.X, ALU.add)
                v.wait_ge(sP.h, g_sc)
                v.tensor_reduce(dap(sc_sb, 0, [[256, 128], [16, 8], [1, 16]]),
                                dap(pr_g, 0,
                                    [[8192, 128], [1024, 8], [64, 16], [1, 64]]),
                                AX.X, ALU.add)
                pad(v)
                v.tensor_reduce(dap(mx_sb, 0, [[16, 128], [1, 16]]),
                                dap(sc_sb, 0, [[256, 128], [16, 16], [1, 16]]),
                                AX.X, ALU.max)
                pad(v)
                v.tensor_tensor(sc_sb[:, :], sc_sb[:, :],
                                dap(mx_sb, 0, [[16, 128], [1, 16], [0, 16]]),
                                ALU.subtract).then_inc(sV.h, 1)
            sV.n += 1
            a_scd.append(sV.n)
            n_scd = sV.n

            @blk.scalar
            def _(s, n_scd=n_scd):
                s.wait_ge(sV.h, n_scd)
                s.activation(e_sb[:, :], sc_sb[:, :], AF.Exp, bias=0.0,
                             scale=0.125).then_inc(sA.h, 1)
            sA.n += 1
            n_exp = sA.n

            @blk.vector
            def _(v, n_exp=n_exp):
                v.wait_ge(sA.h, n_exp)
                v.tensor_reduce(dap(sm_sb, 0, [[16, 128], [1, 16]]),
                                dap(e_sb, 0, [[256, 128], [16, 16], [1, 16]]),
                                AX.X, ALU.add)
                pad(v)
                v.reciprocal(sm_sb[:, :], sm_sb[:, :])
                pad(v)
                v.tensor_tensor(at_sb[:, :], e_sb[:, :],
                                dap(sm_sb, 0, [[16, 128], [1, 16], [0, 16]]),
                                ALU.mult).then_inc(sV.h, 1)
            sV.n += 1
            n_at = sV.n

            @blk.vector
            def _(v, n_evv=ev_v_c[tt]):
                v.wait_ge(sA.h, n_evv)
                pad(v)
                for hh in range(2):
                    v.tensor_tensor(
                        dap(pr_v, 0, [[8192, 128], [1024, 8], [16, 64], [1, 16]]),
                        dap(at_sb, hh * 128, [[256, 128], [16, 8], [0, 64], [1, 16]]),
                        dap(v_sb, 0, [[D, 128], [0, 8], [1, 64], [64, 16]]),
                        ALU.mult)
                    v.tensor_reduce(dap(ctx_f, hh * 512, [[D, 128], [64, 8], [1, 64]]),
                                    dap(pr_v, 0,
                                        [[8192, 128], [1024, 8], [16, 64], [1, 16]]),
                                    AX.X, ALU.add).then_inc(sV.h, 1) if hh == 1 else                     v.tensor_reduce(dap(ctx_f, hh * 512, [[D, 128], [64, 8], [1, 64]]),
                                    dap(pr_v, 0,
                                        [[8192, 128], [1024, 8], [16, 64], [1, 16]]),
                                    AX.X, ALU.add)
            sV.n += 1
            ctx_v.append(sV.n)
            n_cv = sV.n
            w_ctx = ctx_sp[tt - 2] if tt >= 2 else 0

            @blk.scalar
            def _(s, tpb=tpb, n_cv=n_cv, w_ctx=w_ctx):
                s.wait_ge(sV.h, n_cv)
                if w_ctx:
                    s.wait_ge(sY[tpb].h, w_ctx)
                s.activation(ctx_sb[:, tpb * D:(tpb + 1) * D], ctx_f[:, :],
                             AF.Copy).then_inc(sA.h, 1)
            sA.n += 1
            n_cc = sA.n

            @blk.gpsimd
            def _(g, tt=tt, tpb=tpb, n_cc=n_cc):
                g.wait_ge(sA.h, n_cc)
                g.dma_start(dap(s_ctx, tt * 128 * 64,
                                [[64, 128], [262144, 16], [1, 64]]),
                            dap(ctx_sb, tpb * D,
                                [[2 * D, 128], [64, 16], [1, 64]])).then_inc(sY[tpb].h, 16)
            sY[tpb].n += 16
            ctx_sp.append(sY[tpb].n)
        s_tpq.append(sQ.n)
        if S + 1 < NS:
            emit_xload(S + 1)
        pull_prep(6)

    while prep_gens:
        pull_prep(8)

    n_abc_mm = sM.n
    prep_attn(w_o, wt_q, 3, tguard=n_abc_mm,
              extraq=((sW[0], sW[0].n), (sW[1], sW[1].n)))
    n_prep_u = sU.n
    n_prep_g = sG.n
    a_es.close()
    bk_es.close()
    w_es.close()

    xln2T = sb(g_es, "xln2T", [128, NC8 * T], BF16, is_transpose_dest=True,
               side="right")
    d_es = ExitStack()
    ctd = sb(d_es, "ctd", [128, 2 * 4096], BF16)
    xd = sb(d_es, "xd", [128, 4096], F32)
    sqd = sb(d_es, "sqd", [128, 1024], F32)
    cqs = sb(d_es, "cqs", [128, 4096], BF16)
    cqT = sb(d_es, "cqT", [128, 2 * D], BF16, is_transpose_dest=True)
    aod = sb(d_es, "aod", [128, 4096], F32)
    x1d = sb(d_es, "x1d", [128, 4096], F32)
    xl2 = sb(d_es, "xl2", [128, 4096], BF16)

    d_rnd, d_x1, d_fin, d_xsp, d_ld = [], [], [], [], []
    d_tpq_s, d_tps_s, d_ltq_s, d_lts_s = [], [], [], []
    d_mm, d_aev = [], []

    def emit_dload(S):
        pb = S % 2
        ofs = pb * 4096
        w_ctd = d_rnd[S - 2] if S >= 2 else 0
        w_xd = d_x1[S - 1] if S >= 1 else 0

        @blk.gpsimd
        def _(g, S=S, ofs=ofs, w_ctd=w_ctd, w_xd=w_xd):
            if S == 0:
                g.wait_ge(sY[0].h, sY[0].n)
                g.wait_ge(sY[1].h, sY[1].n)
                g.wait_ge(sU.h, n_prep_u)
                g.wait_ge(sG.h, n_prep_g)
                g.wait_ge(sW[0].h, sW[0].n)
                g.wait_ge(sW[1].h, sW[1].n)
                g.wait_ge(sZ[0].h, sZ[0].n)
                g.wait_ge(sZ[1].h, sZ[1].n)
                g.wait_ge(sX[0].h, sX[0].n)
                g.wait_ge(sX[1].h, sX[1].n)
            if w_ctd:
                g.wait_ge(sV.h, w_ctd)
            g.dma_start(dap(ctd, ofs, [[2 * 4096, 128], [1024, 4], [1, 1024]]),
                        dap(s_ctx, S * 4 * 131072,
                            [[1024, 128], [131072, 4], [1, 1024]])).then_inc(sG.h, 16)
            if w_xd:
                g.wait_ge(sV.h, w_xd)
            g.dma_start(dap(xd, 0, [[4096, 128], [1024, 4], [1, 1024]]),
                        dap(x_in, S * 4 * 131072,
                            [[1024, 128], [131072, 4], [1, 1024]])).then_inc(sG.h, 16)
        sG.n += 32
        d_ld.append(sG.n)

    emit_dload(0)
    for S in range(NS):
        pb = S % 2
        ofs = pb * 4096
        n_ld = d_ld[S]
        w_st = d_fin[S - 2] if S >= 2 else 0

        @blk.scalar
        def _(s, pb=pb, ofs=ofs, n_ld=n_ld, w_st=w_st):
            s.wait_ge(sG.h, n_ld)
            if w_st:
                s.wait_ge(sV.h, w_st)
            for t in range(4):
                s.activation(sqd[:, :], ctd[:, ofs + t * 1024:ofs + (t + 1) * 1024],
                             AF.Square, accum_out=stcol(pb, 0, t))
            act_fence(s).then_inc(sA.h, 1)
        sA.n += 1
        n_st4 = sA.n

        @blk.vector
        def _(v, pb=pb, n_st4=n_st4):
            v.wait_ge(sA.h, n_st4)
            v.tensor_scalar(stc(pb, 0), stc(pb, 0), 1.0 / D, 1e-6,
                            ALU.mult, ALU.add).then_inc(sV.h, 1)
        sV.n += 1
        n_ty1 = sV.n

        @blk.scalar
        def _(s, pb=pb, n_ty1=n_ty1):
            s.wait_ge(sV.h, n_ty1)
            rsqrt(s, stc(pb, 2), stc(pb, 3), stc(pb, 0)).then_inc(sA.h, 1)
        sA.n += 1
        n_rr = sA.n

        @blk.vector
        def _(v, pb=pb, ofs=ofs, n_rr=n_rr):
            v.wait_ge(sA.h, n_rr)
            v.tensor_reduce(stc(pb, 1),
                            dap(ctd, ofs, [[2 * 4096, 128], [1024, 4], [1, 1024]]),
                            AX.X, ALU.max, apply_absolute_value=True)
            pad(v)
            v.tensor_tensor(stc(pb, 1), stc(pb, 1), stc(pb, 2), ALU.mult)
            pad(v)
            v.tensor_scalar(stc(pb, 1), stc(pb, 1), EPS, None, ALU.max)
            pad(v)
            v.reciprocal(stc(pb, 3), stc(pb, 1))
            pad(v)
            v.tensor_scalar(stc(pb, 3), stc(pb, 3), 127.0, None, ALU.mult)
            v.tensor_scalar(stc(pb, 1), stc(pb, 1), 1.0 / 127.0, None, ALU.mult)
            v.tensor_tensor(stc(pb, 2), stc(pb, 2), stc(pb, 3), ALU.mult)
            pad(v)
            v.tensor_tensor(dap(tb2, 0, [[4096, 128], [1024, 4], [1, 1024]]),
                            dap(ctd, ofs, [[2 * 4096, 128], [1024, 4], [1, 1024]]),
                            stbc(pb, 2), ALU.mult)
            rnd_inplace(v).then_inc(sV.h, 1)
        sV.n += 1
        d_rnd.append(sV.n)
        n_rnd = sV.n
        w_tpm = d_mm[4 * (S - 1) + 3] if S >= 1 else 0

        @blk.scalar
        def _(s, pb=pb, n_rnd=n_rnd, w_tpm=w_tpm):
            s.wait_ge(sV.h, n_rnd)
            if w_tpm:
                s.wait_ge(sM.h, w_tpm)
            mi = None
            for t in range(4):
                mi = s.activation(cqs[:, t * 1024:(t + 1) * 1024],
                                  tb1[:, t * 1024:(t + 1) * 1024], AF.Copy,
                                  bias=0.0, scale=stcol(pb, 1, t))
            mi.then_inc(sA.h, 1)
        sA.n += 1
        n_cq = sA.n

        for t4 in range(4):
            tt = S * 4 + t4
            tpb = tt % 2
            w_mmo = d_mm[tt - 2] if tt >= 2 else 0

            def tpd(s, t4=t4, tpb=tpb, n_cq=n_cq, w_mmo=w_mmo):
                s.wait_ge(sA.h, n_cq)
                if w_mmo:
                    s.wait_ge(sM.h, w_mmo)
                for j in range(NC8):
                    s.dma_start(cqT[:, tpb * D + j * 128:tpb * D + (j + 1) * 128],
                                cqs[:, t4 * D + j * 128:t4 * D + (j + 1) * 128],
                                transpose=True).then_inc(sC[tpb].h, 16)
            if t4 % 2 == 0:
                blk.sync(tpd)
            else:
                blk.scalar(tpd)
            sC[tpb].n += 16 * NC8
            n_tpc = sC[tpb].n
            w_aev = d_aev[tt - 1] if tt >= 1 else 0

            @blk.tensor
            def _(t, tpb=tpb, n_tpc=n_tpc, w_aev=w_aev):
                t.wait_ge(sC[tpb].h, n_tpc)
                if w_aev:
                    t.wait_ge(sV.h, w_aev)
                mi = None
                for j in range(NC8):
                    lhs = cqT[:, tpb * D + j * 128:tpb * D + (j + 1) * 128]
                    mi = t.matmul(ps_c[:, :], lhs, wt_q[:, j * D + 0:j * D + 512],
                                  start=(j == 0), stop=(j == NC8 - 1))
                    mi = t.matmul(ps_d[:, :], lhs, wt_q[:, j * D + 512:j * D + 1024],
                                  start=(j == 0), stop=(j == NC8 - 1))
                mi.then_inc(sM.h, 1)
            sM.n += 1
            d_mm.append(sM.n)
            n_omm = sM.n

            @blk.vector
            def _(v, t4=t4, n_omm=n_omm):
                v.wait_ge(sM.h, n_omm)
                v.tensor_scalar(aod[:, t4 * D:t4 * D + 512], ps_c[:, :],
                                gam[:, 3:4], None, ALU.mult, ALU.add,
                                accum_out=st[:, 48 + t4:49 + t4])
                v.tensor_scalar(aod[:, t4 * D + 512:(t4 + 1) * D], ps_d[:, :],
                                gam[:, 3:4], None, ALU.mult, ALU.add,
                                accum_out=st[:, 52 + t4:53 + t4]).then_inc(sV.h, 1)
            sV.n += 1
            d_aev.append(sV.n)
        n_aev4 = sV.n

        @blk.scalar
        def _(s, pb=pb, n_aev4=n_aev4):
            s.wait_ge(sV.h, n_aev4)
            for t in range(4):
                s.activation(sqd[:, :], aod[:, t * 1024:(t + 1) * 1024],
                             AF.Square, accum_out=stcol(pb, 1, t))
            act_fence(s).then_inc(sA.h, 1)
        sA.n += 1
        n_sqa = sA.n

        @blk.vector
        def _(v, pb=pb, n_sqa=n_sqa):
            v.wait_ge(sA.h, n_sqa)
            v.tensor_tensor(stc(pb, 0), st[:, 48:52], st[:, 52:56], ALU.add)
            pad(v)
            v.tensor_scalar(stc(pb, 0), stc(pb, 0), -1.0 / D, None, ALU.mult)
            v.tensor_scalar(stc(pb, 1), stc(pb, 1), 1.0 / D, None, ALU.mult)
            v.tensor_tensor(stc(pb, 2), stc(pb, 0), stc(pb, 0), ALU.mult)
            pad(v)
            v.tensor_tensor(stc(pb, 1), stc(pb, 1), stc(pb, 2), ALU.subtract)
            pad(v)
            v.tensor_scalar(stc(pb, 1), stc(pb, 1), EPS, None,
                            ALU.add).then_inc(sV.h, 1)
        sV.n += 1
        n_ty2 = sV.n

        @blk.scalar
        def _(s, pb=pb, n_ty2=n_ty2):
            s.wait_ge(sV.h, n_ty2)
            rsqrt(s, stc(pb, 2), stc(pb, 3), stc(pb, 1))
            s.drain()
            mi = None
            for t in range(4):
                mi = s.activation(tb2[:, t * 1024:(t + 1) * 1024],
                                  aod[:, t * 1024:(t + 1) * 1024], AF.Identity,
                                  bias=stcol(pb, 0, t), scale=1.0)
            mi.then_inc(sA.h, 1)
        sA.n += 1
        n_t1b = sA.n
        w_xsp = d_xsp[S - 1] if S >= 1 else 0

        @blk.vector
        def _(v, pb=pb, n_t1b=n_t1b, w_xsp=w_xsp):
            v.wait_ge(sA.h, n_t1b)
            if w_xsp:
                v.wait_ge(sG.h, w_xsp)
            v.tensor_tensor(dap(tb2, 0, [[4096, 128], [1024, 4], [1, 1024]]),
                            dap(tb2, 0, [[4096, 128], [1024, 4], [1, 1024]]),
                            stbc(pb, 2), ALU.mult)
            v.tensor_tensor(x1d[:, :], tb2[:, :], xd[:, :],
                            ALU.add).then_inc(sV.h, 1)
        sV.n += 1
        d_x1.append(sV.n)
        n_x1 = sV.n
        if S + 1 < NS:
            emit_dload(S + 1)

        @blk.gpsimd
        def _(g, S=S, n_x1=n_x1):
            g.wait_ge(sV.h, n_x1)
            g.dma_start(dap(s_x1, S * 4 * 131072,
                            [[1024, 128], [131072, 4], [1, 1024]]),
                        dap(x1d, 0, [[4096, 128], [1024, 4], [1, 1024]])
                        ).then_inc(sG.h, 16)
        sG.n += 16
        d_xsp.append(sG.n)

        @blk.scalar
        def _(s, pb=pb, n_x1=n_x1):
            s.wait_ge(sV.h, n_x1)
            for t in range(4):
                xt = x1d[:, t * 1024:(t + 1) * 1024]
                s.activation(sqd[:, :], xt, AF.Copy, accum_out=stcol(pb, 0, t))
                s.activation(sqd[:, :], xt, AF.Square, accum_out=stcol(pb, 1, t))
            act_fence(s).then_inc(sA.h, 1)
        sA.n += 1
        n_stx = sA.n

        @blk.vector
        def _(v, pb=pb, n_stx=n_stx):
            v.wait_ge(sA.h, n_stx)
            v.tensor_scalar(stc(pb, 0), stc(pb, 0), -1.0 / D, None, ALU.mult)
            v.tensor_scalar(stc(pb, 1), stc(pb, 1), 1.0 / D, None, ALU.mult)
            v.tensor_tensor(stc(pb, 2), stc(pb, 0), stc(pb, 0), ALU.mult)
            pad(v)
            v.tensor_tensor(stc(pb, 1), stc(pb, 1), stc(pb, 2), ALU.subtract)
            pad(v)
            v.tensor_scalar(stc(pb, 1), stc(pb, 1), EPS, None,
                            ALU.add).then_inc(sV.h, 1)
        sV.n += 1
        n_ty3 = sV.n

        @blk.scalar
        def _(s, pb=pb, n_ty3=n_ty3):
            s.wait_ge(sV.h, n_ty3)
            rsqrt(s, stc(pb, 2), stc(pb, 3), stc(pb, 1))
            s.drain()
            mi = None
            for t in range(4):
                mi = s.activation(tb2[:, t * 1024:(t + 1) * 1024],
                                  x1d[:, t * 1024:(t + 1) * 1024], AF.Identity,
                                  bias=stcol(pb, 0, t), scale=1.0)
            mi.then_inc(sA.h, 1)
        sA.n += 1
        n_t1c = sA.n
        w_lt = d_ltq_s[S - 1] if S >= 1 else 0

        @blk.vector
        def _(v, pb=pb, n_t1c=n_t1c, w_lt=w_lt):
            v.wait_ge(sA.h, n_t1c)
            if w_lt:
                v.wait_ge(sL.h, w_lt)
            v.tensor_tensor(dap(xl2, 0, [[4096, 128], [1024, 4], [1, 1024]]),
                            dap(tb2, 0, [[4096, 128], [1024, 4], [1, 1024]]),
                            stbc(pb, 2), ALU.mult).then_inc(sV.h, 1)
        sV.n += 1
        d_fin.append(sV.n)
        n_fin = sV.n

        for t4 in range(4):
            tt = S * 4 + t4

            @blk.sync
            def _(s, t4=t4, tt=tt, n_fin=n_fin):
                s.wait_ge(sV.h, n_fin)
                for j in range(NC8):
                    s.dma_start(xln2T[:, j * T + tt * 128:j * T + (tt + 1) * 128],
                                xl2[:, t4 * D + j * 128:t4 * D + (j + 1) * 128],
                                transpose=True).then_inc(sL.h, 16)
            sL.n += 16 * NC8
        d_ltq_s.append(sL.n)

    n_dl_end = sL.n
    n_dg_end = sG.n
    n_dv_end = sV.n
    n_da_end = sA.n
    d_es.close()
    wt_es.close()
    r_es.close()

    e_es = ExitStack()
    w1s = sb(e_es, "w1s", [128, 2 * 1024], BF16)
    w2s = sb(e_es, "w2s", [128, 2 * 4096], BF16)
    hT = sb(e_es, "hT", [128, 32 * 1024], BF16)
    yst = sb(e_es, "yst", [128, 2 * 1024], BF16)
    ytok = sb(e_es, "ytok", [128, 8 * 1024], BF16, is_transpose_dest=True)
    x1f = sb(e_es, "x1f", [128, 4096], F32)
    obuf = sb(e_es, "obuf", [128, 4096], F32)

    out_st, fin_all = [], []
    ffn2_mm_prev = 0
    fin_prev_block = 0
    f1_mm, f2_mm = [], []
    for Tb in range(4):
        relu_cnt = []
        for fc in range(32):
            gi = Tb * 32 + fc
            w_slab = f1_mm[(gi - 2) * 2 + 1] if gi >= 2 else 0

            @blk.gpsimd
            def _(g, Tb=Tb, fc=fc, w_slab=w_slab):
                if Tb == 0 and fc == 0:
                    g.wait_ge(sV.h, n_dv_end)
                    g.wait_ge(sA.h, n_da_end)
                    g.wait_ge(sL.h, n_dl_end)
                    g.wait_ge(sG.h, n_dg_end)
                    g.wait_ge(sC[0].h, sC[0].n)
                    g.wait_ge(sC[1].h, sC[1].n)
                if w_slab:
                    g.wait_ge(sM.h, w_slab)
                g.dma_start(dap(w1s, (fc % 2) * 1024, [[2048, 128], [1, 1024]]),
                            dap(s_w1t, fc * 131072, [[128, 128], [16384, 8], [1, 128]])
                            ).then_inc(sB[fc % 2].h, 16)
            sB[fc % 2].n += 16
            n_sl = sB[fc % 2].n
            for n in range(2):
                u = fc * 2 + n
                bank = ps_c if u % 2 == 0 else ps_d
                w_relu = relu_cnt[u - 2] if u >= 2 else 0

                @blk.tensor
                def _(t, Tb=Tb, fc=fc, n=n, bank=bank, n_sl=n_sl,
                      w_relu=w_relu, w_prev2=ffn2_mm_prev):
                    t.wait_ge(sB[fc % 2].h, n_sl)
                    if fc == 0 and n == 0:
                        t.wait_ge(sL.h, n_dl_end)
                        if w_prev2:
                            t.wait_ge(sM.h, w_prev2)
                    if w_relu:
                        t.wait_ge(sA.h, w_relu)
                    mi = None
                    for dc in range(NC8):
                        lhs = w1s[:, (fc % 2) * 1024 + dc * 128:
                                  (fc % 2) * 1024 + (dc + 1) * 128]
                        mi = t.matmul(bank[:, :], lhs,
                                      xln2T[:, dc * T + Tb * 1024 + n * 512:
                                            dc * T + Tb * 1024 + (n + 1) * 512],
                                      start=(dc == 0), stop=(dc == NC8 - 1))
                    mi.then_inc(sM.h, 1)
                sM.n += 1
                f1_mm.append(sM.n)
                n_f1 = sM.n

                @blk.scalar
                def _(s, fc=fc, n=n, bank=bank, n_f1=n_f1):
                    s.wait_ge(sM.h, n_f1)
                    s.activation(hT[:, fc * 1024 + n * 512:fc * 1024 + (n + 1) * 512],
                                 bank[:, :], AF.Relu, bias=b1c[:, fc:fc + 1],
                                 scale=1.0).then_inc(sA.h, 1)
                sA.n += 1
                relu_cnt.append(sA.n)
        n_relu_all = sA.n

        yev, ytp = [], []
        for dc in range(NC8):
            gj = Tb * NC8 + dc
            w_slab2 = f2_mm[(gj - 2) * 2 + 1] if gj >= 2 else 0

            @blk.gpsimd
            def _(g, dc=dc, w_slab2=w_slab2):
                if w_slab2:
                    g.wait_ge(sM.h, w_slab2)
                g.dma_start(dap(w2s, (dc % 2) * 4096, [[8192, 128], [1, 4096]]),
                            dap(s_w2t, dc * 524288, [[128, 128], [16384, 32], [1, 128]])
                            ).then_inc(sB[dc % 2].h, 16)
            sB[dc % 2].n += 16
            n_sl = sB[dc % 2].n
            for n in range(2):
                u2 = dc * 2 + n
                boff = 0 if u2 % 2 == 0 else 512
                w_yev = yev[u2 - 2] if u2 >= 2 else 0
                w_ytp = ytp[dc - 2] if (dc >= 2 and n == 0) else 0
                w_ytp = w_ytp and w_ytp[dc % 2]

                @blk.tensor
                def _(t, dc=dc, n=n, boff=boff, n_sl=n_sl,
                      n_relu_all=n_relu_all, w_yev=w_yev):
                    t.wait_ge(sB[dc % 2].h, n_sl)
                    if dc == 0 and n == 0:
                        t.wait_ge(sA.h, n_relu_all)
                    if w_yev:
                        t.wait_ge(sV.h, w_yev)
                    mi = None
                    for fc in range(32):
                        lhs = w2s[:, (dc % 2) * 4096 + fc * 128:
                                  (dc % 2) * 4096 + (fc + 1) * 128]
                        mi = t.matmul(ps_a[:, boff:boff + 512], lhs,
                                      hT[:, fc * 1024 + n * 512:fc * 1024 + (n + 1) * 512],
                                      start=(fc == 0), stop=(fc == 31))
                    mi.then_inc(sM.h, 1)
                sM.n += 1
                f2_mm.append(sM.n)
                n_f2 = sM.n

                @blk.vector
                def _(v, dc=dc, n=n, boff=boff, n_f2=n_f2, w_ytp=w_ytp):
                    v.wait_ge(sM.h, n_f2)
                    if w_ytp:
                        v.wait_ge(sT[dc % 2].h, w_ytp)
                    v.tensor_scalar(yst[:, (dc % 2) * 1024 + n * 512:
                                        (dc % 2) * 1024 + (n + 1) * 512],
                                    ps_a[:, boff:boff + 512],
                                    b2c[:, dc:dc + 1], None,
                                    ALU.add).then_inc(sV.h, 1)
                sV.n += 1
                yev.append(sV.n)
            n_ye = sV.n

            def ytpb(s, dc=dc, n_ye=n_ye, fin_prev_block=fin_prev_block):
                s.wait_ge(sV.h, n_ye)
                if fin_prev_block:
                    s.wait_ge(sV.h, fin_prev_block)
                for ts_ in range(8):
                    s.dma_start(ytok[:, ts_ * 1024 + dc * 128:ts_ * 1024 + (dc + 1) * 128],
                                yst[:, (dc % 2) * 1024 + ts_ * 128:
                                    (dc % 2) * 1024 + (ts_ + 1) * 128],
                                transpose=True).then_inc(sT[dc % 2].h, 16)
            if dc % 2 == 0:
                blk.sync(ytpb)
            else:
                blk.scalar(ytpb)
            sT[dc % 2].n += 128
            ytp.append((sT[0].n, sT[1].n))
        n_ytp_all = sT[0].n
        n_ytp_all_s = sT[1].n

        for gq in range(2):
            gg = Tb * 2 + gq
            pb = gg % 2
            w_xlf = fin_all[gg - 1] if gg >= 1 else 0

            @blk.sync
            def _(s, Tb=Tb, gq=gq, w_xlf=w_xlf):
                if w_xlf:
                    s.wait_ge(sV.h, w_xlf)
                s.dma_start(dap(x1f, 0, [[4096, 128], [1024, 4], [1, 1024]]),
                            dap(s_x1, (Tb * 8 + gq * 4) * 131072,
                                [[1024, 128], [131072, 4], [1, 1024]])
                            ).then_inc(sQ.h, 16)
            sQ.n += 16
            n_xl = sQ.n
            w_ost = out_st[gg - 1] if gg >= 1 else 0
            w_st = fin_all[gg - 2] if gg >= 2 else 0

            @blk.scalar
            def _(s, pb=pb, gq=gq, n_ytp_all=n_ytp_all, n_ytp_all_s=n_ytp_all_s,
                  w_ost=w_ost, w_st=w_st):
                s.wait_ge(sT[0].h, n_ytp_all)
                s.wait_ge(sT[1].h, n_ytp_all_s)
                if w_ost:
                    s.wait_ge(sO.h, w_ost)
                if w_st:
                    s.wait_ge(sV.h, w_st)
                for t in range(4):
                    yt = ytok[:, gq * 4096 + t * 1024:gq * 4096 + (t + 1) * 1024]
                    ot = obuf[:, t * 1024:(t + 1) * 1024]
                    s.activation(ot, yt, AF.Copy, accum_out=stcol(pb, 0, t))
                    s.activation(ot, yt, AF.Square, accum_out=stcol(pb, 1, t))
                act_fence(s).then_inc(sA.h, 1)
            sA.n += 1
            n_st8 = sA.n

            @blk.vector
            def _(v, pb=pb, n_st8=n_st8):
                v.wait_ge(sA.h, n_st8)
                v.tensor_scalar(stc(pb, 0), stc(pb, 0), -1.0 / D, None, ALU.mult)
                v.tensor_scalar(stc(pb, 1), stc(pb, 1), 1.0 / D, None, ALU.mult)
                v.tensor_tensor(stc(pb, 2), stc(pb, 0), stc(pb, 0), ALU.mult)
                pad(v)
                v.tensor_tensor(stc(pb, 1), stc(pb, 1), stc(pb, 2), ALU.subtract)
                pad(v)
                v.tensor_scalar(stc(pb, 1), stc(pb, 1), EPS, None,
                                ALU.add).then_inc(sV.h, 1)
            sV.n += 1
            n_ty = sV.n

            @blk.scalar
            def _(s, pb=pb, gq=gq, n_ty=n_ty):
                s.wait_ge(sV.h, n_ty)
                rsqrt(s, stc(pb, 2), stc(pb, 3), stc(pb, 1))
                s.drain()
                mi = None
                for t in range(4):
                    mi = s.activation(obuf[:, t * 1024:(t + 1) * 1024],
                                      ytok[:, gq * 4096 + t * 1024:
                                           gq * 4096 + (t + 1) * 1024],
                                      AF.Identity, bias=stcol(pb, 0, t), scale=1.0)
                mi.then_inc(sA.h, 1)
            sA.n += 1
            n_t1 = sA.n

            @blk.vector
            def _(v, pb=pb, n_t1=n_t1, n_xl=n_xl):
                v.wait_ge(sA.h, n_t1)
                v.wait_ge(sQ.h, n_xl)
                v.tensor_tensor(dap(obuf, 0, [[4096, 128], [1024, 4], [1, 1024]]),
                                dap(obuf, 0, [[4096, 128], [1024, 4], [1, 1024]]),
                                stbc(pb, 2), ALU.mult)
                v.tensor_tensor(obuf[:, :], obuf[:, :], x1f[:, :],
                                ALU.add).then_inc(sV.h, 1)
            sV.n += 1
            fin_all.append(sV.n)
            n_ob = sV.n

            @blk.sync
            def _(s, Tb=Tb, gq=gq, n_ob=n_ob):
                s.wait_ge(sV.h, n_ob)
                s.dma_start(dap(out, (Tb * 8 + gq * 4) * 131072,
                                [[1024, 128], [131072, 4], [1, 1024]]),
                            obuf[:, :]).then_inc(sO.h, 16)
            sO.n += 16
            out_st.append(sO.n)
        fin_prev_block = fin_all[-1]
        ffn2_mm_prev = sM.n

    @blk.sync
    def _(s):
        s.wait_ge(sO.h, sO.n)
        s.wait_ge(sQ.h, sQ.n)
        s.wait_ge(sG.h, sG.n)
        for ctr in (sU, sL, sW[0], sW[1], sZ[0], sZ[1], sX[0], sX[1],
                    sY[0], sY[1], sC[0], sC[1], sT[0], sT[1],
                    sD[0], sD[1], sB[0], sB[1]):
            s.wait_ge(ctr.h, ctr.n)

    e_es.close()
    p_es.close()
    g_es.close()
    return nc


_NC_CACHE = None


def kernel(L0_attn_x, L0_attn_wq, L0_attn_wk, L0_attn_wv, L0_attn_wo,
           L0_ffn_W1, L0_ffn_b1, L0_ffn_W2, L0_ffn_b2):
    global _NC_CACHE
    x = np.ascontiguousarray(np.asarray(L0_attn_x, dtype=np.float32))
    wq = np.ascontiguousarray(np.asarray(L0_attn_wq, dtype=np.float32))
    wk = np.ascontiguousarray(np.asarray(L0_attn_wk, dtype=np.float32))
    wv = np.ascontiguousarray(np.asarray(L0_attn_wv, dtype=np.float32))
    wo = np.ascontiguousarray(np.asarray(L0_attn_wo, dtype=np.float32))
    w1 = np.ascontiguousarray(np.asarray(L0_ffn_W1, dtype=np.float32))
    b1 = np.ascontiguousarray(np.asarray(L0_ffn_b1, dtype=np.float32)).reshape(32, 128)
    w2 = np.ascontiguousarray(np.asarray(L0_ffn_W2, dtype=np.float32))
    b2 = np.ascontiguousarray(np.asarray(L0_ffn_b2, dtype=np.float32)).reshape(8, 128)

    if _NC_CACHE is None:
        _NC_CACHE = build()
    nc = _NC_CACHE
    in_maps = [{"x": np.ascontiguousarray(x[i]), "wq": wq, "wk": wk, "wv": wv,
                "wo": wo, "w1": w1, "b1": b1, "w2": w2, "b2": b2}
               for i in range(8)]
    res = run_bass_kernel_spmd(nc, in_maps, core_ids=list(range(8)))
    return np.stack([r["out"] for r in res.results], axis=0)


# version 14
# speedup vs baseline: 1.2741x; 1.0032x over previous
import numpy as np
from contextlib import ExitStack

import concourse.bass as bass
import concourse.mybir as mybir
from concourse.bass_utils import run_bass_kernel_spmd

F32 = mybir.dt.float32
BF16 = mybir.dt.bfloat16
AF = mybir.ActivationFunctionType
ALU = mybir.AluOpType
AX = mybir.AxisListType

T, D, H, DH, FF = 4096, 1024, 16, 64, 4096
NT, NC8, NS = 32, 8, 8
C23 = 12582912.0
C52 = 6755399441055744.0
EPS = 1e-5


def dap(t, off, dims):
    return bass.AP(t, off, [list(d) for d in dims])


class DR:
    _OPS = {"tensor_scalar", "tensor_tensor", "tensor_copy", "reduce_sum",
            "reduce_max", "tensor_reduce", "reciprocal", "memset",
            "scalar_tensor_tensor", "tensor_single_scalar", "activation"}

    def __init__(self, e):
        self._e = e

    def __getattr__(self, name):
        f = getattr(self._e, name)
        if name in DR._OPS:
            def g(*a, **k):
                r = f(*a, **k)
                self._e.drain()
                return r
            return g
        return f


def build():
    nc = bass.Bass("TRN2", target_bir_lowering=False,
                   detect_race_conditions=False)

    x_in = nc.dram_tensor("x", [T, D], F32, kind="ExternalInput")
    w_q = nc.dram_tensor("wq", [D, D], F32, kind="ExternalInput")
    w_k = nc.dram_tensor("wk", [D, D], F32, kind="ExternalInput")
    w_v = nc.dram_tensor("wv", [D, D], F32, kind="ExternalInput")
    w_o = nc.dram_tensor("wo", [D, D], F32, kind="ExternalInput")
    w_1 = nc.dram_tensor("w1", [FF, D], F32, kind="ExternalInput")
    b_1 = nc.dram_tensor("b1", [32, 128], F32, kind="ExternalInput")
    w_2 = nc.dram_tensor("w2", [D, FF], F32, kind="ExternalInput")
    b_2 = nc.dram_tensor("b2", [8, 128], F32, kind="ExternalInput")
    out = nc.dram_tensor("out", [T, D], F32, kind="ExternalOutput")

    s_ctx = nc.dram_tensor("s_ctx", [T, D], BF16)
    s_x1 = nc.dram_tensor("s_x1", [T, D], F32)
    s_w1t = nc.dram_tensor("s_w1t", [32, 8, 128, 128], BF16)
    s_w2t = nc.dram_tensor("s_w2t", [8, 32, 128, 128], BF16)

    g_es = ExitStack()

    def sb(stk, name, shape, dtype, **kw):
        return stk.enter_context(nc.sbuf_tensor(name, shape, dtype, **kw))

    wsel = sb(g_es, "wsel", [128, 1], F32)
    ones = sb(g_es, "ones", [128, 128], F32)
    kq = sb(g_es, "kq", [128, 64], F32)
    gam = sb(g_es, "gam", [128, 8], F32)
    b1c = sb(g_es, "b1c", [128, 32], F32)
    b2c = sb(g_es, "b2c", [128, 16], F32)
    gat = sb(g_es, "gat", [128, 128], BF16)
    gatT = sb(g_es, "gatT", [128, 128], BF16, is_transpose_dest=True)
    st = sb(g_es, "st", [128, 128], F32)

    def stc(p, i):
        return st[:, 64 * p + 4 * i:64 * p + 4 * (i + 1)]

    def stcol(p, i, t):
        c = 64 * p + 4 * i + t
        return st[:, c:c + 1]

    def stbc(p, i):
        return dap(st, 64 * p + 4 * i, [[128, 128], [1, 4], [0, 1024]])

    r_es = ExitStack()
    tb1 = sb(r_es, "tb1", [128, 4096], F32)
    tb2 = sb(r_es, "tb2", [128, 4096], F32)

    wt_es = ExitStack()
    wt_q = sb(wt_es, "wt_q", [128, NC8 * D], BF16, is_transpose_dest=True)

    w_es = ExitStack()
    wbig = sb(w_es, "wbig", [128, 4096], F32)
    qbuf = sb(w_es, "qbuf", [128, 4096], BF16)
    wtst = sb(w_es, "wtst", [128, 4096], BF16, is_transpose_dest=True)
    bk_es = ExitStack()
    wt_k = sb(bk_es, "wt_k", [128, NC8 * D], BF16, is_transpose_dest=True)
    wt_v = sb(bk_es, "wt_v", [128, NC8 * D], BF16, is_transpose_dest=True)
    pre_es = ExitStack()
    wbig2 = sb(pre_es, "wbig2", [128, 4096], F32)
    qbuf2 = sb(pre_es, "qbuf2", [128, 4096], BF16)

    p_es = ExitStack()
    ps_g = p_es.enter_context(nc.psum_tensor("ps_g", [128, 512], F32))
    ps_a = p_es.enter_context(nc.psum_tensor("ps_a", [128, 1024], F32))
    ps_b = p_es.enter_context(nc.psum_tensor("ps_b", [128, 1024], F32))
    ps_c = p_es.enter_context(nc.psum_tensor("ps_c", [128, 512], F32))
    ps_d = p_es.enter_context(nc.psum_tensor("ps_d", [128, 512], F32))

    class Ctr:
        def __init__(self, name):
            self.h = g_es.enter_context(nc.semaphore(name))
            self.n = 0
    sQ, sG, sV, sA, sM, sS, sP = (Ctr("sQ"), Ctr("sG"), Ctr("sV"), Ctr("sA"),
                                  Ctr("sM"), Ctr("sS"), Ctr("sP"))
    sU = Ctr("sU")
    sW = [Ctr("sW0"), Ctr("sW1")]
    sZ = [Ctr("sZ0"), Ctr("sZ1")]
    sX = [Ctr("sX0"), Ctr("sX1")]
    sY = [Ctr("sY0"), Ctr("sY1")]
    sC = [Ctr("sC0"), Ctr("sC1")]
    sL = Ctr("sL")
    sT = [Ctr("sT0"), Ctr("sT1")]
    sO = Ctr("sO")
    sD = [Ctr("sD0"), Ctr("sD1")]
    sB = [Ctr("sB0"), Ctr("sB1")]

    blk = g_es.enter_context(nc.Block())

    def pad(v):
        v.tensor_scalar(st[:, 120:124], st[:, 124:128], 1.0, None, ALU.mult)

    def rnd_inplace(v):
        v.tensor_scalar(tb1[:, :], tb2[:, :], C23, C23, ALU.add, ALU.subtract)
        v.tensor_scalar(tb2[:, :], tb2[:, :], C52, C52, ALU.add, ALU.subtract)
        v.tensor_tensor(tb2[:, :], tb2[:, :], tb1[:, :], ALU.subtract)
        return v.scalar_tensor_tensor(tb1[:, :], tb2[:, :], wsel[:, 0:1],
                                      tb1[:, :], ALU.mult, ALU.add)

    def act_fence(s):
        s.drain()
        return s.activation(kq[:, 63:64], wsel[:, 0:1], AF.Copy)

    def rsqrt(s, dst, tmp, src):
        s.activation(tmp, src, AF.Ln)
        s.drain()
        return s.activation(dst, tmp, AF.Exp, bias=0.0, scale=-0.5)

    @blk.scalar
    def _(s):
        s.add_instruction(mybir.InstLoadActFuncSet(
            name=nc.get_next_instruction_name(), act_func_set_id=6, ins=[], outs=[]))

    @blk.vector
    def _(v):
        v.memset(wsel[:, :], 2.0 ** -30)
        v.drain()
        v.tensor_scalar(wsel[:, :], wsel[:, :], 1.0, 1.0, ALU.add, ALU.subtract)
        v.drain()
        v.tensor_scalar(wsel[:, :], wsel[:, :], 1e30, 1.0, ALU.mult, ALU.min)
        v.drain()
        v.memset(ones[:, :], 1.0)
        v.memset(st[:, :], 0.0)
        v.memset(gat[:, :], 0.0)
        v.memset(wbig[0:32, 128:256], 0.0).then_inc(sV.h, 1)
    sV.n += 1
    n_ms0 = sV.n

    @blk.gpsimd
    def _(g):
        g.wait_ge(sV.h, n_ms0)
        g.dma_start(dap(wbig, 0, [[4096, 32], [1, 128]]),
                    dap(b_1, 0, [[128, 32], [1, 128]])).then_inc(sG.h, 16)
        g.dma_start(dap(wbig, 128, [[4096, 8], [1, 128]]),
                    dap(b_2, 0, [[128, 8], [1, 128]])).then_inc(sG.h, 16)
    sG.n += 32
    n_bld = sG.n

    @blk.vector
    def _(v):
        v.wait_ge(sG.h, n_bld)
        v.tensor_copy(gat[0:32, 0:128], wbig[0:32, 0:128])
        v.tensor_tensor(qbuf[0:32, 0:128], wbig[0:32, 0:128],
                        gat[0:32, 0:128], ALU.subtract)
        v.tensor_copy(qbuf[0:16, 128:256], wbig[0:16, 128:256])
        v.tensor_tensor(qbuf[0:16, 256:384], wbig[0:16, 128:256],
                        qbuf[0:16, 128:256], ALU.subtract).then_inc(sV.h, 1)
    sV.n += 1
    n_bv = sV.n

    @blk.sync
    def _(s):
        s.wait_ge(sV.h, n_bv)
        s.dma_start(gatT[:, 0:32], gat[0:32, 0:128], transpose=True).then_inc(sU.h, 16)
        s.dma_start(wtst[:, 0:32], qbuf[0:32, 0:128], transpose=True).then_inc(sU.h, 16)
        s.dma_start(gatT[:, 32:48], qbuf[0:16, 128:256], transpose=True).then_inc(sU.h, 16)
        s.dma_start(wtst[:, 32:48], qbuf[0:16, 256:384], transpose=True).then_inc(sU.h, 16)
    sU.n += 64
    n_bt = sU.n

    @blk.vector
    def _(v):
        v.wait_ge(sU.h, n_bt)
        v.tensor_tensor(b1c[:, :], gatT[:, 0:32], wtst[:, 0:32], ALU.add)
        v.tensor_tensor(b2c[:, :], gatT[:, 32:48], wtst[:, 32:48],
                        ALU.add).then_inc(sV.h, 1)
    sV.n += 1

    slotV = {}
    slotQ = {}
    slotW = {}

    def prep_attn(w_dram, dst_wt, gcol, tguard=0, wb=None, qb=None, extraq=()):
        wb = wbig if wb is None else wb
        qb = qbuf if qb is None else qb
        sd = sD[0] if wb is wbig else sD[1]
        wfree = slotV.get(id(wb), 0)
        for half in range(2):
            red = wfree if half == 0 else sV.n

            @blk.gpsimd
            def _(g, half=half, red=red):
                if red:
                    g.wait_ge(sV.h, red)
                g.dma_start(dap(wb, 0, [[4096, 128], [1024, 4], [1, 1024]]),
                            dap(w_dram, half * 524288,
                                [[1024, 128], [131072, 4], [1, 1024]])).then_inc(sd.h, 16)
            sd.n += 16
            ld = sd.n

            @blk.vector
            def _(v, half=half, ld=ld):
                v.wait_ge(sd.h, ld)
                v.reduce_sum(kq[:, 40 + half:41 + half], wb[:, :], AX.X,
                             apply_absolute_value=True).then_inc(sV.h, 1)
            sV.n += 1

        @blk.vector
        def _(v):
            v.drain()
            v.tensor_tensor(kq[:, 42:43], kq[:, 40:41], kq[:, 41:42],
                            ALU.add).then_inc(sV.h, 1)
        sV.n += 1
        pd = sV.n

        @blk.tensor
        def _(t, pd=pd):
            t.wait_ge(sV.h, pd)
            t.matmul(ps_g[:, 0:1], ones[:, :], kq[:, 42:43],
                     start=True, stop=True).then_inc(sM.h, 1)
        sM.n += 1
        md = sM.n

        @blk.vector
        def _(v, md=md):
            v.wait_ge(sM.h, md)
            v.tensor_scalar(kq[:, 43:44], ps_g[:, 0:1], 1.0 / (D * D), EPS,
                            ALU.mult, ALU.max)
            pad(v)
            v.tensor_copy(gam[:, gcol:gcol + 1], kq[:, 43:44])
            v.reciprocal(kq[:, 44:45], kq[:, 43:44]).then_inc(sV.h, 1)
        sV.n += 1

        qfree = slotQ.get(id(qb), 0) or 0
        for half in range(2):
            qd0 = sV.n
            tp0 = sU.n
            actg = sA.n

            @blk.gpsimd
            def _(g, half=half, qd0=qd0):
                g.wait_ge(sV.h, qd0)
                g.dma_start(dap(wb, 0, [[4096, 128], [1024, 4], [1, 1024]]),
                            dap(w_dram, half * 524288,
                                [[1024, 128], [131072, 4], [1, 1024]])).then_inc(sd.h, 16)
            sd.n += 16
            ld = sd.n

            @blk.vector
            def _(v, half=half, ld=ld, tp0=tp0, actg=actg, qfree=qfree,
                  extraq=extraq):
                v.wait_ge(sd.h, ld)
                if actg:
                    v.wait_ge(sA.h, actg)
                qg = tp0 if half == 1 else qfree
                if qg:
                    v.wait_ge(sU.h, qg)
                if half == 0:
                    for sem_, val_ in extraq:
                        if val_:
                            v.wait_ge(sem_.h, val_)
                v.tensor_scalar(tb2[:, :], wb[:, :], kq[:, 44:45], None, ALU.mult)
                v.tensor_scalar(tb1[:, :], tb2[:, :], C23, C23, ALU.add, ALU.subtract)
                v.tensor_scalar(wb[:, :], tb2[:, :], C52, C52, ALU.add, ALU.subtract)
                v.tensor_tensor(wb[:, :], wb[:, :], tb1[:, :], ALU.subtract)
                v.scalar_tensor_tensor(tb1[:, :], wb[:, :], wsel[:, 0:1],
                                       tb1[:, :], ALU.mult, ALU.add)
                v.tensor_scalar(qb[:, :], tb1[:, :], -1.0, 1.0,
                                ALU.max, ALU.min).then_inc(sV.h, 1)
            sV.n += 1
            qd = sV.n

            @blk.sync
            def _(s, half=half, qd=qd, tguard=tguard):
                s.wait_ge(sV.h, qd)
                if tguard and half == 0:
                    s.wait_ge(sM.h, tguard)
                for cl in range(4):
                    c = half * 4 + cl
                    for j in range(NC8):
                        s.dma_start(dst_wt[:, j * D + c * 128:j * D + (c + 1) * 128],
                                    qb[:, cl * D + j * 128:cl * D + (j + 1) * 128],
                                    transpose=True).then_inc(sU.h, 16)
            sU.n += 16 * 32
        slotV[id(wb)] = sV.n
        slotQ[id(qb)] = sU.n
        return sU.n

    def prep_ffn(w_dram, spill, layout):
        wv0 = slotV.get(id(wbig), 0)
        for grp in range(8):
            red = sA.n

            @blk.gpsimd
            def _(g, grp=grp, red=red, wv0=wv0):
                if grp == 0 and wv0:
                    g.wait_ge(sV.h, wv0)
                if red:
                    g.wait_ge(sA.h, red)
                if layout == "w1":
                    g.dma_start(dap(wbig, 0, [[4096, 128], [1024, 4], [1, 1024]]),
                                dap(w_dram, grp * 524288,
                                    [[1024, 128], [131072, 4], [1, 1024]])
                                ).then_inc(sD[0].h, 16)
                else:
                    g.dma_start(dap(wbig, 0, [[4096, 128], [1, 4096]]),
                                dap(w_dram, grp * 524288, [[4096, 128], [1, 4096]])
                                ).then_inc(sD[0].h, 16)
            sD[0].n += 16
            ld = sD[0].n

            @blk.scalar
            def _(s, grp=grp, ld=ld):
                s.wait_ge(sD[0].h, ld)
                for c in range(4):
                    s.activation(sqr[:, :], wbig[:, c * 1024:(c + 1) * 1024],
                                 AF.Abs, accum_out=kq[:, 50 + c:51 + c])
                act_fence(s).then_inc(sA.h, 1)
            sA.n += 1
            n_ab = sA.n

            @blk.gpsimd
            def _(g, grp=grp, n_ab=n_ab):
                g.wait_ge(sA.h, n_ab)
                g.tensor_tensor(kq[:, 54:55], kq[:, 50:51], kq[:, 51:52], ALU.add)
                g.tensor_tensor(kq[:, 55:56], kq[:, 52:53], kq[:, 53:54], ALU.add)
                g.tensor_tensor(kq[:, 54:55], kq[:, 54:55], kq[:, 55:56], ALU.add)
                if grp == 0:
                    g.tensor_copy(kq[:, 46:47], kq[:, 54:55]).then_inc(sP.h, 1)
                else:
                    g.tensor_tensor(kq[:, 46:47], kq[:, 46:47], kq[:, 54:55],
                                    ALU.add).then_inc(sP.h, 1)
            sP.n += 1
            yield
        pd = sP.n

        @blk.tensor
        def _(t, pd=pd):
            t.wait_ge(sP.h, pd)
            t.matmul(ps_g[:, 0:1], ones[:, :], kq[:, 46:47],
                     start=True, stop=True).then_inc(sM.h, 1)
        sM.n += 1
        md = sM.n

        @blk.vector
        def _(v, md=md):
            v.wait_ge(sM.h, md)
            v.tensor_scalar(kq[:, 43:44], ps_g[:, 0:1], 1.0 / (FF * D), EPS,
                            ALU.mult, ALU.add)
            pad(v)
            v.reciprocal(kq[:, 44:45], kq[:, 43:44]).then_inc(sV.h, 1)
        sV.n += 1
        yield

        p1a = sA.n
        qb0 = slotQ.get(id(qbuf), 0)
        wt0 = slotW.get(id(wtst), 0)
        qdone = [0, 0]
        tpdone = [0, 0]
        spdone = [0, 0]
        for grp in range(16):
            sl = grp % 2
            so = sl * 2048
            w_qd = qdone[sl]
            actg = sA.n

            @blk.gpsimd
            def _(g, grp=grp, so=so, sl=sl, w_qd=w_qd, p1a=p1a):
                if grp < 2:
                    g.wait_ge(sA.h, p1a)
                if w_qd:
                    g.wait_ge(sV.h, w_qd)
                if layout == "w1":
                    g.dma_start(dap(wbig, so, [[4096, 128], [1024, 2], [1, 1024]]),
                                dap(w_dram, grp * 262144,
                                    [[1024, 128], [131072, 2], [1, 1024]])
                                ).then_inc(sD[sl].h, 16)
                else:
                    g.dma_start(dap(wbig, so, [[4096, 128], [1, 2048]]),
                                dap(w_dram, (grp // 2) * 524288 + (grp % 2) * 2048,
                                    [[4096, 128], [1, 2048]])
                                ).then_inc(sD[sl].h, 16)
            sD[sl].n += 16
            ld = sD[sl].n
            w_tp = tpdone[sl]

            @blk.vector
            def _(v, grp=grp, so=so, sl=sl, ld=ld, actg=actg, w_tp=w_tp, qb0=qb0):
                v.wait_ge(sD[sl].h, ld)
                if actg:
                    v.wait_ge(sA.h, actg)
                if w_tp:
                    v.wait_ge(sW[sl].h, w_tp)
                elif qb0:
                    v.wait_ge(sU.h, qb0)
                w_ = wbig[:, so:so + 2048]
                q_ = qbuf[:, so:so + 2048]
                t1 = tb1[:, 0:2048]
                t2 = tb2[:, 0:2048]
                v.tensor_scalar(t2, w_, kq[:, 44:45], None, ALU.mult)
                v.tensor_scalar(t1, t2, C23, C23, ALU.add, ALU.subtract)
                v.tensor_scalar(w_, t2, C52, C52, ALU.add, ALU.subtract)
                v.tensor_tensor(w_, w_, t1, ALU.subtract)
                v.scalar_tensor_tensor(t1, w_, wsel[:, 0:1], t1, ALU.mult, ALU.add)
                v.tensor_scalar(q_, t1, -1.0, 1.0, ALU.max, ALU.min).then_inc(sV.h, 1)
            sV.n += 1
            qdone[sl] = sV.n
            qd = sV.n
            w_sp = spdone[sl]

            @blk.sync
            def _(s, grp=grp, so=so, sl=sl, qd=qd, w_sp=w_sp):
                s.wait_ge(sV.h, qd)
                if w_sp:
                    s.wait_ge(sZ[sl].h, w_sp)
                for bi in range(16):
                    s.dma_start(wtst[:, so + bi * 128:so + (bi + 1) * 128],
                                qbuf[:, so + bi * 128:so + (bi + 1) * 128],
                                transpose=True).then_inc(sW[sl].h, 16)
            sW[sl].n += 16 * 16
            tp = sW[sl].n
            tpdone[sl] = tp

            @blk.sync
            def _(s, grp=grp, so=so, sl=sl, tp=tp):
                s.wait_ge(sW[sl].h, tp)
                if layout == "w1":
                    s.dma_start(dap(spill, grp * 2 * 131072,
                                    [[128, 128], [131072, 2], [16384, 8], [1, 128]]),
                                dap(wtst, so, [[4096, 128], [1024, 2], [128, 8], [1, 128]])
                                ).then_inc(sZ[sl].h, 16)
                else:
                    s.dma_start(dap(spill, grp * 262144,
                                    [[128, 128], [16384, 16], [1, 128]]),
                                dap(wtst, so, [[4096, 128], [128, 16], [1, 128]])
                                ).then_inc(sZ[sl].h, 16)
            sZ[sl].n += 16
            spdone[sl] = sZ[sl].n
            yield
        slotV[id(wbig)] = sV.n
        slotQ[id(qbuf)] = None
        slotW[id(wtst)] = None

    prep_attn(w_v, wt_v, 2, wb=wbig2, qb=qbuf2)
    prep_attn(w_q, wt_q, 0)
    n_wprep = prep_attn(w_k, wt_k, 1, wb=wbig2, qb=qbuf2)
    pre_q_tp = n_wprep
    pre_q_v = slotV[id(wbig2)]
    pre_es.close()

    a_es = ExitStack()
    xa = sb(a_es, "xa", [128, 4096], F32)
    sqr = sb(a_es, "sqr", [128, 1024], F32)
    xqs = sb(a_es, "xqs", [128, 2 * 4096], BF16)
    xqTt = sb(a_es, "xqTt", [128, 2 * D], BF16, is_transpose_dest=True)
    q_sb = sb(a_es, "q_sb", [128, D], BF16)
    k_sb = sb(a_es, "k_sb", [128, D], BF16)
    v_sb = sb(a_es, "v_sb", [128, D], BF16)
    pr_g = sb(a_es, "pr_g", [128, 8192], BF16)
    pr_v = sb(a_es, "pr_v", [128, 8192], BF16)
    sc_sb = sb(a_es, "sc_sb", [128, 256], F32)
    mx_sb = sb(a_es, "mx_sb", [128, 16], F32)
    e_sb = sb(a_es, "e_sb", [128, 256], F32)
    sm_sb = sb(a_es, "sm_sb", [128, 16], F32)
    at_sb = sb(a_es, "at_sb", [128, 256], BF16)
    ctx_f = sb(a_es, "ctx_f", [128, D], F32)
    ctx_sb = sb(a_es, "ctx_sb", [128, 2 * D], BF16)

    pbanks = [(ps_a, 0), (ps_a, 512), (ps_b, 0), (ps_b, 512), (ps_c, 0), (ps_d, 0)]
    a_t1a, a_rnd, a_xq, s_tpq, a_xld = [], [], [], [], []
    ev_c, ctx_v, ctx_sp, mm_c, a_scd = [], [], [], [], []
    ev_v_c = {}

    prep_gens = [prep_ffn(w_1, s_w1t, "w1"), prep_ffn(w_2, s_w2t, "w2")]

    def pull_prep(k):
        while k > 0 and prep_gens:
            try:
                next(prep_gens[0])
                k -= 1
            except StopIteration:
                prep_gens.pop(0)

    def emit_xload(S):
        w_t1a = a_t1a[S - 1] if S >= 1 else 0

        @blk.gpsimd
        def _(g, S=S, w_t1a=w_t1a):
            if S == 0:
                g.wait_ge(sU.h, pre_q_tp)
                g.wait_ge(sV.h, pre_q_v)
            if w_t1a:
                g.wait_ge(sA.h, w_t1a)
            g.dma_start(dap(xa, 0, [[4096, 128], [1024, 4], [1, 1024]]),
                        dap(x_in, S * 4 * 131072,
                            [[1024, 128], [131072, 4], [1, 1024]])).then_inc(sG.h, 16)
        sG.n += 16
        a_xld.append(sG.n)

    emit_xload(0)
    for S in range(NS):
        pb = S % 2
        n_xld = a_xld[S]
        w_rnd2 = a_rnd[S - 2] if S >= 2 else 0

        @blk.scalar
        def _(s, pb=pb, n_xld=n_xld, w_rnd2=w_rnd2):
            s.wait_ge(sG.h, n_xld)
            if w_rnd2:
                s.wait_ge(sV.h, w_rnd2)
            for t in range(4):
                xt = xa[:, t * 1024:(t + 1) * 1024]
                s.activation(sqr[:, :], xt, AF.Copy, accum_out=stcol(pb, 0, t))
                s.activation(sqr[:, :], xt, AF.Square, accum_out=stcol(pb, 1, t))
            act_fence(s).then_inc(sA.h, 1)
        sA.n += 1
        n_st8 = sA.n

        @blk.vector
        def _(v, pb=pb, n_st8=n_st8):
            v.wait_ge(sA.h, n_st8)
            v.tensor_scalar(stc(pb, 0), stc(pb, 0), -1.0 / D, None, ALU.mult)
            v.tensor_scalar(stc(pb, 1), stc(pb, 1), 1.0 / D, None, ALU.mult)
            v.tensor_tensor(stc(pb, 2), stc(pb, 0), stc(pb, 0), ALU.mult)
            pad(v)
            v.tensor_tensor(stc(pb, 1), stc(pb, 1), stc(pb, 2), ALU.subtract)
            pad(v)
            v.tensor_scalar(stc(pb, 1), stc(pb, 1), 1.0 + 1e-6, 1e-11,
                            ALU.mult, ALU.add).then_inc(sV.h, 1)
        sV.n += 1
        n_ty1 = sV.n

        @blk.scalar
        def _(s, pb=pb, n_ty1=n_ty1):
            s.wait_ge(sV.h, n_ty1)
            rsqrt(s, stc(pb, 2), stc(pb, 3), stc(pb, 1))
            s.drain()
            mi = None
            for t in range(4):
                mi = s.activation(tb2[:, t * 1024:(t + 1) * 1024],
                                  xa[:, t * 1024:(t + 1) * 1024], AF.Identity,
                                  bias=stcol(pb, 0, t), scale=1.0)
            mi.then_inc(sA.h, 1)
        sA.n += 1
        a_t1a.append(sA.n)
        n_t1a = sA.n

        @blk.vector
        def _(v, pb=pb, n_t1a=n_t1a):
            v.wait_ge(sA.h, n_t1a)
            v.tensor_reduce(stc(pb, 1),
                            dap(tb2, 0, [[4096, 128], [1024, 4], [1, 1024]]),
                            AX.X, ALU.max, apply_absolute_value=True)
            pad(v)
            v.tensor_tensor(stc(pb, 1), stc(pb, 1), stc(pb, 2), ALU.mult)
            pad(v)
            v.tensor_scalar(stc(pb, 1), stc(pb, 1), EPS, None, ALU.max)
            pad(v)
            v.reciprocal(stc(pb, 3), stc(pb, 1))
            pad(v)
            v.tensor_scalar(stc(pb, 3), stc(pb, 3), 127.0, None, ALU.mult)
            v.tensor_scalar(stc(pb, 1), stc(pb, 1), 1.0 / 127.0, None, ALU.mult)
            v.tensor_tensor(stc(pb, 2), stc(pb, 2), stc(pb, 3), ALU.mult)
            pad(v)
            v.tensor_tensor(dap(tb2, 0, [[4096, 128], [1024, 4], [1, 1024]]),
                            dap(tb2, 0, [[4096, 128], [1024, 4], [1, 1024]]),
                            stbc(pb, 2), ALU.mult)
            rnd_inplace(v).then_inc(sV.h, 1)
        sV.n += 1
        a_rnd.append(sV.n)
        n_rnd = sV.n
        w_tpm = mm_c[4 * (S - 2) + 3] if S >= 2 else 0

        @blk.scalar
        def _(s, pb=pb, n_rnd=n_rnd, w_tpm=w_tpm):
            s.wait_ge(sV.h, n_rnd)
            if w_tpm:
                s.wait_ge(sM.h, w_tpm)
            mi = None
            for t in range(4):
                mi = s.activation(xqs[:, pb * 4096 + t * 1024:pb * 4096 + (t + 1) * 1024],
                                  tb1[:, t * 1024:(t + 1) * 1024], AF.Copy,
                                  bias=0.0, scale=stcol(pb, 1, t))
            mi.then_inc(sA.h, 1)
        sA.n += 1
        a_xq.append(sA.n)
        n_xq = sA.n

        for t4 in range(4):
            tt = S * 4 + t4
            tpb = tt % 2
            w_mm = mm_c[tt - 2] if tt >= 2 else 0

            @blk.sync
            def _(s, t4=t4, pb=pb, tpb=tpb, n_xq=n_xq, w_mm=w_mm):
                s.wait_ge(sA.h, n_xq)
                if w_mm:
                    s.wait_ge(sM.h, w_mm)
                for j in range(NC8):
                    s.dma_start(xqTt[:, tpb * D + j * 128:tpb * D + (j + 1) * 128],
                                xqs[:, pb * 4096 + t4 * D + j * 128:
                                    pb * 4096 + t4 * D + (j + 1) * 128],
                                transpose=True).then_inc(sX[tpb].h, 16)
            sX[tpb].n += 16 * NC8
            n_tp = sX[tpb].n
            w_ev = ev_v_c.get(tt - 1, 0) if tt >= 1 else 0

            @blk.tensor
            def _(t, tt=tt, tpb=tpb, n_tp=n_tp, w_ev=w_ev):
                t.wait_ge(sX[tpb].h, n_tp)
                if tt == 0:
                    t.wait_ge(sU.h, n_wprep)
                if w_ev:
                    t.wait_ge(sA.h, w_ev)
                mi = None
                for j in range(NC8):
                    lhs = xqTt[:, tpb * D + j * 128:tpb * D + (j + 1) * 128]
                    for pi, wt in enumerate((wt_q, wt_k, wt_v)):
                        for n in range(2):
                            pb_, off = pbanks[pi * 2 + n]
                            mi = t.matmul(pb_[:, off:off + 512], lhs,
                                          wt[:, j * D + n * 512:j * D + (n + 1) * 512],
                                          start=(j == 0), stop=(j == NC8 - 1))
                mi.then_inc(sM.h, 1)
            sM.n += 1
            mm_c.append(sM.n)
            n_mm = sM.n
            w_qk = a_scd[tt - 1] if tt >= 1 else 0
            w_mix = ctx_v[tt - 1] if tt >= 1 else 0

            @blk.scalar
            def _(s, n_mm=n_mm, w_qk=w_qk):
                s.wait_ge(sM.h, n_mm)
                if w_qk:
                    s.wait_ge(sV.h, w_qk)
                mi = None
                for pi, dst in enumerate((q_sb, k_sb)):
                    for n in range(2):
                        pb_, off = pbanks[pi * 2 + n]
                        mi = s.activation(dst[:, n * 512:(n + 1) * 512],
                                          pb_[:, off:off + 512],
                                          AF.Copy, bias=0.0, scale=gam[:, pi:pi + 1])
                mi.then_inc(sA.h, 1)
            sA.n += 1
            ev_c.append(sA.n)
            n_evc = sA.n

            @blk.scalar
            def _(s, w_mix=w_mix):
                if w_mix:
                    s.wait_ge(sV.h, w_mix)
                mi = None
                for n in range(2):
                    pb_, off = pbanks[4 + n]
                    mi = s.activation(v_sb[:, n * 512:(n + 1) * 512],
                                      pb_[:, off:off + 512],
                                      AF.Copy, bias=0.0, scale=gam[:, 2:3])
                mi.then_inc(sA.h, 1)
            sA.n += 1
            ev_v_c[tt] = sA.n

            @blk.gpsimd
            def _(g, n_evc=n_evc):
                g.wait_ge(sA.h, n_evc)
                g.tensor_tensor(
                    dap(pr_g, 0, [[8192, 128], [1024, 8], [64, 16], [1, 64]]),
                    dap(q_sb, 0, [[D, 128], [64, 8], [0, 16], [1, 64]]),
                    dap(k_sb, 0, [[D, 128], [0, 8], [64, 16], [1, 64]]),
                    ALU.mult).then_inc(sP.h, 1)
            sP.n += 1
            g_sc = sP.n

            @blk.vector
            def _(v, n_evc=n_evc, g_sc=g_sc):
                v.wait_ge(sA.h, n_evc)
                v.tensor_tensor(
                    dap(pr_v, 0, [[8192, 128], [1024, 8], [64, 16], [1, 64]]),
                    dap(q_sb, 512, [[D, 128], [64, 8], [0, 16], [1, 64]]),
                    dap(k_sb, 0, [[D, 128], [0, 8], [64, 16], [1, 64]]),
                    ALU.mult)
                v.tensor_reduce(dap(sc_sb, 128, [[256, 128], [16, 8], [1, 16]]),
                                dap(pr_v, 0,
                                    [[8192, 128], [1024, 8], [64, 16], [1, 64]]),
                                AX.X, ALU.add)
                v.wait_ge(sP.h, g_sc)
                v.tensor_reduce(dap(sc_sb, 0, [[256, 128], [16, 8], [1, 16]]),
                                dap(pr_g, 0,
                                    [[8192, 128], [1024, 8], [64, 16], [1, 64]]),
                                AX.X, ALU.add)
                pad(v)
                v.tensor_reduce(dap(mx_sb, 0, [[16, 128], [1, 16]]),
                                dap(sc_sb, 0, [[256, 128], [16, 16], [1, 16]]),
                                AX.X, ALU.max)
                pad(v)
                v.tensor_tensor(sc_sb[:, :], sc_sb[:, :],
                                dap(mx_sb, 0, [[16, 128], [1, 16], [0, 16]]),
                                ALU.subtract).then_inc(sV.h, 1)
            sV.n += 1
            a_scd.append(sV.n)
            n_scd = sV.n

            @blk.scalar
            def _(s, n_scd=n_scd):
                s.wait_ge(sV.h, n_scd)
                s.activation(e_sb[:, :], sc_sb[:, :], AF.Exp, bias=0.0,
                             scale=0.125).then_inc(sA.h, 1)
            sA.n += 1
            n_exp = sA.n

            @blk.vector
            def _(v, n_exp=n_exp):
                v.wait_ge(sA.h, n_exp)
                v.tensor_reduce(dap(sm_sb, 0, [[16, 128], [1, 16]]),
                                dap(e_sb, 0, [[256, 128], [16, 16], [1, 16]]),
                                AX.X, ALU.add)
                pad(v)
                v.reciprocal(sm_sb[:, :], sm_sb[:, :])
                pad(v)
                v.tensor_tensor(at_sb[:, :], e_sb[:, :],
                                dap(sm_sb, 0, [[16, 128], [1, 16], [0, 16]]),
                                ALU.mult).then_inc(sV.h, 1)
            sV.n += 1
            n_at = sV.n

            @blk.vector
            def _(v, n_evv=ev_v_c[tt]):
                v.wait_ge(sA.h, n_evv)
                pad(v)
                for hh in range(2):
                    v.tensor_tensor(
                        dap(pr_v, 0, [[8192, 128], [1024, 8], [16, 64], [1, 16]]),
                        dap(at_sb, hh * 128, [[256, 128], [16, 8], [0, 64], [1, 16]]),
                        dap(v_sb, 0, [[D, 128], [0, 8], [1, 64], [64, 16]]),
                        ALU.mult)
                    v.tensor_reduce(dap(ctx_f, hh * 512, [[D, 128], [64, 8], [1, 64]]),
                                    dap(pr_v, 0,
                                        [[8192, 128], [1024, 8], [16, 64], [1, 16]]),
                                    AX.X, ALU.add).then_inc(sV.h, 1) if hh == 1 else                     v.tensor_reduce(dap(ctx_f, hh * 512, [[D, 128], [64, 8], [1, 64]]),
                                    dap(pr_v, 0,
                                        [[8192, 128], [1024, 8], [16, 64], [1, 16]]),
                                    AX.X, ALU.add)
            sV.n += 1
            ctx_v.append(sV.n)
            n_cv = sV.n
            w_ctx = ctx_sp[tt - 2] if tt >= 2 else 0

            @blk.scalar
            def _(s, tpb=tpb, n_cv=n_cv, w_ctx=w_ctx):
                s.wait_ge(sV.h, n_cv)
                if w_ctx:
                    s.wait_ge(sY[tpb].h, w_ctx)
                s.activation(ctx_sb[:, tpb * D:(tpb + 1) * D], ctx_f[:, :],
                             AF.Copy).then_inc(sA.h, 1)
            sA.n += 1
            n_cc = sA.n

            @blk.gpsimd
            def _(g, tt=tt, tpb=tpb, n_cc=n_cc):
                g.wait_ge(sA.h, n_cc)
                g.dma_start(dap(s_ctx, tt * 128 * 64,
                                [[64, 128], [262144, 16], [1, 64]]),
                            dap(ctx_sb, tpb * D,
                                [[2 * D, 128], [64, 16], [1, 64]])).then_inc(sY[tpb].h, 16)
            sY[tpb].n += 16
            ctx_sp.append(sY[tpb].n)
        s_tpq.append(sQ.n)
        if S + 1 < NS:
            emit_xload(S + 1)
        pull_prep(6)

    while prep_gens:
        pull_prep(8)

    n_abc_mm = sM.n
    prep_attn(w_o, wt_q, 3, tguard=n_abc_mm,
              extraq=((sW[0], sW[0].n), (sW[1], sW[1].n)))
    n_prep_u = sU.n
    n_prep_g = sG.n
    a_es.close()
    bk_es.close()
    w_es.close()

    xln2T = sb(g_es, "xln2T", [128, NC8 * T], BF16, is_transpose_dest=True,
               side="right")
    d_es = ExitStack()
    ctd = sb(d_es, "ctd", [128, 2 * 4096], BF16)
    xd = sb(d_es, "xd", [128, 4096], F32)
    sqd = sb(d_es, "sqd", [128, 1024], F32)
    cqs = sb(d_es, "cqs", [128, 4096], BF16)
    cqT = sb(d_es, "cqT", [128, 2 * D], BF16, is_transpose_dest=True)
    aod = sb(d_es, "aod", [128, 4096], F32)
    x1d = sb(d_es, "x1d", [128, 4096], F32)
    xl2 = sb(d_es, "xl2", [128, 4096], BF16)

    d_rnd, d_x1, d_fin, d_xsp, d_ld = [], [], [], [], []
    d_tpq_s, d_tps_s, d_ltq_s, d_lts_s = [], [], [], []
    d_mm, d_aev = [], []

    def emit_dload(S):
        pb = S % 2
        ofs = pb * 4096
        w_ctd = d_rnd[S - 2] if S >= 2 else 0
        w_xd = d_x1[S - 1] if S >= 1 else 0

        @blk.gpsimd
        def _(g, S=S, ofs=ofs, w_ctd=w_ctd, w_xd=w_xd):
            if S == 0:
                g.wait_ge(sY[0].h, sY[0].n)
                g.wait_ge(sY[1].h, sY[1].n)
                g.wait_ge(sU.h, n_prep_u)
                g.wait_ge(sG.h, n_prep_g)
                g.wait_ge(sW[0].h, sW[0].n)
                g.wait_ge(sW[1].h, sW[1].n)
                g.wait_ge(sZ[0].h, sZ[0].n)
                g.wait_ge(sZ[1].h, sZ[1].n)
                g.wait_ge(sX[0].h, sX[0].n)
                g.wait_ge(sX[1].h, sX[1].n)
            if w_ctd:
                g.wait_ge(sV.h, w_ctd)
            g.dma_start(dap(ctd, ofs, [[2 * 4096, 128], [1024, 4], [1, 1024]]),
                        dap(s_ctx, S * 4 * 131072,
                            [[1024, 128], [131072, 4], [1, 1024]])).then_inc(sG.h, 16)
            if w_xd:
                g.wait_ge(sV.h, w_xd)
            g.dma_start(dap(xd, 0, [[4096, 128], [1024, 4], [1, 1024]]),
                        dap(x_in, S * 4 * 131072,
                            [[1024, 128], [131072, 4], [1, 1024]])).then_inc(sG.h, 16)
        sG.n += 32
        d_ld.append(sG.n)

    emit_dload(0)
    for S in range(NS):
        pb = S % 2
        ofs = pb * 4096
        n_ld = d_ld[S]
        w_st = d_fin[S - 2] if S >= 2 else 0

        @blk.scalar
        def _(s, pb=pb, ofs=ofs, n_ld=n_ld, w_st=w_st):
            s.wait_ge(sG.h, n_ld)
            if w_st:
                s.wait_ge(sV.h, w_st)
            for t in range(4):
                s.activation(sqd[:, :], ctd[:, ofs + t * 1024:ofs + (t + 1) * 1024],
                             AF.Square, accum_out=stcol(pb, 0, t))
            act_fence(s).then_inc(sA.h, 1)
        sA.n += 1
        n_st4 = sA.n

        @blk.vector
        def _(v, pb=pb, n_st4=n_st4):
            v.wait_ge(sA.h, n_st4)
            v.tensor_scalar(stc(pb, 0), stc(pb, 0), 1.0 / D, 1e-6,
                            ALU.mult, ALU.add).then_inc(sV.h, 1)
        sV.n += 1
        n_ty1 = sV.n

        @blk.scalar
        def _(s, pb=pb, n_ty1=n_ty1):
            s.wait_ge(sV.h, n_ty1)
            rsqrt(s, stc(pb, 2), stc(pb, 3), stc(pb, 0)).then_inc(sA.h, 1)
        sA.n += 1
        n_rr = sA.n

        @blk.vector
        def _(v, pb=pb, ofs=ofs, n_rr=n_rr):
            v.wait_ge(sA.h, n_rr)
            v.tensor_reduce(stc(pb, 1),
                            dap(ctd, ofs, [[2 * 4096, 128], [1024, 4], [1, 1024]]),
                            AX.X, ALU.max, apply_absolute_value=True)
            pad(v)
            v.tensor_tensor(stc(pb, 1), stc(pb, 1), stc(pb, 2), ALU.mult)
            pad(v)
            v.tensor_scalar(stc(pb, 1), stc(pb, 1), EPS, None, ALU.max)
            pad(v)
            v.reciprocal(stc(pb, 3), stc(pb, 1))
            pad(v)
            v.tensor_scalar(stc(pb, 3), stc(pb, 3), 127.0, None, ALU.mult)
            v.tensor_scalar(stc(pb, 1), stc(pb, 1), 1.0 / 127.0, None, ALU.mult)
            v.tensor_tensor(stc(pb, 2), stc(pb, 2), stc(pb, 3), ALU.mult)
            pad(v)
            v.tensor_tensor(dap(tb2, 0, [[4096, 128], [1024, 4], [1, 1024]]),
                            dap(ctd, ofs, [[2 * 4096, 128], [1024, 4], [1, 1024]]),
                            stbc(pb, 2), ALU.mult)
            rnd_inplace(v).then_inc(sV.h, 1)
        sV.n += 1
        d_rnd.append(sV.n)
        n_rnd = sV.n
        w_tpm = d_mm[4 * (S - 1) + 3] if S >= 1 else 0

        @blk.scalar
        def _(s, pb=pb, n_rnd=n_rnd, w_tpm=w_tpm):
            s.wait_ge(sV.h, n_rnd)
            if w_tpm:
                s.wait_ge(sM.h, w_tpm)
            mi = None
            for t in range(4):
                mi = s.activation(cqs[:, t * 1024:(t + 1) * 1024],
                                  tb1[:, t * 1024:(t + 1) * 1024], AF.Copy,
                                  bias=0.0, scale=stcol(pb, 1, t))
            mi.then_inc(sA.h, 1)
        sA.n += 1
        n_cq = sA.n

        for t4 in range(4):
            tt = S * 4 + t4
            tpb = tt % 2
            w_mmo = d_mm[tt - 2] if tt >= 2 else 0

            def tpd(s, t4=t4, tpb=tpb, n_cq=n_cq, w_mmo=w_mmo):
                s.wait_ge(sA.h, n_cq)
                if w_mmo:
                    s.wait_ge(sM.h, w_mmo)
                for j in range(NC8):
                    s.dma_start(cqT[:, tpb * D + j * 128:tpb * D + (j + 1) * 128],
                                cqs[:, t4 * D + j * 128:t4 * D + (j + 1) * 128],
                                transpose=True).then_inc(sC[tpb].h, 16)
            if t4 % 2 == 0:
                blk.sync(tpd)
            else:
                blk.scalar(tpd)
            sC[tpb].n += 16 * NC8
            n_tpc = sC[tpb].n
            po = ps_a if tt % 2 == 0 else ps_b
            w_aev = d_aev[tt - 2] if tt >= 2 else 0

            @blk.tensor
            def _(t, tpb=tpb, po=po, n_tpc=n_tpc, w_aev=w_aev):
                t.wait_ge(sC[tpb].h, n_tpc)
                if w_aev:
                    t.wait_ge(sV.h, w_aev)
                mi = None
                for j in range(NC8):
                    lhs = cqT[:, tpb * D + j * 128:tpb * D + (j + 1) * 128]
                    mi = t.matmul(po[:, 0:512], lhs, wt_q[:, j * D + 0:j * D + 512],
                                  start=(j == 0), stop=(j == NC8 - 1))
                    mi = t.matmul(po[:, 512:1024], lhs, wt_q[:, j * D + 512:j * D + 1024],
                                  start=(j == 0), stop=(j == NC8 - 1))
                mi.then_inc(sM.h, 1)
            sM.n += 1
            d_mm.append(sM.n)
            n_omm = sM.n

            @blk.vector
            def _(v, t4=t4, po=po, n_omm=n_omm):
                v.wait_ge(sM.h, n_omm)
                v.tensor_scalar(aod[:, t4 * D:t4 * D + 512], po[:, 0:512],
                                gam[:, 3:4], None, ALU.mult, ALU.add,
                                accum_out=st[:, 48 + t4:49 + t4])
                v.tensor_scalar(aod[:, t4 * D + 512:(t4 + 1) * D], po[:, 512:1024],
                                gam[:, 3:4], None, ALU.mult, ALU.add,
                                accum_out=st[:, 52 + t4:53 + t4]).then_inc(sV.h, 1)
            sV.n += 1
            d_aev.append(sV.n)
        n_aev4 = sV.n

        @blk.scalar
        def _(s, pb=pb, n_aev4=n_aev4):
            s.wait_ge(sV.h, n_aev4)
            for t in range(4):
                s.activation(sqd[:, :], aod[:, t * 1024:(t + 1) * 1024],
                             AF.Square, accum_out=stcol(pb, 1, t))
            act_fence(s).then_inc(sA.h, 1)
        sA.n += 1
        n_sqa = sA.n

        @blk.vector
        def _(v, pb=pb, n_sqa=n_sqa):
            v.wait_ge(sA.h, n_sqa)
            v.tensor_tensor(stc(pb, 0), st[:, 48:52], st[:, 52:56], ALU.add)
            pad(v)
            v.tensor_scalar(stc(pb, 0), stc(pb, 0), -1.0 / D, None, ALU.mult)
            v.tensor_scalar(stc(pb, 1), stc(pb, 1), 1.0 / D, None, ALU.mult)
            v.tensor_tensor(stc(pb, 2), stc(pb, 0), stc(pb, 0), ALU.mult)
            pad(v)
            v.tensor_tensor(stc(pb, 1), stc(pb, 1), stc(pb, 2), ALU.subtract)
            pad(v)
            v.tensor_scalar(stc(pb, 1), stc(pb, 1), EPS, None,
                            ALU.add).then_inc(sV.h, 1)
        sV.n += 1
        n_ty2 = sV.n

        @blk.scalar
        def _(s, pb=pb, n_ty2=n_ty2):
            s.wait_ge(sV.h, n_ty2)
            rsqrt(s, stc(pb, 2), stc(pb, 3), stc(pb, 1))
            s.drain()
            mi = None
            for t in range(4):
                mi = s.activation(tb2[:, t * 1024:(t + 1) * 1024],
                                  aod[:, t * 1024:(t + 1) * 1024], AF.Identity,
                                  bias=stcol(pb, 0, t), scale=1.0)
            mi.then_inc(sA.h, 1)
        sA.n += 1
        n_t1b = sA.n
        w_xsp = d_xsp[S - 1] if S >= 1 else 0

        @blk.vector
        def _(v, pb=pb, n_t1b=n_t1b, w_xsp=w_xsp):
            v.wait_ge(sA.h, n_t1b)
            if w_xsp:
                v.wait_ge(sG.h, w_xsp)
            v.tensor_tensor(dap(tb2, 0, [[4096, 128], [1024, 4], [1, 1024]]),
                            dap(tb2, 0, [[4096, 128], [1024, 4], [1, 1024]]),
                            stbc(pb, 2), ALU.mult)
            v.tensor_tensor(x1d[:, :], tb2[:, :], xd[:, :],
                            ALU.add).then_inc(sV.h, 1)
        sV.n += 1
        d_x1.append(sV.n)
        n_x1 = sV.n
        if S + 1 < NS:
            emit_dload(S + 1)

        @blk.gpsimd
        def _(g, S=S, n_x1=n_x1):
            g.wait_ge(sV.h, n_x1)
            g.dma_start(dap(s_x1, S * 4 * 131072,
                            [[1024, 128], [131072, 4], [1, 1024]]),
                        dap(x1d, 0, [[4096, 128], [1024, 4], [1, 1024]])
                        ).then_inc(sG.h, 16)
        sG.n += 16
        d_xsp.append(sG.n)

        @blk.scalar
        def _(s, pb=pb, n_x1=n_x1):
            s.wait_ge(sV.h, n_x1)
            for t in range(4):
                xt = x1d[:, t * 1024:(t + 1) * 1024]
                s.activation(sqd[:, :], xt, AF.Copy, accum_out=stcol(pb, 0, t))
                s.activation(sqd[:, :], xt, AF.Square, accum_out=stcol(pb, 1, t))
            act_fence(s).then_inc(sA.h, 1)
        sA.n += 1
        n_stx = sA.n

        @blk.vector
        def _(v, pb=pb, n_stx=n_stx):
            v.wait_ge(sA.h, n_stx)
            v.tensor_scalar(stc(pb, 0), stc(pb, 0), -1.0 / D, None, ALU.mult)
            v.tensor_scalar(stc(pb, 1), stc(pb, 1), 1.0 / D, None, ALU.mult)
            v.tensor_tensor(stc(pb, 2), stc(pb, 0), stc(pb, 0), ALU.mult)
            pad(v)
            v.tensor_tensor(stc(pb, 1), stc(pb, 1), stc(pb, 2), ALU.subtract)
            pad(v)
            v.tensor_scalar(stc(pb, 1), stc(pb, 1), EPS, None,
                            ALU.add).then_inc(sV.h, 1)
        sV.n += 1
        n_ty3 = sV.n

        @blk.scalar
        def _(s, pb=pb, n_ty3=n_ty3):
            s.wait_ge(sV.h, n_ty3)
            rsqrt(s, stc(pb, 2), stc(pb, 3), stc(pb, 1))
            s.drain()
            mi = None
            for t in range(4):
                mi = s.activation(tb2[:, t * 1024:(t + 1) * 1024],
                                  x1d[:, t * 1024:(t + 1) * 1024], AF.Identity,
                                  bias=stcol(pb, 0, t), scale=1.0)
            mi.then_inc(sA.h, 1)
        sA.n += 1
        n_t1c = sA.n
        w_lt = d_ltq_s[S - 1] if S >= 1 else 0

        @blk.vector
        def _(v, pb=pb, n_t1c=n_t1c, w_lt=w_lt):
            v.wait_ge(sA.h, n_t1c)
            if w_lt:
                v.wait_ge(sL.h, w_lt)
            v.tensor_tensor(dap(xl2, 0, [[4096, 128], [1024, 4], [1, 1024]]),
                            dap(tb2, 0, [[4096, 128], [1024, 4], [1, 1024]]),
                            stbc(pb, 2), ALU.mult).then_inc(sV.h, 1)
        sV.n += 1
        d_fin.append(sV.n)
        n_fin = sV.n

        for t4 in range(4):
            tt = S * 4 + t4

            @blk.sync
            def _(s, t4=t4, tt=tt, n_fin=n_fin):
                s.wait_ge(sV.h, n_fin)
                for j in range(NC8):
                    s.dma_start(xln2T[:, j * T + tt * 128:j * T + (tt + 1) * 128],
                                xl2[:, t4 * D + j * 128:t4 * D + (j + 1) * 128],
                                transpose=True).then_inc(sL.h, 16)
            sL.n += 16 * NC8
        d_ltq_s.append(sL.n)

    n_dl_end = sL.n
    n_dg_end = sG.n
    n_dv_end = sV.n
    n_da_end = sA.n
    d_es.close()
    wt_es.close()
    r_es.close()

    e_es = ExitStack()
    w1s = sb(e_es, "w1s", [128, 2 * 1024], BF16)
    w2s = sb(e_es, "w2s", [128, 2 * 4096], BF16)
    hT = sb(e_es, "hT", [128, 32 * 1024], BF16)
    yst = sb(e_es, "yst", [128, 2 * 1024], BF16)
    ytok = sb(e_es, "ytok", [128, 8 * 1024], BF16, is_transpose_dest=True)
    x1f = sb(e_es, "x1f", [128, 4096], F32)
    obuf = sb(e_es, "obuf", [128, 4096], F32)

    out_st, fin_all = [], []
    ffn2_mm_prev = 0
    fin_prev_block = 0
    f1_mm, f2_mm = [], []
    for Tb in range(4):
        relu_cnt = []
        for fc in range(32):
            gi = Tb * 32 + fc
            w_slab = f1_mm[(gi - 2) * 2 + 1] if gi >= 2 else 0

            @blk.gpsimd
            def _(g, Tb=Tb, fc=fc, w_slab=w_slab):
                if Tb == 0 and fc == 0:
                    g.wait_ge(sV.h, n_dv_end)
                    g.wait_ge(sA.h, n_da_end)
                    g.wait_ge(sL.h, n_dl_end)
                    g.wait_ge(sG.h, n_dg_end)
                    g.wait_ge(sC[0].h, sC[0].n)
                    g.wait_ge(sC[1].h, sC[1].n)
                if w_slab:
                    g.wait_ge(sM.h, w_slab)
                g.dma_start(dap(w1s, (fc % 2) * 1024, [[2048, 128], [1, 1024]]),
                            dap(s_w1t, fc * 131072, [[128, 128], [16384, 8], [1, 128]])
                            ).then_inc(sB[fc % 2].h, 16)
            sB[fc % 2].n += 16
            n_sl = sB[fc % 2].n
            for n in range(2):
                u = fc * 2 + n
                bank = ps_c if u % 2 == 0 else ps_d
                w_relu = relu_cnt[u - 2] if u >= 2 else 0

                @blk.tensor
                def _(t, Tb=Tb, fc=fc, n=n, bank=bank, n_sl=n_sl,
                      w_relu=w_relu, w_prev2=ffn2_mm_prev):
                    t.wait_ge(sB[fc % 2].h, n_sl)
                    if fc == 0 and n == 0:
                        t.wait_ge(sL.h, n_dl_end)
                        if w_prev2:
                            t.wait_ge(sM.h, w_prev2)
                    if w_relu:
                        t.wait_ge(sA.h, w_relu)
                    mi = None
                    for dc in range(NC8):
                        lhs = w1s[:, (fc % 2) * 1024 + dc * 128:
                                  (fc % 2) * 1024 + (dc + 1) * 128]
                        mi = t.matmul(bank[:, :], lhs,
                                      xln2T[:, dc * T + Tb * 1024 + n * 512:
                                            dc * T + Tb * 1024 + (n + 1) * 512],
                                      start=(dc == 0), stop=(dc == NC8 - 1))
                    mi.then_inc(sM.h, 1)
                sM.n += 1
                f1_mm.append(sM.n)
                n_f1 = sM.n

                @blk.scalar
                def _(s, fc=fc, n=n, bank=bank, n_f1=n_f1):
                    s.wait_ge(sM.h, n_f1)
                    s.activation(hT[:, fc * 1024 + n * 512:fc * 1024 + (n + 1) * 512],
                                 bank[:, :], AF.Relu, bias=b1c[:, fc:fc + 1],
                                 scale=1.0).then_inc(sA.h, 1)
                sA.n += 1
                relu_cnt.append(sA.n)
        n_relu_all = sA.n

        yev, ytp = [], []
        for dc in range(NC8):
            gj = Tb * NC8 + dc
            w_slab2 = f2_mm[(gj - 2) * 2 + 1] if gj >= 2 else 0

            @blk.gpsimd
            def _(g, dc=dc, w_slab2=w_slab2):
                if w_slab2:
                    g.wait_ge(sM.h, w_slab2)
                g.dma_start(dap(w2s, (dc % 2) * 4096, [[8192, 128], [1, 4096]]),
                            dap(s_w2t, dc * 524288, [[128, 128], [16384, 32], [1, 128]])
                            ).then_inc(sB[dc % 2].h, 16)
            sB[dc % 2].n += 16
            n_sl = sB[dc % 2].n
            for n in range(2):
                u2 = dc * 2 + n
                boff = 0 if u2 % 2 == 0 else 512
                w_yev = yev[u2 - 2] if u2 >= 2 else 0
                w_ytp = ytp[dc - 2] if (dc >= 2 and n == 0) else 0
                w_ytp = w_ytp and w_ytp[dc % 2]

                @blk.tensor
                def _(t, dc=dc, n=n, boff=boff, n_sl=n_sl,
                      n_relu_all=n_relu_all, w_yev=w_yev):
                    t.wait_ge(sB[dc % 2].h, n_sl)
                    if dc == 0 and n == 0:
                        t.wait_ge(sA.h, n_relu_all)
                    if w_yev:
                        t.wait_ge(sV.h, w_yev)
                    mi = None
                    for fc in range(32):
                        lhs = w2s[:, (dc % 2) * 4096 + fc * 128:
                                  (dc % 2) * 4096 + (fc + 1) * 128]
                        mi = t.matmul(ps_a[:, boff:boff + 512], lhs,
                                      hT[:, fc * 1024 + n * 512:fc * 1024 + (n + 1) * 512],
                                      start=(fc == 0), stop=(fc == 31))
                    mi.then_inc(sM.h, 1)
                sM.n += 1
                f2_mm.append(sM.n)
                n_f2 = sM.n

                @blk.vector
                def _(v, dc=dc, n=n, boff=boff, n_f2=n_f2, w_ytp=w_ytp):
                    v.wait_ge(sM.h, n_f2)
                    if w_ytp:
                        v.wait_ge(sT[dc % 2].h, w_ytp)
                    v.tensor_scalar(yst[:, (dc % 2) * 1024 + n * 512:
                                        (dc % 2) * 1024 + (n + 1) * 512],
                                    ps_a[:, boff:boff + 512],
                                    b2c[:, dc:dc + 1], None,
                                    ALU.add).then_inc(sV.h, 1)
                sV.n += 1
                yev.append(sV.n)
            n_ye = sV.n

            def ytpb(s, dc=dc, n_ye=n_ye, fin_prev_block=fin_prev_block):
                s.wait_ge(sV.h, n_ye)
                if fin_prev_block:
                    s.wait_ge(sV.h, fin_prev_block)
                for ts_ in range(8):
                    s.dma_start(ytok[:, ts_ * 1024 + dc * 128:ts_ * 1024 + (dc + 1) * 128],
                                yst[:, (dc % 2) * 1024 + ts_ * 128:
                                    (dc % 2) * 1024 + (ts_ + 1) * 128],
                                transpose=True).then_inc(sT[dc % 2].h, 16)
            if dc % 2 == 0:
                blk.sync(ytpb)
            else:
                blk.scalar(ytpb)
            sT[dc % 2].n += 128
            ytp.append((sT[0].n, sT[1].n))
        n_ytp_all = sT[0].n
        n_ytp_all_s = sT[1].n

        for gq in range(2):
            gg = Tb * 2 + gq
            pb = gg % 2
            w_xlf = fin_all[gg - 1] if gg >= 1 else 0

            @blk.sync
            def _(s, Tb=Tb, gq=gq, w_xlf=w_xlf):
                if w_xlf:
                    s.wait_ge(sV.h, w_xlf)
                s.dma_start(dap(x1f, 0, [[4096, 128], [1024, 4], [1, 1024]]),
                            dap(s_x1, (Tb * 8 + gq * 4) * 131072,
                                [[1024, 128], [131072, 4], [1, 1024]])
                            ).then_inc(sQ.h, 16)
            sQ.n += 16
            n_xl = sQ.n
            w_ost = out_st[gg - 1] if gg >= 1 else 0
            w_st = fin_all[gg - 2] if gg >= 2 else 0

            @blk.scalar
            def _(s, pb=pb, gq=gq, n_ytp_all=n_ytp_all, n_ytp_all_s=n_ytp_all_s,
                  w_ost=w_ost, w_st=w_st):
                s.wait_ge(sT[0].h, n_ytp_all)
                s.wait_ge(sT[1].h, n_ytp_all_s)
                if w_ost:
                    s.wait_ge(sO.h, w_ost)
                if w_st:
                    s.wait_ge(sV.h, w_st)
                for t in range(4):
                    yt = ytok[:, gq * 4096 + t * 1024:gq * 4096 + (t + 1) * 1024]
                    ot = obuf[:, t * 1024:(t + 1) * 1024]
                    s.activation(ot, yt, AF.Copy, accum_out=stcol(pb, 0, t))
                    s.activation(ot, yt, AF.Square, accum_out=stcol(pb, 1, t))
                act_fence(s).then_inc(sA.h, 1)
            sA.n += 1
            n_st8 = sA.n

            @blk.vector
            def _(v, pb=pb, n_st8=n_st8):
                v.wait_ge(sA.h, n_st8)
                v.tensor_scalar(stc(pb, 0), stc(pb, 0), -1.0 / D, None, ALU.mult)
                v.tensor_scalar(stc(pb, 1), stc(pb, 1), 1.0 / D, None, ALU.mult)
                v.tensor_tensor(stc(pb, 2), stc(pb, 0), stc(pb, 0), ALU.mult)
                pad(v)
                v.tensor_tensor(stc(pb, 1), stc(pb, 1), stc(pb, 2), ALU.subtract)
                pad(v)
                v.tensor_scalar(stc(pb, 1), stc(pb, 1), EPS, None,
                                ALU.add).then_inc(sV.h, 1)
            sV.n += 1
            n_ty = sV.n

            @blk.scalar
            def _(s, pb=pb, gq=gq, n_ty=n_ty):
                s.wait_ge(sV.h, n_ty)
                rsqrt(s, stc(pb, 2), stc(pb, 3), stc(pb, 1))
                s.drain()
                mi = None
                for t in range(4):
                    mi = s.activation(obuf[:, t * 1024:(t + 1) * 1024],
                                      ytok[:, gq * 4096 + t * 1024:
                                           gq * 4096 + (t + 1) * 1024],
                                      AF.Identity, bias=stcol(pb, 0, t), scale=1.0)
                mi.then_inc(sA.h, 1)
            sA.n += 1
            n_t1 = sA.n

            @blk.vector
            def _(v, pb=pb, n_t1=n_t1, n_xl=n_xl):
                v.wait_ge(sA.h, n_t1)
                v.wait_ge(sQ.h, n_xl)
                v.tensor_tensor(dap(obuf, 0, [[4096, 128], [1024, 4], [1, 1024]]),
                                dap(obuf, 0, [[4096, 128], [1024, 4], [1, 1024]]),
                                stbc(pb, 2), ALU.mult)
                v.tensor_tensor(obuf[:, :], obuf[:, :], x1f[:, :],
                                ALU.add).then_inc(sV.h, 1)
            sV.n += 1
            fin_all.append(sV.n)
            n_ob = sV.n

            @blk.sync
            def _(s, Tb=Tb, gq=gq, n_ob=n_ob):
                s.wait_ge(sV.h, n_ob)
                s.dma_start(dap(out, (Tb * 8 + gq * 4) * 131072,
                                [[1024, 128], [131072, 4], [1, 1024]]),
                            obuf[:, :]).then_inc(sO.h, 16)
            sO.n += 16
            out_st.append(sO.n)
        fin_prev_block = fin_all[-1]
        ffn2_mm_prev = sM.n

    @blk.sync
    def _(s):
        s.wait_ge(sO.h, sO.n)
        s.wait_ge(sQ.h, sQ.n)
        s.wait_ge(sG.h, sG.n)
        for ctr in (sU, sL, sW[0], sW[1], sZ[0], sZ[1], sX[0], sX[1],
                    sY[0], sY[1], sC[0], sC[1], sT[0], sT[1],
                    sD[0], sD[1], sB[0], sB[1]):
            s.wait_ge(ctr.h, ctr.n)

    e_es.close()
    p_es.close()
    g_es.close()
    return nc


_NC_CACHE = None


def kernel(L0_attn_x, L0_attn_wq, L0_attn_wk, L0_attn_wv, L0_attn_wo,
           L0_ffn_W1, L0_ffn_b1, L0_ffn_W2, L0_ffn_b2):
    global _NC_CACHE
    x = np.ascontiguousarray(np.asarray(L0_attn_x, dtype=np.float32))
    wq = np.ascontiguousarray(np.asarray(L0_attn_wq, dtype=np.float32))
    wk = np.ascontiguousarray(np.asarray(L0_attn_wk, dtype=np.float32))
    wv = np.ascontiguousarray(np.asarray(L0_attn_wv, dtype=np.float32))
    wo = np.ascontiguousarray(np.asarray(L0_attn_wo, dtype=np.float32))
    w1 = np.ascontiguousarray(np.asarray(L0_ffn_W1, dtype=np.float32))
    b1 = np.ascontiguousarray(np.asarray(L0_ffn_b1, dtype=np.float32)).reshape(32, 128)
    w2 = np.ascontiguousarray(np.asarray(L0_ffn_W2, dtype=np.float32))
    b2 = np.ascontiguousarray(np.asarray(L0_ffn_b2, dtype=np.float32)).reshape(8, 128)

    if _NC_CACHE is None:
        _NC_CACHE = build()
    nc = _NC_CACHE
    in_maps = [{"x": np.ascontiguousarray(x[i]), "wq": wq, "wk": wk, "wv": wv,
                "wo": wo, "w1": w1, "b1": b1, "w2": w2, "b2": b2}
               for i in range(8)]
    res = run_bass_kernel_spmd(nc, in_maps, core_ids=list(range(8)))
    return np.stack([r["out"] for r in res.results], axis=0)
